# version 05
# baseline (speedup 1.0000x reference)
import numpy as np
import ml_dtypes
from contextlib import ExitStack

import concourse.bass as bass
import concourse.mybir as mybir
from concourse.bass_utils import run_bass_kernel_spmd

F32 = mybir.dt.float32
BF16 = mybir.dt.bfloat16
I32 = mybir.dt.int32
U32 = mybir.dt.uint32
AF = mybir.ActivationFunctionType
ALU = mybir.AluOpType
AX = mybir.AxisListType

T = 4096
D = 1024
NT = T // 128
ND = D // 128
LN_EPS = 1e-5

ENGS = ("sync", "gpsimd", "tensor", "vector", "scalar")


class Prog:

    def __init__(self, nc, ctx):
        self.nc = nc
        self.ctx = ctx
        self.items = {e: [] for e in ENGS}
        self.sems = {}
        self.counts = {}
        self.waited = {e: {} for e in ENGS}
        for e in ENGS:
            self._sem("s_" + e)
        self.dma_idx = {}
        for e in ("sync", "gpsimd", "scalar", "vector"):
            for i in range(8):
                self._sem(f"d_{e}{i}")
            self.dma_idx[e] = 0

    def _sem(self, name):
        self.sems[name] = self.ctx.enter_context(self.nc.semaphore(name))
        self.counts[name] = 0

    def wait(self, eng, semname, val):
        if val <= 0:
            return
        w = self.waited[eng]
        if w.get(semname, 0) >= val:
            return
        w[semname] = val
        self.items[eng].append(("w", semname, val))

    def after(self, eng, tags):
        for t in tags:
            if t is None:
                continue
            self.wait(eng, t[0], t[1])

    def op(self, eng, fn, after=()):
        self.after(eng, after)
        sname = "s_" + eng
        self.counts[sname] += 1
        self.items[eng].append(("o", fn, [(sname, 1)]))
        return (sname, self.counts[sname])

    def dma(self, eng, out, in_, after=(), **kw):
        self.after(eng, after)
        idx = self.dma_idx[eng]
        self.dma_idx[eng] = (idx + 1) % 8
        sname = f"d_{eng}{idx}"
        self.counts[sname] += 16
        fn = (lambda e, o=out, i=in_, kw=kw: e.dma_start(out=o, in_=i, **kw))
        self.items[eng].append(("o", fn, [(sname, 16)]))
        return (sname, self.counts[sname])

    def barrier(self):
        snap = [(s, v) for s, v in self.counts.items() if v > 0]
        for e in ENGS:
            for s, v in snap:
                self.wait(e, s, v)

    def replay(self):
        nc = self.nc
        nc.all_engine_barrier()
        ksr = nc._kernel_sem_range
        start = ksr.start + (3 if nc._bir_kernel_barrier_sem is not None else 2)
        start += len(nc._monotonic_sems)
        rng = range(start, ksr.stop)
        nc.gpsimd.dma_reset(rng)
        nc.gpsimd.sem_clear(rng)
        nc.all_engine_barrier()
        with nc.Block() as block:
            def mk(e):
                def emit(eng):
                    for it in self.items[e]:
                        if it[0] == "w":
                            eng.wait_ge(self.sems[it[1]], it[2])
                        else:
                            inst = it[1](eng)
                            for s, n in it[2]:
                                inst.then_inc(self.sems[s], n)
                return emit
            block.sync(mk("sync"))
            block.gpsimd(mk("gpsimd"))
            block.tensor(mk("tensor"))
            block.vector(mk("vector"))
            block.scalar(mk("scalar"))



def sbuf(nc, ctx, name, shape, dtype=F32):
    return ctx.enter_context(nc.sbuf_tensor(name, shape, dtype))


def psum(nc, ctx, name, shape, dtype=F32):
    return ctx.enter_context(nc.psum_tensor(name, shape, dtype))


def load_col(p, dst_col, vec_hbm, off, n):
    return p.dma("scalar", dst_col,
                 vec_hbm[off:off + n].rearrange("(p one) -> p one", one=1))


class Common:

    def __init__(self, p, nc, ctx, io):
        self.iden = sbuf(nc, ctx, "c_iden", [128, 128], BF16)
        self.t_iden = p.dma("sync", self.iden[:], io["iden"])
        self.ones = sbuf(nc, ctx, "c_ones", [1, 128], F32)
        self.t_ones = p.op("vector", (lambda e, o=self.ones[:]: e.memset(o, 1.0)))
        self.row = sbuf(nc, ctx, "c_row", [1, 1024], F32)
        self.row_last = None


def bcast_vec(p, nc, ctx, cm, name, vec_hbm, n, ps_pool, war=(), tile=None):
    row = cm.row
    if tile is None:
        tile = sbuf(nc, ctx, name + "_bc", [128, n])
    t_row = p.dma("scalar", row[0:1, 0:n],
                  vec_hbm[0:n].rearrange("(one n) -> one n", one=1),
                  after=[cm.row_last])
    last = [None] * len(ps_pool)
    tc = None
    for c0 in range(0, n, 512):
        c1 = min(c0 + 512, n)
        idx = (c0 // 512) % len(ps_pool)
        ps = ps_pool[idx]
        tm = p.op("tensor", (lambda e, o=ps[:, 0:c1 - c0], l=cm.ones[:],
                             r=row[0:1, c0:c1]: e.matmul(o, l, r, start=True, stop=True)),
                  after=[t_row, cm.t_ones, last[idx]] + list(war))
        tc = p.op("vector", (lambda e, o=tile[:, c0:c1], i=ps[:, 0:c1 - c0]:
                             e.tensor_copy(o, i)), after=[tm])
        last[idx] = tc
    cm.row_last = tc
    return tile, tc


class LNS:

    def __init__(self, nc, ctx, name, width=D):
        self.width = width
        self.ssum = sbuf(nc, ctx, name + "_ssum", [128, 1])
        self.sqacc = sbuf(nc, ctx, name + "_sqacc", [128, 1])
        self.mean = sbuf(nc, ctx, name + "_mean", [128, 1])
        self.var = sbuf(nc, ctx, name + "_var", [128, 1])
        self.m2 = sbuf(nc, ctx, name + "_m2", [128, 1])
        self.sd = sbuf(nc, ctx, name + "_sd", [128, 1])
        self.rstd = sbuf(nc, ctx, name + "_rstd", [128, 1])
        self.sq = sbuf(nc, ctx, name + "_sq", [128, width])


def ln_stats(p, x_ap, w, after=(), eps=LN_EPS, mean_zero=False):
    width = w.width
    t_sq = p.op("scalar", (lambda e, o=w.sq[:, 0:width], i=x_ap, a=w.sqacc[:]:
                           e.activation(o, i, AF.Square, accum_out=a)), after=after)
    t_v1 = p.op("vector", (lambda e, o=w.var[:], i=w.sqacc[:]:
                           e.tensor_scalar_mul(o, i, 1.0 / width)), after=[t_sq])
    if mean_zero:
        t_v3 = t_v1
        t_mean = None
    else:
        t_sum = p.op("vector", (lambda e, o=w.ssum[:], i=x_ap:
                                e.reduce_sum(o, i, axis=AX.X)), after=after)
        t_mean = p.op("vector", (lambda e, o=w.mean[:], i=w.ssum[:]:
                                 e.tensor_scalar_mul(o, i, 1.0 / width)), after=[t_sum])
        t_v2 = p.op("vector", (lambda e, o=w.m2[:], i=w.mean[:]:
                               e.tensor_tensor(o, i, i, ALU.mult)), after=[t_mean])
        t_v3 = p.op("vector", (lambda e, o=w.var[:], a=w.var[:], b=w.m2[:]:
                               e.tensor_sub(o, a, b)), after=[t_v1, t_v2])
    t_v4 = p.op("vector", (lambda e, o=w.var[:], i=w.var[:], c=float(eps):
                           e.tensor_scalar_add(o, i, c)), after=[t_v3])
    t_v5 = p.op("scalar", (lambda e, o=w.sd[:], i=w.var[:]:
                           e.activation(o, i, AF.Sqrt)), after=[t_v4])
    t_rstd = p.op("vector", (lambda e, o=w.rstd[:], i=w.sd[:]:
                             e.reciprocal(o, i)), after=[t_v5])
    return t_mean, t_rstd


def ln_apply(p, out_ap, x_ap, w, tags):
    return p.op("vector", (lambda e, o=out_ap, i=x_ap, m=w.mean[:], r=w.rstd[:]:
                           e.tensor_scalar(o, i, m, r, ALU.subtract, ALU.mult)),
                after=tags)


class WTrans:

    def __init__(self, p, nc, ctx, cm, name, n_ps=2):
        self.p, self.nc, self.cm = p, nc, cm
        self.raw = [sbuf(nc, ctx, f"{name}_raw{i}", [128, D], BF16) for i in range(2)]
        self.tr = [psum(nc, ctx, f"{name}_tr{i}", [128, 128], BF16) for i in range(n_ps)]
        self.n_ps = n_ps
        self.last_raw = [None, None]
        self.last_tr = [None] * n_ps
        self.k = 0

    def slab(self, w_hbm_rows, in_width, dest_cols):
        p, cm = self.p, self.cm
        sl = self.k % 2
        self.k += 1
        t_load = p.dma("gpsimd", self.raw[sl][:, 0:in_width], w_hbm_rows,
                       after=[self.last_raw[sl]])
        out_tags = []
        for ic in range(in_width // 128):
            psl = (self.k * 8 + ic) % self.n_ps
            t_tr = p.op("tensor", (lambda e, o=self.tr[psl][:],
                                   i=self.raw[sl][:, ic * 128:(ic + 1) * 128],
                                   idn=cm.iden[:]: e.transpose(o, i, idn)),
                        after=[t_load, cm.t_iden, self.last_tr[psl]])
            t_cp = p.op("vector", (lambda e, o=dest_cols[ic], i=self.tr[psl][:]:
                                   e.tensor_copy(o, i)), after=[t_tr])
            self.last_tr[psl] = t_cp
            out_tags.append(t_cp)
        self.last_raw[sl] = out_tags[-1]
        return out_tags


class TrPool:

    def __init__(self, p, nc, ctx, cm, name):
        self.p, self.cm = p, cm
        self.trs = [psum(nc, ctx, f"{name}_tr{i}", [128, 128], BF16) for i in range(2)]
        self.last = [None, None]
        self.k = 0

    def tr(self, in_bf_ap, dest_ap, after=(), dest_eng="vector"):
        p, cm = self.p, self.cm
        sl = self.k % 2
        self.k += 1
        t_tr = p.op("tensor", (lambda e, o=self.trs[sl][:], i=in_bf_ap,
                               idn=cm.iden[:]: e.transpose(o, i, idn)),
                    after=list(after) + [cm.t_iden, self.last[sl]])
        t_cp = p.op(dest_eng, (lambda e, o=dest_ap, i=self.trs[sl][:]:
                               e.tensor_copy(o, i)), after=[t_tr])
        self.last[sl] = t_cp
        return t_cp


class WT2:

    def __init__(self, p, nc, ctx, trp, name, scale=1.0, raw_width=4096):
        self.p, self.trp = p, trp
        self.raw = [sbuf(nc, ctx, f"{name}_raw{i}", [128, raw_width], BF16) for i in range(2)]
        self.last_raw = [None, None]
        self.k = 0
        self.scale = scale

    def slab(self, w_hbm_rows, in_width, dest_cols):
        p = self.p
        sl = self.k % 2
        self.k += 1
        t_load = p.dma("gpsimd", self.raw[sl][:, 0:in_width], w_hbm_rows,
                       after=[self.last_raw[sl]])
        tags = []
        for ic in range(in_width // 128):
            if self.scale == 1.0:
                t = self.trp.tr(self.raw[sl][:, ic * 128:(ic + 1) * 128],
                                dest_cols[ic], after=[t_load])
            else:
                trp, sc = self.trp, self.scale
                psl = trp.k % 2
                trp.k += 1
                t_tr = p.op("tensor", (lambda e, o=trp.trs[psl][:],
                                       i=self.raw[sl][:, ic * 128:(ic + 1) * 128],
                                       idn=trp.cm.iden[:]: e.transpose(o, i, idn)),
                            after=[t_load, trp.cm.t_iden, trp.last[psl]])
                t = p.op("scalar", (lambda e, o=dest_cols[ic], i=trp.trs[psl][:],
                                    s=sc: e.mul(o, i, s)), after=[t_tr])
                trp.last[psl] = t
            tags.append(t)
        self.last_raw[sl] = tags[-1]
        return tags


def emit_a0(p, nc, ctx, cm, io):
    H, DH = 16, 64
    trp = TrPool(p, nc, ctx, cm, "a0trp")

    with ExitStack() as s_pro:
        bqk = sbuf(nc, ctx, "a0_bqk", [128, 16])
        for oc in range(16):
            load_col(p, bqk[:, oc:oc + 1], io["in_proj_b"], oc * 128, 128)
        p.barrier()
        p.op("vector", (lambda e, o=bqk[:, 8:16], i=bqk[:, 8:16]:
                        e.tensor_scalar_mul(o, i, 0.125)))
        p.barrier()

    lns = LNS(nc, ctx, "a0_lns")

    with ExitStack() as sB:
        srcT = sbuf(nc, sB, "a0_srcT", [128, ND * T], BF16)
        s1 = sB.enter_context(ExitStack())
        x_t = [sbuf(nc, s1, f"a0_x{i}", [128, D]) for i in range(2)]
        src_t = [sbuf(nc, s1, f"a0_src{i}", [128, D]) for i in range(2)]
        srcbf = [sbuf(nc, s1, f"a0_srcbf{i}", [128, D], BF16) for i in range(2)]
        lastu = {}
        for ti in range(NT):
            sl = ti % 2
            r0 = ti * 128
            tx = p.dma("sync", x_t[sl][:], io["x"][r0:r0 + 128, :],
                       after=[lastu.get(("x", sl))])
            tm, tr = ln_stats(p, x_t[sl][:], lns, after=[tx])
            ts = ln_apply(p, src_t[sl][:], x_t[sl][:], lns,
                          [tm, tr, lastu.get(("srcdma", sl)), lastu.get(("srcbf", sl))])
            lastu[("x", sl)] = ts
            lastu[("srcdma", sl)] = p.dma("sync", io["src"][r0:r0 + 128, :],
                                          src_t[sl][:], after=[ts])
            tbf = p.op("scalar", (lambda e, o=srcbf[sl][:], i=src_t[sl][:]:
                                  e.copy(o, i)), after=[ts, lastu.get(("sbf", sl))])
            lastu[("srcbf", sl)] = tbf
            tlast = None
            for ic in range(ND):
                tlast = trp.tr(srcbf[sl][:, ic * 128:(ic + 1) * 128],
                               srcT[:, ic * T + r0: ic * T + r0 + 128], after=[tbf])
            lastu[("sbf", sl)] = tlast
        p.barrier()
        s1.close()

        qT = sbuf(nc, sB, "a0_qT", [128, 2 * T], BF16)
        kT = sbuf(nc, sB, "a0_kT", [128, 2 * T], BF16)
        vg = sbuf(nc, sB, "a0_vg", [128, NT * 256], BF16)
        pt = [sbuf(nc, sB, f"a0_pt{i}", [128, T], BF16) for i in range(2)]
        lparts = sbuf(nc, sB, "a0_lparts", [128, 4])
        lsum = sbuf(nc, sB, "a0_lsum", [128, 1])
        linv = sbuf(nc, sB, "a0_linv", [128, 1])
        vp = sbuf(nc, sB, "a0_vp", [128, 64], BF16)
        acc = sbuf(nc, sB, "a0_acc", [128, T])
        wblk = [sbuf(nc, sB, f"a0_wblk{i}", [128, D], BF16) for i in range(2)]
        zA = psum(nc, sB, "a0_zA", [128, 1536])
        zB = psum(nc, sB, "a0_zB", [128, 1536])
        wt = WT2(p, nc, sB, trp, "a0_wt", raw_width=D)
        vb_bc, t_vb = bcast_vec(p, nc, sB, cm, "a0_vb",
                                io["in_proj_b"][2 * D:3 * D], D, [zA, zB])

        mm_slot = [zA[:, 0:512], zB[:, 0:512]]
        last_slot = [t_vb, t_vb]
        last_wblk = [None, None]
        t_postdma = {}
        lp_last = None
        pt_last = [None, None]
        prev_ac_last = None
        gelu_done = None

        for g in range(4):
            for part in ("q", "k", "v"):
                base_oc = {"q": 0, "k": 8, "v": 16}[part]
                for oc_loc in range(2):
                    oc = base_oc + 2 * g + oc_loc
                    wsl = oc % 2
                    dcols = [wblk[wsl][:, ic * 128:(ic + 1) * 128] for ic in range(ND)]
                    if last_wblk[wsl] is not None:
                        wt.last_raw[wt.k % 2] = last_wblk[wsl]
                    ttags = wt.slab(io["in_proj_w"][oc * 128:(oc + 1) * 128, :], D, dcols)
                    tw = ttags[-1]
                    if part in ("q", "k"):
                        dest = qT if part == "q" else kT
                        rb = oc_loc
                        mmt = None
                        for tc in range(8):
                            psl = tc % 2
                            for ic in range(ND):
                                mmt = p.op("tensor", (lambda e, o=mm_slot[psl],
                                           l=wblk[wsl][:, ic * 128:(ic + 1) * 128],
                                           r=srcT[:, ic * T + tc * 512: ic * T + (tc + 1) * 512],
                                           s=(ic == 0), st=(ic == ND - 1):
                                           e.matmul(o, l, r, start=s, stop=st)),
                                           after=[tw, last_slot[psl]])
                            tev = p.op("scalar", (lambda e,
                                       o=dest[:, rb * T + tc * 512: rb * T + (tc + 1) * 512],
                                       i=mm_slot[psl], b=bqk[:, oc:oc + 1],
                                       s=(0.125 if part == "k" else 1.0):
                                       e.activation(o, i, AF.Identity, bias=b, scale=s)),
                                       after=[mmt])
                            last_slot[psl] = tev
                            last_wblk[wsl] = tev
                    else:
                        vcol0 = oc_loc * 128
                        mmt = None
                        for ti in range(NT):
                            psl = ti % 2
                            for ic in range(ND):
                                mmt = p.op("tensor", (lambda e,
                                           o=mm_slot[psl][:, 0:128],
                                           l=srcT[:, ic * T + ti * 128: ic * T + ti * 128 + 128],
                                           r=wblk[wsl][:, ic * 128:(ic + 1) * 128],
                                           s=(ic == 0), st=(ic == ND - 1):
                                           e.matmul(o, l, r, start=s, stop=st)),
                                           after=[tw, last_slot[psl]])
                            tev = p.op("vector", (lambda e,
                                       o=vg[:, ti * 256 + vcol0: ti * 256 + vcol0 + 128],
                                       i=mm_slot[psl][:, 0:128],
                                       b=vb_bc[:, g * 256 + vcol0: g * 256 + vcol0 + 128]:
                                       e.scalar_tensor_tensor(out=o, in0=i, scalar=1.0,
                                           in1=b, op0=ALU.mult, op1=ALU.add)),
                                       after=[mmt, t_vb])
                            last_slot[psl] = tev
                            last_wblk[wsl] = tev
            for hq in range(4):
                h = 4 * g + hq
                rb = hq // 2
                pb = 64 * (hq % 2)
                ptsl = h % 2
                ev_tags = []
                zregs = [(zA, 0), (zA, 512), (zA, 1024), (zB, 0), (zB, 512),
                         (zB, 1024), (zA, 0), (zA, 512)]
                ac_tags = []
                evA = evB = evA2 = None
                for ti in range(NT):
                    s0 = ti * 128
                    ztags = {}

                    def zmm(tc, war):
                        zt, zo = zregs[tc]
                        return p.op("tensor", (lambda e, o=zt[:, zo:zo + 512],
                                    l=kT[pb:pb + 64, rb * T + s0: rb * T + s0 + 128],
                                    r=qT[pb:pb + 64, rb * T + tc * 512: rb * T + (tc + 1) * 512]:
                                    e.matmul(o, l, r, start=True, stop=True)),
                                    after=war)
                    ztags[0] = zmm(0, [prev_ac_last, last_slot[0], last_slot[1], evA2])
                    ztags[1] = zmm(1, [])
                    ztags[2] = zmm(2, [])
                    evA = p.op("scalar", (lambda e, o=pt[ptsl][:, 0:1536], i=zA[:, 0:1536],
                               a=lparts[:, 0:1]: e.activation(o, i, AF.Exp, accum_out=a)),
                               after=[ztags[2], pt_last[ptsl], lp_last])
                    ztags[3] = zmm(3, [])
                    ztags[4] = zmm(4, [])
                    ztags[5] = zmm(5, [])
                    evB = p.op("scalar", (lambda e, o=pt[ptsl][:, 1536:3072], i=zB[:, 0:1536],
                               a=lparts[:, 1:2]: e.activation(o, i, AF.Exp, accum_out=a)),
                               after=[ztags[5]])
                    ztags[6] = zmm(6, [evA, ac_tags[-1] if ac_tags else None])
                    ztags[7] = zmm(7, [evA])
                    evA2 = p.op("scalar", (lambda e, o=pt[ptsl][:, 3072:4096], i=zA[:, 0:1024],
                                a=lparts[:, 2:3]: e.activation(o, i, AF.Exp, accum_out=a)),
                                after=[ztags[7]])
                    ev_tags = [evA, evB, evA2]
                    t_l = p.op("vector", (lambda e, o=lsum[:], i=lparts[:, 0:3]:
                               e.reduce_sum(o, i, axis=AX.X)), after=ev_tags)
                    lp_last = t_l
                    t_li = p.op("vector", (lambda e, o=linv[:], i=lsum[:]:
                                e.reciprocal(o, i)), after=[t_l])
                    t_vp = p.op("vector", (lambda e, o=vp[:],
                                i=vg[:, ti * 256 + hq * 64: ti * 256 + hq * 64 + 64],
                                s=linv[:]: e.tensor_scalar_mul(o, i, s)),
                                after=[t_li, pt_last[ptsl]])
                    new_ac = []
                    for tc in range(8):
                        asl = tc % 2
                        t_av = p.op("tensor", (lambda e, o=zA[0:64, asl * 512: asl * 512 + 512],
                                    l=vp[:], r=pt[ptsl][:, tc * 512:(tc + 1) * 512]:
                                    e.matmul(o, l, r, start=True, stop=True)),
                                    after=[evA2, t_vp,
                                           new_ac[-2] if len(new_ac) >= 2 else None])
                        pt_last[ptsl] = t_av
                        if ti == 0:
                            t_ac = p.op("vector", (lambda e,
                                        o=acc[pb:pb + 64, tc * 512:(tc + 1) * 512],
                                        i=zA[0:64, asl * 512: asl * 512 + 512]:
                                        e.tensor_copy(o, i)),
                                        after=[t_av, gelu_done, t_postdma.get(h // 2)])
                        else:
                            t_ac = p.op("vector", (lambda e,
                                        o=acc[pb:pb + 64, tc * 512:(tc + 1) * 512],
                                        i=zA[0:64, asl * 512: asl * 512 + 512]:
                                        e.scalar_tensor_tensor(out=o, in0=i, scalar=1.0,
                                            in1=o, op0=ALU.mult, op1=ALU.add)),
                                        after=[t_av, ac_tags[-8] if len(ac_tags) >= 8 else None])
                        new_ac.append(t_ac)
                        ac_tags.append(t_ac)
                    prev_ac_last = new_ac[-1]
                if hq % 2 == 1:
                    gelu_done = p.op("scalar", (lambda e, o=pt[0][:], i=acc[:]:
                                     e.activation(o, i, AF.Gelu)),
                                     after=[prev_ac_last, pt_last[0]])
                    pr = h // 2
                    t_postdma[pr] = p.dma("sync", io["postt"][pr * 128:(pr + 1) * 128, :],
                                          pt[0][:], after=[gelu_done])
                    pt_last[0] = t_postdma[pr]
        p.barrier()

    with ExitStack() as sC:
        outwT = sbuf(nc, sC, "a0_outwT", [128, ND * D], BF16)
        wt4 = WT2(p, nc, sC, trp, "a0_wt4", scale=1.0 / 64.0, raw_width=D)
        for oco in range(ND):
            dcols = [outwT[:, ic * D + oco * 128: ic * D + oco * 128 + 128]
                     for ic in range(ND)]
            wt4.slab(io["out_w"][oco * 128:(oco + 1) * 128, :], D, dcols)
        pblk = sbuf(nc, sC, "a0_pblk", [128, ND * 512], BF16)
        stage = sbuf(nc, sC, "a0_stage", [128, ND * 512], BF16)
        src_t4 = sbuf(nc, sC, "a0_src4", [128, D])
        ua = sbuf(nc, sC, "a0_ua", [128, D])
        ub = sbuf(nc, sC, "a0_ub", [128, D])
        xl1 = sbuf(nc, sC, "a0_xl1", [128, D])
        xl1bf = sbuf(nc, sC, "a0_xl1bf", [128, D], BF16)
        at2 = [psum(nc, sC, f"a0_at2{i}", [128, 512]) for i in range(2)]
        outb_bc, t_outb = bcast_vec(p, nc, sC, cm, "a0_outb", io["out_b"], D, at2)
        g1_bc, t_g1 = bcast_vec(p, nc, sC, cm, "a0_g1", io["ln1_g"], D, at2)
        b1_bc, t_b1 = bcast_vec(p, nc, sC, cm, "a0_b1", io["ln1_b"], D, at2)
        p.barrier()
        last_ev4 = [None, None]
        last_stage = None
        last_pblk = None
        last_src4 = None
        last_xl1 = []
        for tb in range(8):
            tpb = []
            for ic in range(ND):
                tpb.append(p.dma("sync", pblk[:, ic * 512:(ic + 1) * 512],
                                 io["postt"][ic * 128:(ic + 1) * 128,
                                             tb * 512:(tb + 1) * 512],
                                 after=[last_pblk, t_postdma.get(ic)]))
            tstg = []
            for tsub in range(4):
                r0 = tb * 512 + tsub * 128
                tsrc = p.dma("sync", src_t4[:], io["src"][r0:r0 + 128, :],
                             after=[last_src4])
                mmt = [None, None]
                for oc in range(2):
                    for ic in range(ND):
                        mmt[oc] = p.op("tensor", (lambda e, o=at2[oc][:],
                                       l=pblk[:, ic * 512 + tsub * 128: ic * 512 + tsub * 128 + 128],
                                       r=outwT[:, ic * D + oc * 512: ic * D + (oc + 1) * 512],
                                       s=(ic == 0), st=(ic == ND - 1):
                                       e.matmul(o, l, r, start=s, stop=st)),
                                       after=[tpb[-1], last_ev4[oc]])
                tu = []
                for oc in range(2):
                    t1 = p.op("vector", (lambda e, o=ua[:, oc * 512:(oc + 1) * 512],
                              i=at2[oc][:], b=outb_bc[:, oc * 512:(oc + 1) * 512]:
                              e.scalar_tensor_tensor(out=o, in0=i, scalar=1.0, in1=b,
                                  op0=ALU.mult, op1=ALU.add)), after=[mmt[oc], t_outb])
                    t2 = p.op("vector", (lambda e, o=ub[:, oc * 512:(oc + 1) * 512],
                              a=ua[:, oc * 512:(oc + 1) * 512],
                              b=src_t4[:, oc * 512:(oc + 1) * 512]:
                              e.tensor_add(o, a, b)), after=[t1, tsrc])
                    last_ev4[oc] = t2
                    tu.append(t2)
                last_src4 = tu[-1]
                tm, tr = ln_stats(p, ub[:], lns, after=tu)
                ta = ln_apply(p, ua[:], ub[:], lns, [tm, tr])
                tg = p.op("vector", (lambda e, o=ub[:], a=ua[:], b=g1_bc[:]:
                          e.tensor_mul(o, a, b)), after=[ta, t_g1])
                tb2 = p.op("vector", (lambda e, o=xl1[:], a=ub[:], b=b1_bc[:]:
                           e.tensor_add(o, a, b)), after=[tg, t_b1] + last_xl1)
                t_xd = p.dma("sync", io["xln1"][r0:r0 + 128, :], xl1[:], after=[tb2])
                tbf = p.op("scalar", (lambda e, o=xl1bf[:], i=xl1[:]: e.copy(o, i)),
                           after=[tb2])
                last_xl1 = [t_xd, tbf]
                for ic in range(ND):
                    tstg.append(trp.tr(xl1bf[:, ic * 128:(ic + 1) * 128],
                                       stage[:, ic * 512 + tsub * 128: ic * 512 + tsub * 128 + 128],
                                       after=[tbf, last_stage]))
            xlt_dma = []
            for ic in range(ND):
                xlt_dma.append(p.dma("sync", io["xln1t"][ic, :, tb * 512:(tb + 1) * 512],
                                     stage[:, ic * 512:(ic + 1) * 512], after=[tstg[-1]]))
            last_stage = xlt_dma[-1]
            last_pblk = mmt[1]
        p.barrier()

    with ExitStack() as sD:
        s5 = sD.enter_context(ExitStack())
        lin1T = sbuf(nc, s5, "a0_lin1T", [128, ND * 4096], BF16)
        lin2T = sbuf(nc, s5, "a0_lin2T", [128, 32 * D], BF16)
        wt5 = WT2(p, nc, s5, trp, "a0_wt5", raw_width=2048)
        for j in range(32):
            dcols = [lin1T[:, ic * 4096 + j * 128: ic * 4096 + j * 128 + 128]
                     for ic in range(ND)]
            wt5.slab(io["lin1_w"][j * 128:(j + 1) * 128, :], D, dcols)
        for oro in range(ND):
            dcols = [lin2T[:, jc * D + oro * 128: jc * D + oro * 128 + 128]
                     for jc in range(32)]
            wt5.slab(io["lin2_w"][oro * 128:(oro + 1) * 128, 0:2048], 2048, dcols[:16])
            wt5.slab(io["lin2_w"][oro * 128:(oro + 1) * 128, 2048:4096], 2048, dcols[16:])
        b1c = sbuf(nc, s5, "a0_b1c", [128, 32])
        for j in range(32):
            load_col(p, b1c[:, j:j + 1], io["lin1_b"], j * 128, 128)
        hT = sbuf(nc, s5, "a0_hT", [128, 32 * 512], BF16)
        xlt_in = [sbuf(nc, s5, "a0_xlt0", [128, ND * 512], BF16)]
        xl1_t5 = sbuf(nc, s5, "a0_xl1t5", [128, D])
        ya = sbuf(nc, s5, "a0_ya", [128, D])
        u2 = sbuf(nc, s5, "a0_u2", [128, D])
        hps = [psum(nc, sD, f"a0_hps{i}", [128, 512]) for i in range(2)]
        yps = [psum(nc, sD, f"a0_yps{i}", [128, 512]) for i in range(2)]
        l2b_bc, t_l2b = bcast_vec(p, nc, s5, cm, "a0_l2b", io["lin2_b"], D, yps)
        p.barrier()
        last_hps = [None, None]
        last_yps = [None, None]
        last_xlt = [None, None]
        last_hT_reader = None
        last_xl5 = None
        last_u2d = None
        for tbk in range(8):
            xsl = 0
            txl = []
            for ic in range(ND):
                txl.append(p.dma("sync", xlt_in[xsl][:, ic * 512:(ic + 1) * 512],
                                 io["xln1t"][ic, :, tbk * 512:(tbk + 1) * 512],
                                 after=[last_xlt[xsl]]))
            th = {}
            for j in range(32):
                psl = j % 2
                mmt = None
                for ic in range(ND):
                    mmt = p.op("tensor", (lambda e, o=hps[psl][:],
                               l=lin1T[:, ic * 4096 + j * 128: ic * 4096 + j * 128 + 128],
                               r=xlt_in[xsl][:, ic * 512:(ic + 1) * 512],
                               s=(ic == 0), st=(ic == ND - 1):
                               e.matmul(o, l, r, start=s, stop=st)),
                               after=[txl[-1], last_hps[psl], last_hT_reader])
                th[j] = p.op("scalar", (lambda e, o=hT[:, j * 512:(j + 1) * 512],
                             i=hps[psl][:], b=b1c[:, j:j + 1]:
                             e.activation(o, i, AF.Gelu, bias=b)), after=[mmt])
                last_hps[psl] = th[j]
            last_xlt[xsl] = th[31]
            hT_readers = []
            for tsub in range(4):
                r0 = tbk * 512 + tsub * 128
                txl5 = p.dma("sync", xl1_t5[:], io["xln1"][r0:r0 + 128, :],
                             after=[last_xl5])
                mmt = [None, None]
                for oc in range(2):
                    for j in range(32):
                        mmt[oc] = p.op("tensor", (lambda e, o=yps[oc][:],
                                       l=hT[:, j * 512 + tsub * 128: j * 512 + tsub * 128 + 128],
                                       r=lin2T[:, j * D + oc * 512: j * D + (oc + 1) * 512],
                                       s=(j == 0), st=(j == 31):
                                       e.matmul(o, l, r, start=s, stop=st)),
                                       after=[th[31], last_yps[oc]])
                    hT_readers.append(mmt[oc])
                tev = []
                for oc in range(2):
                    t1 = p.op("vector", (lambda e, o=ya[:, oc * 512:(oc + 1) * 512],
                              i=yps[oc][:], b=l2b_bc[:, oc * 512:(oc + 1) * 512]:
                              e.scalar_tensor_tensor(out=o, in0=i, scalar=1.0, in1=b,
                                  op0=ALU.mult, op1=ALU.add)), after=[mmt[oc], t_l2b])
                    t2 = p.op("vector", (lambda e, o=u2[:, oc * 512:(oc + 1) * 512],
                              a=ya[:, oc * 512:(oc + 1) * 512],
                              b=xl1_t5[:, oc * 512:(oc + 1) * 512]:
                              e.tensor_add(o, a, b)), after=[t1, txl5, last_u2d])
                    last_yps[oc] = t2
                    tev.append(t2)
                last_xl5 = tev[-1]
                last_u2d = p.dma("sync", io["u2"][r0:r0 + 128, :], u2[:], after=tev)
            last_hT_reader = hT_readers[-1]
        p.barrier()
        s5.close()
        g2_bc, t_g2 = bcast_vec(p, nc, sD, cm, "a0_g2", io["ln2_g"], D, yps)
        b2_bc, t_b2 = bcast_vec(p, nc, sD, cm, "a0_b2", io["ln2_b"], D, yps)

        xa = sbuf(nc, sD, "a0_xa", [128, D])
        xb = sbuf(nc, sD, "a0_xb", [128, D])
        xc = sbuf(nc, sD, "a0_xc", [128, D])
        last6 = [None, None]
        last6st = None
        for ti in range(NT):
            r0 = ti * 128
            tu2 = p.dma("sync", xa[:], io["u2"][r0:r0 + 128, :],
                        after=[last6[0], last6st])
            tx = p.dma("sync", xb[:], io["x"][r0:r0 + 128, :], after=[last6[1]])
            tm, tr = ln_stats(p, xa[:], lns, after=[tu2])
            ta = ln_apply(p, xc[:], xa[:], lns, [tm, tr])
            tg = p.op("vector", (lambda e, o=xc[:], a=xc[:], b=g2_bc[:]:
                      e.tensor_mul(o, a, b)), after=[ta, t_g2])
            tb2 = p.op("vector", (lambda e, o=xa[:], a=xc[:], b=b2_bc[:]:
                       e.tensor_add(o, a, b)), after=[tg, t_b2, last6st])
            tm2, tr2 = ln_stats(p, xa[:], lns, after=[tb2])
            ta2 = ln_apply(p, xc[:], xa[:], lns, [tm2, tr2])
            last6[0] = ta2
            tx1 = p.op("vector", (lambda e, o=xa[:], a=xc[:], b=xb[:]:
                       e.tensor_add(o, a, b)), after=[ta2, tx])
            last6[1] = tx1
            last6st = p.dma("sync", io["x1"][r0:r0 + 128, :], xa[:], after=[tx1])
        p.barrier()



def _din(nc, name, shape, dtype=F32):
    return nc.dram_tensor(name, shape, dtype, kind="ExternalInput").ap()


def _dout(nc, name, shape, dtype=F32):
    return nc.dram_tensor(name, shape, dtype, kind="ExternalOutput").ap()


def _dtmp(nc, name, shape, dtype=F32):
    return nc.dram_tensor(name, shape, dtype, kind="Internal").ap()


def build_a0(debug=False):
    nc = bass.Bass("TRN2", target_bir_lowering=False)
    io = {
        "x": _din(nc, "x", [T, D]),
        "in_proj_w": _din(nc, "in_proj_w", [3 * D, D]),
        "in_proj_b": _din(nc, "in_proj_b", [3 * D]),
        "out_w": _din(nc, "out_w", [D, D]),
        "out_b": _din(nc, "out_b", [D]),
        "lin1_w": _din(nc, "lin1_w", [4096, D]),
        "lin1_b": _din(nc, "lin1_b", [4096]),
        "lin2_w": _din(nc, "lin2_w", [D, 4096]),
        "lin2_b": _din(nc, "lin2_b", [D]),
        "ln1_g": _din(nc, "ln1_g", [D]),
        "ln1_b": _din(nc, "ln1_b", [D]),
        "ln2_g": _din(nc, "ln2_g", [D]),
        "ln2_b": _din(nc, "ln2_b", [D]),
        "iden": _din(nc, "iden", [128, 128], BF16),
        "x1": _dout(nc, "x1", [T, D]),
        "src": _dtmp(nc, "a0src", [T, D]),
        "postt": _dtmp(nc, "a0postt", [D, T], BF16),
        "xln1": _dtmp(nc, "a0xln1", [T, D]),
        "xln1t": _dtmp(nc, "a0xln1t", [ND, 128, T], BF16),
        "u2": _dtmp(nc, "a0u2", [T, D]),
    }
    with ExitStack() as ctx:
        p = Prog(nc, ctx)
        cm = Common(p, nc, ctx, io)
        emit_a0(p, nc, ctx, cm, io)
        p.barrier()
        if debug:
            for nm, shp, dt in (("src", [T, D], F32), ("postt", [D, T], BF16),
                                ("xln1", [T, D], F32), ("u2", [T, D], F32),
                                ("xln1t", [ND, 128, T], BF16)):
                dbg = _dout(nc, "dbg_" + nm, shp, dt)
                p.dma("sync", dbg, io[nm])
            p.barrier()
        p.replay()
    return nc


IDEN = np.eye(128, dtype=np.float32).astype(ml_dtypes.bfloat16)


def _a0_inmaps(inputs, cores):
    m = []
    for c in cores:
        m.append({
            "x": np.ascontiguousarray(inputs["L0_attn_src"][c]),
            "in_proj_w": inputs["L0_attn_in_proj_w"],
            "in_proj_b": inputs["L0_attn_in_proj_b"],
            "out_w": inputs["L0_attn_out_w"],
            "out_b": inputs["L0_attn_out_b"],
            "lin1_w": inputs["L0_attn_lin1_w"],
            "lin1_b": inputs["L0_attn_lin1_b"],
            "lin2_w": inputs["L0_attn_lin2_w"],
            "lin2_b": inputs["L0_attn_lin2_b"],
            "ln1_g": inputs["L0_attn_ln1_g"],
            "ln1_b": inputs["L0_attn_ln1_b"],
            "ln2_g": inputs["L0_attn_ln2_g"],
            "ln2_b": inputs["L0_attn_ln2_b"],
            "iden": IDEN,
        })
    return m


def kernel(**inputs):
    return kernel_fused(**inputs)


def emit_f0(p, nc, ctx, cm, io):
    H, DH = 16, 64
    trp = TrPool(p, nc, ctx, cm, "f0trp")
    lns = LNS(nc, ctx, "f0_lns")

    with ExitStack() as sB:
        u1T = sbuf(nc, sB, "f0_u1T", [128, ND * T], BF16)
        zA = psum(nc, sB, "f0_zA", [128, 1024])
        zB = psum(nc, sB, "f0_zB", [128, 1024])
        avp = psum(nc, sB, "f0_avp", [128, 512])
        lbp = psum(nc, sB, "f0_lbp", [128, 512])
        g1_bc, t_g1 = bcast_vec(p, nc, sB, cm, "f0_g1", io["ln1_w"], D, [zA, zB])
        b1_bc, t_b1 = bcast_vec(p, nc, sB, cm, "f0_b1", io["ln1_b"], D, [zA, zB])
        s1 = sB.enter_context(ExitStack())
        x_t = [sbuf(nc, s1, f"f0_x{i}", [128, D]) for i in range(2)]
        xs_t = [sbuf(nc, s1, f"f0_xs{i}", [128, D]) for i in range(2)]
        u1_t = [sbuf(nc, s1, f"f0_u1{i}", [128, D]) for i in range(2)]
        u1bf = [sbuf(nc, s1, f"f0_u1bf{i}", [128, D], BF16) for i in range(2)]
        lastu = {}
        for ti in range(NT):
            sl = ti % 2
            r0 = ti * 128
            tx = p.dma("sync", x_t[sl][:], io["x"][r0:r0 + 128, :],
                       after=[lastu.get(("x", sl))])
            tm, tr = ln_stats(p, x_t[sl][:], lns, after=[tx])
            ts = ln_apply(p, xs_t[sl][:], x_t[sl][:], lns,
                          [tm, tr, lastu.get(("xsd", sl)), lastu.get(("bf", sl))])
            lastu[("x", sl)] = ts
            lastu[("xsd", sl)] = p.dma("sync", io["xs"][r0:r0 + 128, :],
                                       xs_t[sl][:], after=[ts])
            tm2, tr2 = ln_stats(p, xs_t[sl][:], lns, after=[ts])
            ta = ln_apply(p, u1_t[sl][:], xs_t[sl][:], lns, [tm2, tr2])
            tg = p.op("vector", (lambda e, o=u1_t[sl][:], a=u1_t[sl][:], b=g1_bc[:]:
                      e.tensor_mul(o, a, b)), after=[ta, t_g1])
            tb = p.op("vector", (lambda e, o=u1_t[sl][:], a=u1_t[sl][:], b=b1_bc[:]:
                      e.tensor_add(o, a, b)), after=[tg, t_b1])
            tbf = p.op("scalar", (lambda e, o=u1bf[sl][:], i=u1_t[sl][:]:
                                  e.copy(o, i)), after=[tb, lastu.get(("bf", sl))])
            tlast = None
            for ic in range(ND):
                tlast = trp.tr(u1bf[sl][:, ic * 128:(ic + 1) * 128],
                               u1T[:, ic * T + r0: ic * T + r0 + 128], after=[tbf])
            lastu[("bf", sl)] = tlast
        p.barrier()
        s1.close()

        wqkv = sbuf(nc, sB, "f0_wqkv", [128, ND * 3 * D], BF16)
        t_w = []
        for ic in range(ND):
            t_w.append(p.dma("gpsimd", wqkv[:, ic * 3 * D:(ic + 1) * 3 * D],
                             io["w_qkv"][ic * 128:(ic + 1) * 128, :]))
        qT = sbuf(nc, sB, "f0_qT", [128, 2 * T], BF16)
        kT = sbuf(nc, sB, "f0_kT", [128, 2 * T], BF16)
        VW = 260
        vg = sbuf(nc, sB, "f0_vg", [128, NT * VW], BF16)
        ptile = sbuf(nc, sB, "f0_pt", [128, 1024], BF16)
        postT = sbuf(nc, sB, "f0_postT", [128, 512])
        linv_r = sbuf(nc, sB, "f0_linv", [1, 512])
        lbc_sb = sbuf(nc, sB, "f0_lbc", [128, 512])
        postbf = sbuf(nc, sB, "f0_postbf", [128, 512], BF16)

        mm_slot = [zA[:, 0:512], zB[:, 0:512]]
        last_slot = [t_g1, t_b1]
        t_vones = p.op("vector", (lambda e, o=vg[:]: e.memset(o, 1.0)))
        t_postdma = {}
        last_pt = None
        last_avp = None
        last_lbp = None
        last_postT = None
        last_pbf = None

        for g in range(4):
            for part in ("q", "k"):
                woff = (0 if part == "q" else D) + g * 256
                dest = qT if part == "q" else kT
                for rb in range(2):
                    for tc in range(8):
                        psl = tc % 2
                        mmt = None
                        for ic in range(ND):
                            mmt = p.op("tensor", (lambda e, o=mm_slot[psl],
                                       l=wqkv[:, ic * 3 * D + woff + rb * 128:
                                              ic * 3 * D + woff + (rb + 1) * 128],
                                       r=u1T[:, ic * T + tc * 512: ic * T + (tc + 1) * 512],
                                       s=(ic == 0), st=(ic == ND - 1):
                                       e.matmul(o, l, r, start=s, stop=st)),
                                       after=[t_w[ic], last_slot[psl]])
                        tev = p.op("scalar", (lambda e,
                                   o=dest[:, rb * T + tc * 512: rb * T + (tc + 1) * 512],
                                   i=mm_slot[psl], s=(0.125 if part == "k" else 1.0):
                                   e.activation(o, i, AF.Identity, scale=s)),
                                   after=[mmt])
                        last_slot[psl] = tev
            woff = 2 * D + g * 256
            for ti in range(NT):
                for hh in range(2):
                    psl = hh % 2
                    mmt = None
                    for ic in range(ND):
                        mmt = p.op("tensor", (lambda e, o=mm_slot[psl][:, 0:128],
                                   l=u1T[:, ic * T + ti * 128: ic * T + ti * 128 + 128],
                                   r=wqkv[:, ic * 3 * D + woff + hh * 128:
                                          ic * 3 * D + woff + (hh + 1) * 128],
                                   s=(ic == 0), st=(ic == ND - 1):
                                   e.matmul(o, l, r, start=s, stop=st)),
                                   after=[t_w[ic], last_slot[psl]])
                    base = ti * VW + (2 * hh) * 65
                    tev = p.op("vector", (lambda e,
                               o=vg[:, base:base + 130].rearrange("p (s c) -> p s c", c=65)[:, :, 0:64],
                               i=mm_slot[psl][:, 0:128].rearrange("p (s c) -> p s c", c=64):
                               e.tensor_copy(o, i)), after=[mmt, t_vones])
                    last_slot[psl] = tev
            for hq in range(4):
                h = 4 * g + hq
                rb = hq // 2
                pb = 64 * (hq % 2)
                for tc in range(8):
                    avtags = []
                    for si in range(NT):
                        zsl = (si // 2) % 2
                        zoff = (si % 2) * 512
                        zt = [zA, zB][zsl]
                        tz = p.op("tensor", (lambda e, o=zt[:, zoff:zoff + 512],
                                  l=kT[pb:pb + 64, rb * T + si * 128: rb * T + si * 128 + 128],
                                  r=qT[pb:pb + 64, rb * T + tc * 512: rb * T + (tc + 1) * 512]:
                                  e.matmul(o, l, r, start=True, stop=True)),
                                  after=[last_slot[zsl]])
                        if si % 2 == 1:
                            tex = p.op("scalar", (lambda e, o=ptile[:], i=zt[:],
                                       : e.activation(o, i, AF.Exp)),
                                       after=[tz, last_pt])
                            last_slot[zsl] = tex
                            for k in range(2):
                                sj = si - 1 + k
                                ta = p.op("tensor", (lambda e, o=avp[0:65, :],
                                          l=vg[:, sj * VW + hq * 65: sj * VW + hq * 65 + 65],
                                          r=ptile[:, k * 512:(k + 1) * 512],
                                          s=(sj == 0), st=(sj == NT - 1):
                                          e.matmul(o, l, r, start=s, stop=st)),
                                          after=[tex, last_avp])
                                avtags.append(ta)
                            last_pt = avtags[-1]
                    t_li = p.op("vector", (lambda e, o=linv_r[:], i=avp[64:65, :]:
                                e.reciprocal(o, i)), after=[avtags[-1]])
                    t_lb = p.op("tensor", (lambda e, o=lbp[0:64, :], l=cm.ones[0:1, 0:64],
                                r=linv_r[:]: e.matmul(o, l, r, start=True, stop=True)),
                                after=[t_li, cm.t_ones, last_lbp])
                    t_lbc = p.op("vector", (lambda e, o=lbc_sb[0:64, :], i=lbp[0:64, :]:
                                 e.tensor_copy(o, i)), after=[t_lb])
                    last_lbp = t_lbc
                    t_pn = p.op("vector", (lambda e, o=postbf[0:64, :], a=avp[0:64, :],
                                b=lbc_sb[0:64, :]: e.tensor_mul(o, a, b)),
                                after=[t_lbc, last_pbf])
                    last_avp = t_pn
                    t_pd = p.dma("sync", io["postt"][h * 64:(h + 1) * 64,
                                                     tc * 512:(tc + 1) * 512],
                                 postbf[0:64, :], after=[t_pn])
                    last_pbf = t_pd
        p.barrier()

    with ExitStack() as sC:
        wpr = sbuf(nc, sC, "f0_wpr", [128, ND * D], BF16)
        t_wp = []
        for ic in range(ND):
            t_wp.append(p.dma("gpsimd", wpr[:, ic * D:(ic + 1) * D],
                              io["w_proj"][ic * 128:(ic + 1) * 128, :]))
        pblk = sbuf(nc, sC, "f0_pblk", [128, ND * 512], BF16)
        stage = sbuf(nc, sC, "f0_stage", [128, ND * 512], BF16)
        xs_t4 = sbuf(nc, sC, "f0_xs4", [128, D])
        ua = sbuf(nc, sC, "f0_ua", [128, D])
        ub = sbuf(nc, sC, "f0_ub", [128, D])
        xl2 = sbuf(nc, sC, "f0_xl2", [128, D])
        xl2bf = sbuf(nc, sC, "f0_xl2bf", [128, D], BF16)
        at2 = [psum(nc, sC, f"f0_at2{i}", [128, 512]) for i in range(2)]
        g2_bc, t_g2 = bcast_vec(p, nc, sC, cm, "f0_g2", io["ln2_w"], D, at2)
        b2_bc, t_b2 = bcast_vec(p, nc, sC, cm, "f0_b2", io["ln2_b"], D, at2)
        p.barrier()
        last_ev4 = [None, None]
        last_stage = None
        last_pblk = None
        last_xs4 = None
        last_u2md = None
        last_tbf = None
        for tb in range(8):
            tpb = []
            for ic in range(ND):
                tpb.append(p.dma("sync", pblk[:, ic * 512:(ic + 1) * 512],
                                 io["postt"][ic * 128:(ic + 1) * 128,
                                             tb * 512:(tb + 1) * 512],
                                 after=[last_pblk]))
            tstg = []
            for tsub in range(4):
                r0 = tb * 512 + tsub * 128
                txs = p.dma("sync", xs_t4[:], io["xs"][r0:r0 + 128, :],
                            after=[last_xs4])
                mmt = [None, None]
                for oc in range(2):
                    for ic in range(ND):
                        mmt[oc] = p.op("tensor", (lambda e, o=at2[oc][:],
                                       l=pblk[:, ic * 512 + tsub * 128: ic * 512 + tsub * 128 + 128],
                                       r=wpr[:, ic * D + oc * 512: ic * D + (oc + 1) * 512],
                                       s=(ic == 0), st=(ic == ND - 1):
                                       e.matmul(o, l, r, start=s, stop=st)),
                                       after=[tpb[-1], last_ev4[oc], t_wp[ic]])
                tu = []
                for oc in range(2):
                    t2 = p.op("vector", (lambda e, o=ub[:, oc * 512:(oc + 1) * 512],
                              a=at2[oc][:], b=xs_t4[:, oc * 512:(oc + 1) * 512]:
                              e.tensor_add(o, a, b)), after=[mmt[oc], txs, last_u2md])
                    last_ev4[oc] = t2
                    tu.append(t2)
                last_xs4 = tu[-1]
                last_u2md = p.dma("sync", io["u2m"][r0:r0 + 128, :], ub[:], after=tu)
                tm, tr = ln_stats(p, ub[:], lns, after=tu)
                ta = ln_apply(p, ua[:], ub[:], lns, [tm, tr])
                tg = p.op("vector", (lambda e, o=ua[:], a=ua[:], b=g2_bc[:]:
                          e.tensor_mul(o, a, b)), after=[ta, t_g2])
                tb2 = p.op("vector", (lambda e, o=xl2[:], a=ua[:], b=b2_bc[:]:
                           e.tensor_add(o, a, b)), after=[tg, t_b2, last_tbf])
                tbf = p.op("scalar", (lambda e, o=xl2bf[:], i=xl2[:]: e.copy(o, i)),
                           after=[tb2])
                last_tbf = tbf
                for ic in range(ND):
                    tstg.append(trp.tr(xl2bf[:, ic * 128:(ic + 1) * 128],
                                       stage[:, ic * 512 + tsub * 128: ic * 512 + tsub * 128 + 128],
                                       after=[tbf, last_stage]))
            xlt_dma = []
            for ic in range(ND):
                xlt_dma.append(p.dma("sync", io["xln2t"][ic, :, tb * 512:(tb + 1) * 512],
                                     stage[:, ic * 512:(ic + 1) * 512], after=[tstg[-1]]))
            last_stage = xlt_dma[-1]
            last_pblk = mmt[1]
        p.barrier()

    with ExitStack() as sE:
        gw = sbuf(nc, sE, "f0_gw", [128, ND * 8], BF16)
        t_gw = []
        for ic in range(ND):
            t_gw.append(p.dma("gpsimd", gw[:, ic * 8:(ic + 1) * 8],
                              io["gate_w"][ic * 128:(ic + 1) * 128, :]))
        xlt_in = sbuf(nc, sE, "f0_xlt5", [128, ND * 128], BF16)
        glog = sbuf(nc, sE, "f0_glog", [128, 8])
        gmax = sbuf(nc, sE, "f0_gmax", [128, 8])
        gidx = sbuf(nc, sE, "f0_gidx", [128, 8], U32)
        gidxf = sbuf(nc, sE, "f0_gidxf", [128, 8])
        iota8i = sbuf(nc, sE, "f0_iota8i", [128, 8], I32)
        iota8 = sbuf(nc, sE, "f0_iota8", [128, 8])
        masks = sbuf(nc, sE, "f0_masks", [128, NT * 8])
        gps = psum(nc, sE, "f0_gps", [128, 512])
        gb_bc, t_gb = bcast_vec(p, nc, sE, cm, "f0_gb", io["gate_b"], 8, [gps])
        t_io = p.op("gpsimd", (lambda e, o=iota8i[:]:
                    e.iota(o, [[1, 8]], channel_multiplier=0)))
        t_iof = p.op("vector", (lambda e, o=iota8[:], i=iota8i[:]:
                     e.tensor_copy(o, i)), after=[t_io])
        last_xlt = None
        last_gps = t_gb
        for ti in range(NT):
            r0 = ti * 128
            txl = []
            for ic in range(ND):
                txl.append(p.dma("sync", xlt_in[:, ic * 128:(ic + 1) * 128],
                                 io["xln2t"][ic, :, r0:r0 + 128], after=[last_xlt]))
            mmt = None
            for ic in range(ND):
                mmt = p.op("tensor", (lambda e, o=gps[:, 0:8],
                           l=xlt_in[:, ic * 128:(ic + 1) * 128],
                           r=gw[:, ic * 8:(ic + 1) * 8],
                           s=(ic == 0), st=(ic == ND - 1):
                           e.matmul(o, l, r, start=s, stop=st)),
                           after=[txl[-1], t_gw[ic], last_gps])
            t_lg = p.op("vector", (lambda e, o=glog[:], a=gps[:, 0:8], b=gb_bc[:]:
                        e.tensor_add(o, a, b)), after=[mmt, t_gb])
            last_gps = t_lg
            t_mx = p.op("vector", (lambda e, o=gmax[:], i=glog[:]: e.max(o, i)),
                        after=[t_lg])
            t_ix = p.op("vector", (lambda e, o=gidx[:], m=gmax[:], i=glog[:]:
                        e.max_index(o, m, i)), after=[t_mx])
            t_if = p.op("vector", (lambda e, o=gidxf[:], i=gidx[:]:
                        e.tensor_copy(o, i)), after=[t_ix])
            t_m = p.op("vector", (lambda e, o=masks[:, ti * 8:(ti + 1) * 8],
                       i=iota8[:], s=gidxf[:, 0:1]:
                       e.tensor_single_scalar(o, i, s, ALU.is_equal)),
                       after=[t_if, t_iof])
            last_xlt = t_m
        p.barrier()

        with ExitStack() as s6:
            we1 = sbuf(nc, s6, "f0_we1", [128, ND * 4096], BF16)
            we2 = sbuf(nc, s6, "f0_we2", [128, 32 * D], BF16)
            hT = sbuf(nc, s6, "f0_hT", [128, 32 * 512], BF16)
            xlt6 = sbuf(nc, s6, "f0_xlt6", [128, ND * 512], BF16)
            yout = sbuf(nc, s6, "f0_yout", [128, D])
            hps = [psum(nc, s6, f"f0_hps{i}", [128, 512]) for i in range(2)]
            yps = [psum(nc, s6, f"f0_yps{i}", [128, 512]) for i in range(2)]
            last_we = None
            last_hps = [None, None]
            last_yps = [None, None]
            last_xlt6 = None
            last_yout = None
            for ex in range(8):
                t_we = []
                for ic in range(ND):
                    t_we.append(p.dma("gpsimd", we1[:, ic * 4096:(ic + 1) * 4096],
                                      io["w1"][ex, ic * 128:(ic + 1) * 128, :],
                                      after=[last_we]))
                for jc in range(32):
                    t_we.append(p.dma("gpsimd", we2[:, jc * D:(jc + 1) * D],
                                      io["w2"][ex, jc * 128:(jc + 1) * 128, :],
                                      after=[last_we]))
                t_we_last = t_we[-1]
                for tbk in range(8):
                    txl = []
                    for ic in range(ND):
                        txl.append(p.dma("sync", xlt6[:, ic * 512:(ic + 1) * 512],
                                         io["xln2t"][ic, :, tbk * 512:(tbk + 1) * 512],
                                         after=[last_xlt6]))
                    th = None
                    for j in range(32):
                        psl = j % 2
                        mmt = None
                        for ic in range(ND):
                            mmt = p.op("tensor", (lambda e, o=hps[psl][:],
                                       l=we1[:, ic * 4096 + j * 128: ic * 4096 + j * 128 + 128],
                                       r=xlt6[:, ic * 512:(ic + 1) * 512],
                                       s=(ic == 0), st=(ic == ND - 1):
                                       e.matmul(o, l, r, start=s, stop=st)),
                                       after=[txl[-1], last_hps[psl], t_we_last])
                        th = p.op("scalar", (lambda e, o=hT[:, j * 512:(j + 1) * 512],
                                  i=hps[psl][:]:
                                  e.activation(o, i, AF.Gelu_apprx_tanh)), after=[mmt])
                        last_hps[psl] = th
                    last_xlt6 = th
                    for tsub in range(4):
                        ti = tbk * 4 + tsub
                        r0 = ti * 128
                        mmt = [None, None]
                        for oc in range(2):
                            for j in range(32):
                                mmt[oc] = p.op("tensor", (lambda e, o=yps[oc][:],
                                               l=hT[:, j * 512 + tsub * 128: j * 512 + tsub * 128 + 128],
                                               r=we2[:, j * D + oc * 512: j * D + (oc + 1) * 512],
                                               s=(j == 0), st=(j == 31):
                                               e.matmul(o, l, r, start=s, stop=st)),
                                               after=[th, last_yps[oc]])
                        tev = []
                        for oc in range(2):
                            t1 = p.op("vector", (lambda e, o=yout[:, oc * 512:(oc + 1) * 512],
                                      i=yps[oc][:], m=masks[:, ti * 8 + ex: ti * 8 + ex + 1]:
                                      e.tensor_scalar_mul(o, i, m)),
                                      after=[mmt[oc], last_yout])
                            last_yps[oc] = t1
                            tev.append(t1)
                        last_yout = p.dma("gpsimd", io["moe"][r0:r0 + 128, :], yout[:],
                                          after=tev,
                                          **({} if ex == 0 else {"accum_op": ALU.add}))
                last_we = last_yout
            p.barrier()

    with ExitStack() as sF:
        xa = sbuf(nc, sF, "f0_xa", [128, D])
        xb = sbuf(nc, sF, "f0_xb", [128, D])
        xc = sbuf(nc, sF, "f0_xc", [128, D])
        last7 = [None, None, None]
        lastst = None
        for ti in range(NT):
            r0 = ti * 128
            tu = p.dma("sync", xa[:], io["u2m"][r0:r0 + 128, :], after=[last7[0]])
            tm_ = p.dma("sync", xb[:], io["moe"][r0:r0 + 128, :], after=[last7[1]])
            tadd = p.op("vector", (lambda e, o=xa[:], a=xa[:], b=xb[:]:
                        e.tensor_add(o, a, b)), after=[tu, tm_])
            last7[0] = tadd
            tx = p.dma("sync", xb[:], io["x"][r0:r0 + 128, :], after=[tadd])
            tm, tr = ln_stats(p, xa[:], lns, after=[tadd])
            ta = ln_apply(p, xc[:], xa[:], lns, [tm, tr, lastst])
            tx2 = p.op("vector", (lambda e, o=xc[:], a=xc[:], b=xb[:]:
                       e.tensor_add(o, a, b)), after=[ta, tx])
            last7[1] = tx2
            last7[2] = tx2
            lastst = p.dma("sync", io["x2"][r0:r0 + 128, :], xc[:], after=[tx2])
        p.barrier()


def build_f0():
    nc = bass.Bass("TRN2", target_bir_lowering=False)
    io = {
        "x": _din(nc, "x", [T, D]),
        "w_qkv": _din(nc, "w_qkv", [D, 3 * D]),
        "w_proj": _din(nc, "w_proj", [D, D]),
        "ln1_w": _din(nc, "ln1_w", [D]),
        "ln1_b": _din(nc, "ln1_b", [D]),
        "ln2_w": _din(nc, "ln2_w", [D]),
        "ln2_b": _din(nc, "ln2_b", [D]),
        "gate_w": _din(nc, "gate_w", [D, 8]),
        "gate_b": _din(nc, "gate_b", [8]),
        "w1": _din(nc, "w1", [8, D, 4096]),
        "w2": _din(nc, "w2", [8, 4096, D]),
        "iden": _din(nc, "iden", [128, 128], BF16),
        "x2": _dout(nc, "x2", [T, D]),
        "xs": _dtmp(nc, "f0xs", [T, D]),
        "postt": _dtmp(nc, "f0postt", [D, T], BF16),
        "u2m": _dtmp(nc, "f0u2m", [T, D]),
        "xln2t": _dtmp(nc, "f0xln2t", [ND, 128, T], BF16),
        "moe": _dtmp(nc, "f0moe", [T, D]),
    }
    with ExitStack() as ctx:
        p = Prog(nc, ctx)
        cm = Common(p, nc, ctx, io)
        emit_f0(p, nc, ctx, cm, io)
        p.barrier()
        p.replay()
    return nc


def _f0_inmaps(inputs, x1_shards, cores):
    m = []
    for j, c in enumerate(cores):
        m.append({
            "x": np.ascontiguousarray(x1_shards[j]),
            "w_qkv": inputs["L0_ffn_w_qkv"],
            "w_proj": inputs["L0_ffn_w_proj"],
            "ln1_w": inputs["L0_ffn_ln1_w"],
            "ln1_b": inputs["L0_ffn_ln1_b"],
            "ln2_w": inputs["L0_ffn_ln2_w"],
            "ln2_b": inputs["L0_ffn_ln2_b"],
            "gate_w": inputs["L0_ffn_gate_w"],
            "gate_b": inputs["L0_ffn_gate_b"],
            "w1": inputs["L0_ffn_w1"],
            "w2": inputs["L0_ffn_w2"],
            "iden": IDEN,
        })
    return m


def emit_a1(p, nc, ctx, cm, io):
    H = 16
    trp = TrPool(p, nc, ctx, cm, "a1trp")
    lns = LNS(nc, ctx, "a1_lns")

    DEC = [float(np.log(1.0 - 2.0 ** (-5.0 - h))) for h in range(H)]

    tab = ExitStack()
    ctx.enter_context(tab)
    i32r = sbuf(nc, ctx, "a1_i32r", [1, 128], I32)
    posr = sbuf(nc, ctx, "a1_posr", [1, 128])
    ang64 = sbuf(nc, ctx, "a1_ang64", [1, 64])
    angrep = sbuf(nc, ctx, "a1_angrep", [1, D])
    tmp32 = sbuf(nc, ctx, "a1_tmp32", [1, 32])
    t_i = p.op("gpsimd", (lambda e, o=i32r[:]: e.iota(o, [[1, 128]], channel_multiplier=0)))
    t_p = p.op("vector", (lambda e, o=posr[:], i=i32r[:]: e.tensor_copy(o, i)), after=[t_i])
    t_a1 = p.op("vector", (lambda e, o=tmp32[:], i=posr[0:1, 0:32]:
                 e.tensor_scalar_mul(o, i, float(-np.log(10000.0) / 31.0))), after=[t_p])
    t_a2 = p.op("scalar", (lambda e, o=tmp32[:], i=tmp32[:]:
                 e.activation(o, i, AF.Exp)), after=[t_a1])
    t_a3 = p.op("vector", (lambda e, o=ang64[0:1, 0:64:2], i=tmp32[:]:
                 e.tensor_copy(o, i)), after=[t_a2])
    t_a4 = p.op("vector", (lambda e, o=ang64[0:1, 1:64:2], i=tmp32[:]:
                 e.tensor_copy(o, i)), after=[t_a2, t_a3])
    t_ar = []
    for rr in range(16):
        t_ar.append(p.op("vector", (lambda e, o=angrep[0:1, rr * 64:(rr + 1) * 64],
                    i=ang64[:]: e.tensor_copy(o, i)), after=[t_a4, t_a3]))
    t_ar_last = t_ar[-1]

    with ExitStack() as s1:
        x_t = [sbuf(nc, s1, f"a1_x{i}", [128, D]) for i in range(2)]
        w_t = [sbuf(nc, s1, f"a1_w{i}", [128, D]) for i in range(2)]
        wbf = [sbuf(nc, s1, f"a1_wbf{i}", [128, D], BF16) for i in range(2)]
        stage1 = sbuf(nc, s1, "a1_stage1", [128, ND * 512], BF16)
        lastu = {}
        last_st1 = None
        for tb in range(8):
            tstg = []
            for tsub in range(4):
                ti = tb * 4 + tsub
                sl = ti % 2
                r0 = ti * 128
                tx = p.dma("sync", x_t[sl][:], io["x"][r0:r0 + 128, :],
                           after=[lastu.get(("x", sl))])
                tm, tr = ln_stats(p, x_t[sl][:], lns, after=[tx])
                ts = ln_apply(p, w_t[sl][:], x_t[sl][:], lns,
                              [tm, tr, lastu.get(("wbf", sl))])
                lastu[("x", sl)] = ts
                tbf = p.op("scalar", (lambda e, o=wbf[sl][:], i=w_t[sl][:]:
                                      e.copy(o, i)), after=[ts, lastu.get(("bf", sl))])
                lastu[("wbf", sl)] = tbf
                for ic in range(ND):
                    tstg.append(trp.tr(wbf[sl][:, ic * 128:(ic + 1) * 128],
                                       stage1[:, ic * 512 + tsub * 128: ic * 512 + tsub * 128 + 128],
                                       after=[tbf, last_st1]))
                lastu[("bf", sl)] = tstg[-1]
            dmas = []
            for ic in range(ND):
                dmas.append(p.dma("sync", io["xwt"][ic, :, tb * 512:(tb + 1) * 512],
                                  stage1[:, ic * 512:(ic + 1) * 512], after=[tstg[-1]]))
            last_st1 = dmas[-1]
        p.barrier()

    qrT = sbuf(nc, ctx, "a1_qrT", [128, 8 * T], BF16)
    krT = sbuf(nc, ctx, "a1_krT", [128, 8 * T], BF16)
    io["_qrT"] = qrT
    io["_krT"] = krT
    with ExitStack() as s2:
        xwt_in = sbuf(nc, s2, "a1_xwt2", [128, ND * 128], BF16)
        qps = [psum(nc, s2, f"a1_qps{i}", [128, 512]) for i in range(2)]
        cps = psum(nc, s2, "a1_cps", [128, 1024])
        dk_r = sbuf(nc, s2, "a1_dkr", [1, 16])
        omr_r = sbuf(nc, s2, "a1_omr", [1, 16])
        t_d1 = p.op("vector", (lambda e, o=dk_r[:], i=posr[0:1, 0:16]:
                    e.tensor_scalar(o, i, float(-np.log(2.0)), float(-5.0 * np.log(2.0)),
                                    ALU.mult, ALU.add)), after=[t_p])
        t_d2 = p.op("scalar", (lambda e, o=omr_r[:], i=dk_r[:]:
                    e.activation(o, i, AF.Exp)), after=[t_d1])
        t_d3 = p.op("vector", (lambda e, o=omr_r[:], i=omr_r[:]:
                    e.tensor_scalar(o, i, -1.0, 1.0, ALU.mult, ALU.add)), after=[t_d2])
        t_d4 = p.op("scalar", (lambda e, o=dk_r[:], i=omr_r[:]:
                    e.activation(o, i, AF.Ln)), after=[t_d3])
        t_d5 = p.op("vector", (lambda e, o=omr_r[:], i=omr_r[:]:
                    e.tensor_scalar(o, i, -1.0, 1.0, ALU.mult, ALU.add)), after=[t_d4])
        t_d6 = p.op("vector", (lambda e, o=omr_r[:], i=omr_r[:]:
                    e.reciprocal(o, i)), after=[t_d5])
        omr_bc = sbuf(nc, s2, "a1_omrbc", [128, 16])
        t_ob = p.op("tensor", (lambda e, o=cps[:, 0:16], l=cm.ones[:], r=omr_r[:]:
                    e.matmul(o, l, r, start=True, stop=True)), after=[t_d6, cm.t_ones])
        t_ob2 = p.op("vector", (lambda e, o=omr_bc[:], i=cps[:, 0:16]:
                     e.tensor_copy(o, i)), after=[t_ob])
        dk_bc = sbuf(nc, s2, "a1_dkbc", [128, 16])
        t_db = p.op("tensor", (lambda e, o=cps[:, 16:32], l=cm.ones[:], r=dk_r[:]:
                    e.matmul(o, l, r, start=True, stop=True)), after=[t_d4, cm.t_ones, t_ob2])
        t_db2 = p.op("vector", (lambda e, o=dk_bc[:], i=cps[:, 16:32]:
                     e.tensor_copy(o, i)), after=[t_db])
        ep_t = sbuf(nc, s2, "a1_ept", [128, 16])
        t_e1 = p.op("tensor", (lambda e, o=cps[:, 32:48], l=posr[:], r=dk_r[:]:
                    e.matmul(o, l, r, start=True, stop=True)), after=[t_d4, t_p, t_db2])
        t_e2 = p.op("vector", (lambda e, o=ep_t[:], i=cps[:, 32:48]:
                    e.tensor_scalar_mul(o, i, -1.0)), after=[t_e1])
        t_e3 = p.op("scalar", (lambda e, o=ep_t[:], i=ep_t[:]:
                    e.activation(o, i, AF.Exp)), after=[t_e2])
        u_ps = sbuf(nc, s2, "a1_ups", [128, D])
        prow = sbuf(nc, s2, "a1_prow", [1, 128])
        s2a = s2.enter_context(ExitStack())
        wq = sbuf(nc, s2a, "a1_wq", [128, ND * D], BF16)
        wk = wq
        bq_bc, t_bq = bcast_vec(p, nc, s2a, cm, "a1_bq", io["bq"], D, qps)
        bk_bc, t_bk = bcast_vec(p, nc, s2a, cm, "a1_bk", io["bk"], D, qps)
        t_bk = p.op("vector", (lambda e, o=bk_bc[:], i=bk_bc[:]:
                    e.tensor_scalar_mul(o, i, 0.125)), after=[t_bk])
        cosf = sbuf(nc, s2a, "a1_cosf", [128, D])
        sinf = sbuf(nc, s2a, "a1_sinf", [128, D])
        rri = sbuf(nc, s2a, "a1_rri", [128, D], I32)
        rrf = sbuf(nc, s2a, "a1_rrf", [128, D])
        q_f = sbuf(nc, s2a, "a1_qf", [128, D])
        t1_f = sbuf(nc, s2a, "a1_t1f", [128, D])
        t2_f = sbuf(nc, s2a, "a1_t2f", [128, D])
        qrbf = sbuf(nc, s2a, "a1_qrbf", [128, D], BF16)
        ap_t = sbuf(nc, s2a, "a1_apt", [128, 16])
        rr_t = sbuf(nc, s2a, "a1_rrt", [128, 16])
        sc1 = sbuf(nc, s2a, "a1_sc1", [128, 1])
        sc2 = sbuf(nc, s2a, "a1_sc2", [128, 1])
        TWO_PI = float(2.0 * np.pi)

        def build_trig(r0, after):
            tpr = p.op("vector", (lambda e, o=prow[:], i=posr[:], c=float(r0):
                       e.tensor_scalar_add(o, i, c)),
                       after=list(after) + [last_q.get("trigmm")])
            war = last_q.get("trig")
            tm1 = p.op("tensor", (lambda e, o=cps[:, 0:512], l=prow[:], r=angrep[0:1, 0:512]:
                       e.matmul(o, l, r, start=True, stop=True)), after=[tpr, t_ar_last, war])
            tm2 = p.op("tensor", (lambda e, o=cps[:, 512:1024], l=prow[:],
                       r=angrep[0:1, 512:1024]:
                       e.matmul(o, l, r, start=True, stop=True)), after=[tpr, t_ar_last, war])
            last_q["trigmm"] = tm2
            tags = []
            for fn, dest, shift in (("sin", sinf, 0.0), ("cos", cosf, float(np.pi / 2))):
                tu = p.op("vector", (lambda e, o=u_ps[:], i=cps[:], sh=shift:
                          e.tensor_scalar(o, i, 1.0 / TWO_PI, sh / TWO_PI,
                                          ALU.mult, ALU.add)), after=[tm1, tm2, tags[-1] if tags else None])
                ti1 = p.op("vector", (lambda e, o=rri[:], i=u_ps[:]:
                           e.tensor_copy(o, i)), after=[tu])
                ti2 = p.op("vector", (lambda e, o=rrf[:], i=rri[:]:
                           e.tensor_copy(o, i)), after=[ti1])
                ts_ = p.op("vector", (lambda e, o=u_ps[:], a=u_ps[:], b=rrf[:]:
                           e.tensor_sub(o, a, b)), after=[ti2])
                tsc = p.op("vector", (lambda e, o=u_ps[:], i=u_ps[:]:
                           e.tensor_scalar_mul(o, i, TWO_PI)), after=[ts_])
                tsn = p.op("scalar", (lambda e, o=dest[:], i=u_ps[:]:
                           e.activation(o, i, AF.Sin)), after=[tsc])
                tags.append(tsn)
            last_q["trig"] = tags[-1]
            return tags

        def build_ap(r0, after):
            s0 = r0 % 512
            tpr = p.op("vector", (lambda e, o=prow[:], i=posr[:]:
                       e.tensor_scalar_add(o, i, float(r0 + 1))), after=after)
            tm = p.op("tensor", (lambda e, o=cps[:, 0:16], l=prow[:], r=dk_r[:]:
                      e.matmul(o, l, r, start=True, stop=True)), after=[tpr, t_d4])
            te = p.op("scalar", (lambda e, o=rr_t[:], i=cps[:, 0:16]:
                      e.activation(o, i, AF.Exp)), after=[tm])
            t1 = p.op("vector", (lambda e, o=rr_t[:], i=rr_t[:]:
                      e.tensor_scalar(o, i, -1.0, 1.0, ALU.mult, ALU.add)), after=[te])
            t2 = p.op("vector", (lambda e, o=rr_t[:], a=rr_t[:], b=omr_bc[:]:
                      e.tensor_mul(o, a, b)), after=[t1, t_ob2])
            t3 = p.op("scalar", (lambda e, o=rr_t[:], i=rr_t[:]:
                      e.activation(o, i, AF.Ln)), after=[t2])
            t4 = p.op("scalar", (lambda e, o=rr_t[:], i=rr_t[:]:
                      e.activation(o, i, AF.Exp, scale=-0.5)), after=[t3])
            t5 = p.op("vector", (lambda e, o=ap_t[:], i=dk_bc[:]:
                      e.tensor_scalar_mul(o, i, float(s0))), after=[t_db2, t4])
            return t4, t5

        pos_col = sbuf(nc, s2a, "a1_poscol", [128, 1])
        t_pc = p.op("gpsimd", (lambda e, o=rri[:, 0:1]:
                    e.iota(o, [[1, 1]], channel_multiplier=1)))
        t_pc2 = p.op("vector", (lambda e, o=pos_col[:], i=rri[:, 0:1]:
                     e.tensor_copy(o, i)), after=[t_pc])

        last_q = {}
        for part, wsb, bias_bc, t_bias, destT in (
                ("q", wq, bq_bc, t_bq, qrT), ("k", wk, bk_bc, t_bk, krT)):
            twl = []
            for ic in range(ND):
                twl.append(p.dma("gpsimd", wsb[:, ic * D:(ic + 1) * D],
                                 io["Wq" if part == "q" else "Wk"][ic * 128:(ic + 1) * 128, :],
                                 after=[last_q.get("wconsume")]))
            for ti in range(NT):
                r0 = ti * 128
                txw = []
                for ic in range(ND):
                    txw.append(p.dma("sync", xwt_in[:, ic * 128:(ic + 1) * 128],
                                     io["xwt"][ic, :, r0:r0 + 128],
                                     after=[last_q.get("xwt")]))
                mmt = [None, None]
                for oc in range(2):
                    for ic in range(ND):
                        mmt[oc] = p.op("tensor", (lambda e, o=qps[oc][:],
                                       l=xwt_in[:, ic * 128:(ic + 1) * 128],
                                       r=wsb[:, ic * D + oc * 512: ic * D + (oc + 1) * 512],
                                       s=(ic == 0), st=(ic == ND - 1):
                                       e.matmul(o, l, r, start=s, stop=st)),
                                       after=[txw[-1], twl[ic], last_q.get(("ev", oc))])
                sc_k = 0.125 if part == "k" else 1.0
                tq = []
                for oc in range(2):
                    tq.append(p.op("vector", (lambda e, o=q_f[:, oc * 512:(oc + 1) * 512],
                              i=qps[oc][:], b=bias_bc[:, oc * 512:(oc + 1) * 512], s=sc_k:
                              e.scalar_tensor_tensor(out=o, in0=i, scalar=s, in1=b,
                                  op0=ALU.mult, op1=ALU.add)),
                              after=[mmt[oc], t_bias, last_q.get("qf")]))
                    last_q[("ev", oc)] = tq[-1]
                ttr = build_trig(r0, tq)
                tr1 = p.op("vector", (lambda e, o=t1_f[:, 0:D:2], i=q_f[:, 1:D:2]:
                           e.tensor_scalar_mul(o, i, -1.0)), after=tq + [last_q.get("t1")])
                tr2 = p.op("vector", (lambda e, o=t1_f[:, 1:D:2], i=q_f[:, 0:D:2]:
                           e.tensor_copy(o, i)), after=tq + [tr1])
                tm1 = p.op("vector", (lambda e, o=t2_f[:], a=q_f[:], b=cosf[:]:
                           e.tensor_mul(o, a, b)), after=tq + ttr + [last_q.get("t2")])
                tm2 = p.op("vector", (lambda e, o=t1_f[:], a=t1_f[:], b=sinf[:]:
                           e.tensor_mul(o, a, b)), after=[tr1, tr2] + ttr)
                tad = p.op("vector", (lambda e, o=t2_f[:], a=t2_f[:], b=t1_f[:]:
                           e.tensor_add(o, a, b)), after=[tm1, tm2])
                last_q["t1"] = tad
                last_q["qf"] = tad
                if part == "q":
                    s0 = r0 % 512
                    tp1 = p.op("vector", (lambda e, o=sc1[:], i=pos_col[:], c=float(s0):
                               e.tensor_scalar_add(o, i, c)), after=[t_pc2, last_q.get("sc1")])
                    tp2 = p.op("vector", (lambda e, o=rr_t[:], a=dk_bc[:], b=sc1[:]:
                               e.tensor_scalar_mul(o, a, b)), after=[tp1, t_db2, last_q.get("rrt")])
                    last_q["sc1"] = tp2
                    tp3 = p.op("scalar", (lambda e, o=rr_t[:], i=rr_t[:]:
                               e.activation(o, i, AF.Exp)), after=[tp2])
                    tp4 = p.op("vector", (lambda e, o=sc2[:], i=pos_col[:], c=float(r0 + 1):
                               e.tensor_scalar_add(o, i, c)), after=[t_pc2, last_q.get("sc2")])
                    tp5 = p.op("vector", (lambda e, o=ap_t[:], a=dk_bc[:], b=sc2[:]:
                               e.tensor_scalar_mul(o, a, b)), after=[tp4, last_q.get("apt")])
                    last_q["sc2"] = tp5
                    tp6 = p.op("scalar", (lambda e, o=ap_t[:], i=ap_t[:]:
                               e.activation(o, i, AF.Exp)), after=[tp5])
                    tp7 = p.op("vector", (lambda e, o=ap_t[:], i=ap_t[:]:
                               e.tensor_scalar(o, i, -1.0, 1.0, ALU.mult, ALU.add)), after=[tp6])
                    tp8 = p.op("vector", (lambda e, o=ap_t[:], a=ap_t[:], b=omr_bc[:]:
                               e.tensor_mul(o, a, b)), after=[tp7, t_ob2])
                    tp9 = p.op("scalar", (lambda e, o=ap_t[:], i=ap_t[:]:
                               e.activation(o, i, AF.Ln)), after=[tp8])
                    tp10 = p.op("scalar", (lambda e, o=ap_t[:], i=ap_t[:]:
                                e.activation(o, i, AF.Exp, scale=-0.5)), after=[tp9])
                    tfold = p.op("vector", (lambda e, o=rr_t[:], a=rr_t[:], b=ap_t[:]:
                                 e.tensor_mul(o, a, b)), after=[tp3, tp10])
                    last_q["apt"] = tfold
                    last_q["rrt"] = tfold
                    ftab, t_ft = rr_t, tfold
                else:
                    ftab, t_ft = ep_t, t_e3
                tcast = []
                for h in range(H):
                    tcast.append(p.op("vector", (lambda e,
                                 o=qrbf[:, h * 64:(h + 1) * 64],
                                 i=t2_f[:, h * 64:(h + 1) * 64],
                                 s=ftab[:, h:h + 1]:
                                 e.tensor_scalar_mul(o, i, s)),
                                 after=[tad, t_ft, last_q.get("qrbf")]))
                last_q["t2"] = tcast[-1]
                ttp = None
                for ic in range(ND):
                    ttp = trp.tr(qrbf[:, ic * 128:(ic + 1) * 128],
                                 destT[:, ic * T + r0: ic * T + r0 + 128],
                                 after=[tcast[-1]])
                last_q["xwt"] = ttp
                last_q["qrbf"] = ttp
                last_q["wconsume"] = tq[-1]
        p.barrier()
        s2a.close()

        wv = sbuf(nc, s2, "a1_wv", [128, ND * 2048], BF16)
        vstage = sbuf(nc, s2, "a1_vstage", [128, 16 * 129], BF16)
        gstage = sbuf(nc, s2, "a1_gstage", [128, 2048], BF16)
        bv_bc1, t_bv1 = bcast_vec(p, nc, s2, cm, "a1_bv1", io["bv"][0:1024], D, qps)
        bv_bc2, t_bv2 = bcast_vec(p, nc, s2, cm, "a1_bv2", io["bv"][1024:2048], D, qps)
        t_vone = p.op("vector", (lambda e, o=vstage[:]: e.memset(o, 1.0)))
        for part in ("v", "g"):
            t_wv = []
            for ic in range(ND):
                t_wv.append(p.dma("gpsimd", wv[:, ic * 2048:(ic + 1) * 2048],
                                  io["Wv" if part == "v" else "Wg"][ic * 128:(ic + 1) * 128, :],
                                  after=[last_q.get("vst")]))
            if part == "g":
                bv_bc1, t_bv1 = bcast_vec(p, nc, s2, cm, "a1_bg1", io["bg"][0:1024], D, qps)
                bv_bc2, t_bv2 = bcast_vec(p, nc, s2, cm, "a1_bg2", io["bg"][1024:2048], D, qps)
            for ti in range(NT):
                r0 = ti * 128
                txw = []
                for ic in range(ND):
                    txw.append(p.dma("sync", xwt_in[:, ic * 128:(ic + 1) * 128],
                                     io["xwt"][ic, :, r0:r0 + 128],
                                     after=[last_q.get("xwt")]))
                for half in range(2):
                    bbc = [bv_bc1, bv_bc2][half]
                    tbb = [t_bv1, t_bv2][half]
                    for oc in range(2):
                        mmt = None
                        for ic in range(ND):
                            mmt = p.op("tensor", (lambda e, o=qps[oc][:],
                                       l=xwt_in[:, ic * 128:(ic + 1) * 128],
                                       r=wv[:, ic * 2048 + half * 1024 + oc * 512:
                                            ic * 2048 + half * 1024 + (oc + 1) * 512],
                                       s=(ic == 0), st=(ic == ND - 1):
                                       e.matmul(o, l, r, start=s, stop=st)),
                                       after=[txw[-1], t_wv[ic], last_q.get(("pv", oc))])
                        co = half * 1024 + oc * 512
                        if part == "v":
                            base = (co // 128) * 129
                            tev = p.op("vector", (lambda e,
                                       o=vstage[:, base:base + 4 * 129].rearrange(
                                           "p (s c) -> p s c", c=129)[:, :, 0:128],
                                       i=qps[oc][:].rearrange("p (s c) -> p s c", c=128),
                                       b=bbc[:, (co % 1024):(co % 1024) + 512].rearrange(
                                           "p (s c) -> p s c", c=128):
                                       e.scalar_tensor_tensor(out=o, in0=i, scalar=1.0,
                                           in1=b, op0=ALU.mult, op1=ALU.add)),
                                       after=[mmt, tbb, t_vone, last_q.get("vst")])
                            last_q["vst"] = tev
                        else:
                            t_g1 = p.op("vector", (lambda e, o=u_ps[:, 0:512],
                                        i=qps[oc][:], b=bbc[:, (co % 1024):(co % 1024) + 512]:
                                        e.tensor_add(o, i, b)),
                                        after=[mmt, tbb, last_q.get("gs1")])
                            tev = p.op("scalar", (lambda e, o=gstage[:, co:co + 512],
                                       i=u_ps[:, 0:512]:
                                       e.activation(o, i, AF.Silu)),
                                       after=[t_g1, last_q.get("vst")])
                            last_q["vst"] = tev
                            last_q["gs1"] = tev
                        last_q[("pv", oc)] = tev
                if part == "v":
                    tdm = p.dma("sync", io["vr"][r0:r0 + 128, :], vstage[:],
                                after=[last_q["vst"]])
                else:
                    tdm = p.dma("sync", io["sg"][r0:r0 + 128, :], gstage[:],
                                after=[last_q["vst"]])
                last_q["vst"] = tdm
                last_q["gs1"] = tdm
        p.barrier()

    with ExitStack() as s3:
        lns128 = LNS(nc, s3, "a1_lns128", width=128)
        zpp = [psum(nc, s3, f"a1_zpp{i}", [128, 512]) for i in range(2)]
        ops = [psum(nc, s3, f"a1_ops{i}", [128, 129]) for i in range(4)]
        vr_in = [sbuf(nc, s3, f"a1_vrin{i}", [128, 129], BF16) for i in range(2)]
        qkbf = sbuf(nc, s3, "a1_qkbf", [128, 512], BF16)
        cmask = sbuf(nc, s3, "a1_cmask", [128, 4 * 512])
        grid = sbuf(nc, s3, "a1_grid", [128, 512])
        i512 = sbuf(nc, s3, "a1_i512", [1, 512], I32)
        f512 = sbuf(nc, s3, "a1_f512", [1, 512])
        pcol = sbuf(nc, s3, "a1_pcol", [128, 1])
        den = sbuf(nc, s3, "a1_den", [128, 1])
        sgt = sbuf(nc, s3, "a1_sgt", [128, 128], BF16)
        ret = sbuf(nc, s3, "a1_ret", [128, 128])
        obf = sbuf(nc, s3, "a1_obf", [128, 128], BF16)
        stage3 = sbuf(nc, s3, "a1_stage3", [128, 512], BF16)
        t_i5 = p.op("gpsimd", (lambda e, o=i512[:]: e.iota(o, [[1, 512]], channel_multiplier=0)))
        t_f5 = p.op("vector", (lambda e, o=f512[:], i=i512[:]: e.tensor_copy(o, i)), after=[t_i5])
        t_pc3 = p.op("gpsimd", (lambda e, o=i512[0:1, 0:1]: 0) if False else
                     (lambda e, o=i512[0:1, 0:1]: e.iota(o, [[1, 1]], channel_multiplier=0)))
        icol = sbuf(nc, s3, "a1_icol", [128, 1], I32)
        t_ic = p.op("gpsimd", (lambda e, o=icol[:]: e.iota(o, [[1, 1]], channel_multiplier=1)))
        t_ic2 = p.op("vector", (lambda e, o=pcol[:], i=icol[:]: e.tensor_copy(o, i)), after=[t_ic])
        t_gr = p.op("tensor", (lambda e, o=zpp[0][:], l=cm.ones[:], r=f512[:]:
                    e.matmul(o, l, r, start=True, stop=True)), after=[t_f5, cm.t_ones])
        t_gr2 = p.op("vector", (lambda e, o=grid[:], i=zpp[0][:]: e.tensor_copy(o, i)), after=[t_gr])
        tmsk = []
        for j in range(4):
            tj = p.op("vector", (lambda e, o=den[:], i=pcol[:], c=float(128 * j):
                      e.tensor_scalar_add(o, i, c)), after=[t_ic2, tmsk[-1] if tmsk else None])
            tmsk.append(p.op("vector", (lambda e, o=cmask[:, j * 512:(j + 1) * 512],
                        i=grid[:], s=den[:]:
                        e.tensor_single_scalar(o, i, s, ALU.is_ge)), after=[tj, t_gr2]))
        t_cm = tmsk[-1]
        p.barrier()

        last3 = {}
        for h in range(H):
            hp, pb = h // 2, 64 * (h % 2)
            dec = DEC[h]
            for sc in range(8):
                av_started = False
                avt = None
                for tt in range(4 * sc + 4):
                    tvr = p.dma("scalar", vr_in[tt % 2][:],
                                io["vr"][tt * 128:(tt + 1) * 128, h * 129:(h + 1) * 129],
                                after=[last3.get(("vr", tt % 2))])
                    zsl = tt % 2
                    tz = p.op("tensor", (lambda e, o=zpp[zsl][:],
                              l=krT[pb:pb + 64, hp * T + tt * 128: hp * T + tt * 128 + 128],
                              r=qrT[pb:pb + 64, hp * T + sc * 512: hp * T + (sc + 1) * 512]:
                              e.matmul(o, l, r, start=True, stop=True)),
                              after=[last3.get(("z", zsl))])
                    f_tile = float(np.exp(dec * (512 * sc - 128 * tt)))
                    diag = (tt >= 4 * sc)
                    if diag:
                        jj = tt - 4 * sc
                        ta_ = p.op("scalar", (lambda e, o=grid[:], i=zpp[zsl][:], f=f_tile:
                                   e.activation(o, i, AF.Identity, scale=f)),
                                   after=[tz, last3.get("qk"), last3.get("qk2"), t_cm,
                                          last3.get("grd")])
                        tev = p.op("vector", (lambda e, o=qkbf[:], a=grid[:],
                                   b=cmask[:, jj * 512:(jj + 1) * 512]:
                                   e.tensor_mul(o, a, b)), after=[ta_, last3.get("qkb")])
                        last3[("z", zsl)] = ta_
                        last3["grd"] = tev
                    else:
                        tev = p.op("scalar", (lambda e, o=qkbf[:], i=zpp[zsl][:], f=f_tile:
                                   e.activation(o, i, AF.Identity, scale=f)),
                                   after=[tz, last3.get("qk"), last3.get("qkb"),
                                          last3.get("qk2")])
                        last3[("z", zsl)] = tev
                    last3["qkb"] = tev
                    last3["qk"] = tev
                    newav = []
                    for sub in range(4):
                        avt = p.op("tensor", (lambda e, o=ops[sub][0:128, :],
                                   l=qkbf[:, sub * 128:(sub + 1) * 128],
                                   r=vr_in[tt % 2][:],
                                   s=(not av_started), st=(tt == 4 * sc + 3):
                                   e.matmul(o, l, r, start=s, stop=st)),
                                   after=[tev, tvr, last3.get(("ev", sub))])
                        newav.append(avt)
                    av_started = True
                    last3[("vr", tt % 2)] = newav[-1]
                    last3["qk2"] = newav[-1]
                for sub in range(4):
                    r0 = sc * 512 + sub * 128
                    t_d = p.op("vector", (lambda e, o=den[:], i=ops[sub][:, 128:129]:
                               e.tensor_scalar_mul(o, i, -1.0)),
                               after=[newav[-1], last3.get("den")])
                    t_d1 = p.op("vector", (lambda e, o=den[:], a=ops[sub][:, 128:129], b=den[:]:
                                e.tensor_max(o, a, b)), after=[t_d])
                    t_d2 = p.op("vector", (lambda e, o=den[:], i=den[:]:
                                e.tensor_scalar_max(o, i, 1.0)), after=[t_d1])
                    t_d3 = p.op("vector", (lambda e, o=den[:], i=den[:]:
                                e.reciprocal(o, i)), after=[t_d2])
                    t_r = p.op("vector", (lambda e, o=ret[:], i=ops[sub][:, 0:128],
                               s=den[:]: e.tensor_scalar_mul(o, i, s)),
                               after=[t_d3, last3.get("ret")])
                    last3[("ev", sub)] = t_r
                    last3["den"] = t_r
                    tm, tr = ln_stats(p, ret[:], lns128, after=[t_r])
                    ta = ln_apply(p, ret[:], ret[:], lns128, [tm, tr])
                    tsg = p.dma("gpsimd", sgt[:],
                                io["sg"][r0:r0 + 128, h * 128:(h + 1) * 128],
                                after=[last3.get("sgt")])
                    tob = p.op("vector", (lambda e, o=obf[:], a=ret[:], b=sgt[:]:
                               e.tensor_mul(o, a, b)), after=[ta, tsg, last3.get("obf")])
                    last3["sgt"] = tob
                    last3["ret"] = tob
                    tst = trp.tr(obf[:], stage3[:, sub * 128:(sub + 1) * 128],
                                 after=[tob, last3.get("st3")])
                    last3["obf"] = tst
                    last3["st3d"] = tst
                tdm = p.dma("sync", io["ot"][h, :, sc * 512:(sc + 1) * 512], stage3[:],
                            after=[last3["st3d"]])
                last3["st3"] = tdm
        p.barrier()

    with ExitStack() as sC:
        wo = sbuf(nc, sC, "a1_wo", [128, 16 * D], BF16)
        t_wo = []
        for jc in range(16):
            t_wo.append(p.dma("gpsimd", wo[:, jc * D:(jc + 1) * D],
                              io["Wo"][jc * 128:(jc + 1) * 128, :]))
        otb = sbuf(nc, sC, "a1_otb", [128, 16 * 512], BF16)
        xb_ = sbuf(nc, sC, "a1_xb", [128, D])
        ya_ = sbuf(nc, sC, "a1_ya", [128, D])
        yb_ = sbuf(nc, sC, "a1_yb", [128, D])
        at2 = [psum(nc, sC, f"a1_at2{i}", [128, 512]) for i in range(2)]
        bo_bc, t_bo = bcast_vec(p, nc, sC, cm, "a1_bo", io["bo"], D, at2)
        p.barrier()
        last4 = {}
        for tb in range(8):
            tob_ = []
            for jc in range(16):
                tob_.append(p.dma("sync", otb[:, jc * 512:(jc + 1) * 512],
                                  io["ot"][jc, :, tb * 512:(tb + 1) * 512],
                                  after=[last4.get("otb")]))
            for tsub in range(4):
                r0 = tb * 512 + tsub * 128
                mmt = [None, None]
                for oc in range(2):
                    for jc in range(16):
                        mmt[oc] = p.op("tensor", (lambda e, o=at2[oc][:],
                                       l=otb[:, jc * 512 + tsub * 128: jc * 512 + tsub * 128 + 128],
                                       r=wo[:, jc * D + oc * 512: jc * D + (oc + 1) * 512],
                                       s=(jc == 0), st=(jc == 15):
                                       e.matmul(o, l, r, start=s, stop=st)),
                                       after=[tob_[-1], t_wo[jc], last4.get(("ev", oc))])
                tev = []
                for oc in range(2):
                    t1 = p.op("vector", (lambda e, o=ya_[:, oc * 512:(oc + 1) * 512],
                              i=at2[oc][:], b=bo_bc[:, oc * 512:(oc + 1) * 512]:
                              e.scalar_tensor_tensor(out=o, in0=i, scalar=1.0, in1=b,
                                  op0=ALU.mult, op1=ALU.add)),
                              after=[mmt[oc], t_bo, last4.get("ya")])
                    last4[("ev", oc)] = t1
                    tev.append(t1)
                last4["ya"] = tev[-1]
                tx = p.dma("sync", xb_[:], io["x"][r0:r0 + 128, :], after=[last4.get("xb")])
                tm, tr = ln_stats(p, ya_[:], lns, after=tev)
                ta = ln_apply(p, yb_[:], ya_[:], lns, [tm, tr, last4.get("x3d")])
                tx3 = p.op("vector", (lambda e, o=yb_[:], a=yb_[:], b=xb_[:]:
                           e.tensor_add(o, a, b)), after=[ta, tx])
                last4["xb"] = tx3
                last4["x3d"] = p.dma("sync", io["x3"][r0:r0 + 128, :], yb_[:],
                                     after=[tx3])
            last4["otb"] = mmt[1]
        p.barrier()


def build_a1(debug=False):
    nc = bass.Bass("TRN2", target_bir_lowering=False)
    io = {
        "x": _din(nc, "x", [T, D]),
        "Wq": _din(nc, "Wq", [D, D]),
        "bq": _din(nc, "bq", [D]),
        "Wk": _din(nc, "Wk", [D, D]),
        "bk": _din(nc, "bk", [D]),
        "Wv": _din(nc, "Wv", [D, 2 * D]),
        "bv": _din(nc, "bv", [2 * D]),
        "Wg": _din(nc, "Wg", [D, 2 * D]),
        "bg": _din(nc, "bg", [2 * D]),
        "Wo": _din(nc, "Wo", [2 * D, D]),
        "bo": _din(nc, "bo", [D]),
        "iden": _din(nc, "iden", [128, 128], BF16),
        "x3": _dout(nc, "x3", [T, D]),
        "xwt": _dtmp(nc, "a1xwt", [ND, 128, T], BF16),
        "vr": _dtmp(nc, "a1vr", [T, 16 * 129], BF16),
        "sg": _dtmp(nc, "a1sg", [T, 2 * D], BF16),
        "ot": _dtmp(nc, "a1ot", [16, 128, T], BF16),
    }
    with ExitStack() as ctx:
        p = Prog(nc, ctx)
        cm = Common(p, nc, ctx, io)
        emit_a1(p, nc, ctx, cm, io)
        p.barrier()
        if debug:
            for nm, shp, dt in (("vr", [T, 16 * 129], BF16), ("sg", [T, 2 * D], BF16),
                                 ("ot", [16, 128, T], BF16), ("xwt", [ND, 128, T], BF16)):
                dbg = _dout(nc, "dbg_" + nm, shp, dt)
                p.dma("sync", dbg, io[nm])
            dbq = _dout(nc, "dbg_qrT", [128, 8 * T], BF16)
            dbk = _dout(nc, "dbg_krT", [128, 8 * T], BF16)
            p.dma("sync", dbq, io["_qrT"][:])
            p.dma("sync", dbk, io["_krT"][:])
            p.barrier()
        p.replay()
    return nc


def _a1_inmaps(inputs, x2_shards, cores):
    m = []
    for j, c in enumerate(cores):
        m.append({
            "x": np.ascontiguousarray(x2_shards[j]),
            "Wq": inputs["L1_attn_Wq"], "bq": inputs["L1_attn_bq"],
            "Wk": inputs["L1_attn_Wk"], "bk": inputs["L1_attn_bk"],
            "Wv": inputs["L1_attn_Wv"], "bv": inputs["L1_attn_bv"],
            "Wg": inputs["L1_attn_Wg"], "bg": inputs["L1_attn_bg"],
            "Wo": inputs["L1_attn_Wo"], "bo": inputs["L1_attn_bo"],
            "iden": IDEN,
        })
    return m


def emit_f1(p, nc, ctx, cm, io, debug=False):
    trp = TrPool(p, nc, ctx, cm, "f1trp")
    lns = LNS(nc, ctx, "f1_lns")

    with ExitStack() as s1:
        nw_ps = [psum(nc, s1, f"f1_nwps{i}", [128, 512]) for i in range(2)]
        nw_bc, t_nw = bcast_vec(p, nc, s1, cm, "f1_nw", io["norm_weight"], D, nw_ps)
        x_t = [sbuf(nc, s1, f"f1_x{i}", [128, D]) for i in range(2)]
        w_t = [sbuf(nc, s1, f"f1_w{i}", [128, D]) for i in range(2)]
        wbf = [sbuf(nc, s1, f"f1_wbf{i}", [128, D], BF16) for i in range(2)]
        stage1 = sbuf(nc, s1, "f1_stage1", [128, ND * 512], BF16)
        lastu = {}
        last_st1 = None
        for tb in range(8):
            tstg = []
            for tsub in range(4):
                ti = tb * 4 + tsub
                sl = ti % 2
                r0 = ti * 128
                tx = p.dma("sync", x_t[sl][:], io["x"][r0:r0 + 128, :],
                           after=[lastu.get(("x", sl))])
                tm, tr = ln_stats(p, x_t[sl][:], lns, after=[tx])
                ts = ln_apply(p, w_t[sl][:], x_t[sl][:], lns, [tm, tr])
                lastu[("x", sl)] = ts
                tm2, tr2 = ln_stats(p, w_t[sl][:], lns, after=[ts],
                                    eps=1e-8, mean_zero=True)
                ta = p.op("vector", (lambda e, o=w_t[sl][:], i=w_t[sl][:],
                          r=lns.rstd[:]: e.tensor_scalar_mul(o, i, r)),
                          after=[tr2, lastu.get(("wbf", sl))])
                tg = p.op("vector", (lambda e, o=w_t[sl][:], a=w_t[sl][:], b=nw_bc[:]:
                          e.tensor_mul(o, a, b)), after=[ta, t_nw])
                tbf = p.op("scalar", (lambda e, o=wbf[sl][:], i=w_t[sl][:]:
                                      e.copy(o, i)), after=[tg, lastu.get(("bf", sl))])
                lastu[("wbf", sl)] = tbf
                for ic in range(ND):
                    tstg.append(trp.tr(wbf[sl][:, ic * 128:(ic + 1) * 128],
                                       stage1[:, ic * 512 + tsub * 128: ic * 512 + tsub * 128 + 128],
                                       after=[tbf, last_st1]))
                lastu[("bf", sl)] = tstg[-1]
            dmas = []
            for ic in range(ND):
                dmas.append(p.dma("sync", io["nt"][ic, :, tb * 512:(tb + 1) * 512],
                                  stage1[:, ic * 512:(ic + 1) * 512], after=[tstg[-1]]))
            last_st1 = dmas[-1]
        p.barrier()

    combs = sbuf(nc, ctx, "f1_combs", [128, NT * 8])
    io["_combs"] = combs
    with ExitStack() as s2:
        rwT = sbuf(nc, s2, "f1_rwT", [128, ND * 8])
        rwbf = sbuf(nc, s2, "f1_rwbf", [128, ND * 8], BF16)
        t_rw = []
        for ic in range(ND):
            for ex in range(8):
                t_rw.append(load_col(p, rwT[:, ic * 8 + ex: ic * 8 + ex + 1],
                                     io["router_w"][ex, :], ic * 128, 128))
        t_rwb = p.op("vector", (lambda e, o=rwbf[:], i=rwT[:]: e.tensor_copy(o, i)),
                     after=[t_rw[-1]])
        xlt_in = sbuf(nc, s2, "f1_xlt2", [128, ND * 128], BF16)
        probs = sbuf(nc, s2, "f1_probs", [128, 8])
        gmax = sbuf(nc, s2, "f1_gmax", [128, 8])
        gidx = sbuf(nc, s2, "f1_gidx", [128, 8], U32)
        gidxf = sbuf(nc, s2, "f1_gidxf", [128, 8])
        iota8i = sbuf(nc, s2, "f1_iota8i", [128, 8], I32)
        iota8 = sbuf(nc, s2, "f1_iota8", [128, 8])
        wsum = sbuf(nc, s2, "f1_wsum", [128, 1])
        wn = sbuf(nc, s2, "f1_wn", [128, 2])
        eq2 = sbuf(nc, s2, "f1_eq2", [128, 8])
        gps = psum(nc, s2, "f1_gps", [128, 512])
        t_io = p.op("gpsimd", (lambda e, o=iota8i[:]:
                    e.iota(o, [[1, 8]], channel_multiplier=0)))
        t_iof = p.op("vector", (lambda e, o=iota8[:], i=iota8i[:]:
                     e.tensor_copy(o, i)), after=[t_io])
        last_xlt = None
        last_gps = None
        for ti in range(NT):
            r0 = ti * 128
            txl = []
            for ic in range(ND):
                txl.append(p.dma("sync", xlt_in[:, ic * 128:(ic + 1) * 128],
                                 io["nt"][ic, :, r0:r0 + 128], after=[last_xlt]))
            mmt = None
            for ic in range(ND):
                mmt = p.op("tensor", (lambda e, o=gps[:, 0:8],
                           l=xlt_in[:, ic * 128:(ic + 1) * 128],
                           r=rwbf[:, ic * 8:(ic + 1) * 8],
                           s=(ic == 0), st=(ic == ND - 1):
                           e.matmul(o, l, r, start=s, stop=st)),
                           after=[txl[-1], t_rwb, last_gps])
            t_pr = p.op("scalar", (lambda e, o=probs[:], i=gps[:, 0:8]:
                        e.activation(o, i, AF.Sigmoid)), after=[mmt])
            last_gps = t_pr
            t_mx = p.op("vector", (lambda e, o=gmax[:], i=probs[:]: e.max(o, i)),
                        after=[t_pr])
            t_ix = p.op("vector", (lambda e, o=gidx[:], m=gmax[:], i=probs[:]:
                        e.max_index(o, m, i)), after=[t_mx])
            t_if = p.op("vector", (lambda e, o=gidxf[:], i=gidx[:]:
                        e.tensor_copy(o, i)), after=[t_ix])
            t_ws = p.op("vector", (lambda e, o=wsum[:], i=gmax[:, 0:2]:
                        e.reduce_sum(o, i, axis=AX.X)), after=[t_mx])
            t_w2 = p.op("vector", (lambda e, o=wsum[:], i=wsum[:]:
                        e.tensor_scalar_add(o, i, 1e-6)), after=[t_ws])
            t_w3 = p.op("vector", (lambda e, o=wsum[:], i=wsum[:]:
                        e.reciprocal(o, i)), after=[t_w2])
            t_w4 = p.op("vector", (lambda e, o=wn[:], i=gmax[:, 0:2], s=wsum[:]:
                        e.tensor_scalar_mul(o, i, s)), after=[t_w3])
            t_c1 = p.op("vector", (lambda e, o=combs[:, ti * 8:(ti + 1) * 8],
                        i=iota8[:], s1=gidxf[:, 0:1], s2=wn[:, 0:1]:
                        e.tensor_scalar(o, i, s1, s2, ALU.is_equal, ALU.mult)),
                        after=[t_if, t_w4, t_iof])
            t_c2 = p.op("vector", (lambda e, o=eq2[:],
                        i=iota8[:], s1=gidxf[:, 1:2], s2=wn[:, 1:2]:
                        e.tensor_scalar(o, i, s1, s2, ALU.is_equal, ALU.mult)),
                        after=[t_c1])
            t_c3 = p.op("vector", (lambda e, o=combs[:, ti * 8:(ti + 1) * 8],
                        a=combs[:, ti * 8:(ti + 1) * 8], b=eq2[:]:
                        e.tensor_add(o, a, b)), after=[t_c2])
            last_xlt = t_c3
        p.barrier()

    with ExitStack() as s3:
        sw1 = sbuf(nc, s3, "f1_sw1", [128, ND * 4096], BF16)
        sw2 = sbuf(nc, s3, "f1_sw2", [128, 32 * D], BF16)
        t_w1, t_w2_ = [], []
        for ic in range(ND):
            t_w1.append(p.dma("gpsimd", sw1[:, ic * 4096:(ic + 1) * 4096],
                              io["sw1"][ic * 128:(ic + 1) * 128, :]))
        for jc in range(32):
            t_w2_.append(p.dma("gpsimd", sw2[:, jc * D:(jc + 1) * D],
                               io["sw2"][jc * 128:(jc + 1) * 128, :]))
        sb1c = sbuf(nc, s3, "f1_sb1c", [128, 32])
        t_b1 = [load_col(p, sb1c[:, j:j + 1], io["sb1"], j * 128, 128)
                for j in range(32)]
        hT = sbuf(nc, s3, "f1_hT", [128, 32 * 512], BF16)
        xlt3 = sbuf(nc, s3, "f1_xlt3", [128, ND * 512], BF16)
        yout = sbuf(nc, s3, "f1_yout", [128, D])
        hps = [psum(nc, s3, f"f1_hps{i}", [128, 512]) for i in range(2)]
        yps = [psum(nc, s3, f"f1_yps{i}", [128, 512]) for i in range(2)]
        sb2_bc, t_b2 = bcast_vec(p, nc, s3, cm, "f1_sb2", io["sb2"], D, yps)
        last_hps = [None, None]
        last_yps = [t_b2, t_b2]
        last_xlt3 = None
        last_yout = None
        for tbk in range(8):
            txl = []
            for ic in range(ND):
                txl.append(p.dma("sync", xlt3[:, ic * 512:(ic + 1) * 512],
                                 io["nt"][ic, :, tbk * 512:(tbk + 1) * 512],
                                 after=[last_xlt3]))
            th = None
            for j in range(32):
                psl = j % 2
                mmt = None
                for ic in range(ND):
                    mmt = p.op("tensor", (lambda e, o=hps[psl][:],
                               l=sw1[:, ic * 4096 + j * 128: ic * 4096 + j * 128 + 128],
                               r=xlt3[:, ic * 512:(ic + 1) * 512],
                               s=(ic == 0), st=(ic == ND - 1):
                               e.matmul(o, l, r, start=s, stop=st)),
                               after=[txl[-1], t_w1[ic], last_hps[psl]])
                th = p.op("scalar", (lambda e, o=hT[:, j * 512:(j + 1) * 512],
                          i=hps[psl][:], b=sb1c[:, j:j + 1]:
                          e.activation(o, i, AF.Relu, bias=b)),
                          after=[mmt, t_b1[j]])
                last_hps[psl] = th
            last_xlt3 = th
            for tsub in range(4):
                r0 = tbk * 512 + tsub * 128
                mmt = [None, None]
                for oc in range(2):
                    for j in range(32):
                        mmt[oc] = p.op("tensor", (lambda e, o=yps[oc][:],
                                       l=hT[:, j * 512 + tsub * 128: j * 512 + tsub * 128 + 128],
                                       r=sw2[:, j * D + oc * 512: j * D + (oc + 1) * 512],
                                       s=(j == 0), st=(j == 31):
                                       e.matmul(o, l, r, start=s, stop=st)),
                                       after=[th, t_w2_[-1], last_yps[oc]])
                tev = []
                for oc in range(2):
                    t1 = p.op("vector", (lambda e, o=yout[:, oc * 512:(oc + 1) * 512],
                              i=yps[oc][:], b=sb2_bc[:, oc * 512:(oc + 1) * 512]:
                              e.scalar_tensor_tensor(out=o, in0=i, scalar=1.0, in1=b,
                                  op0=ALU.mult, op1=ALU.add)),
                              after=[mmt[oc], t_b2, last_yout])
                    last_yps[oc] = t1
                    tev.append(t1)
                last_yout = p.dma("sync", io["acc"][r0:r0 + 128, :], yout[:], after=tev)
        p.barrier()
        if debug:
            p.dma("sync", io["dbg_accmid"], io["acc"])
            p.barrier()

    with ExitStack() as s6:
        we1 = sbuf(nc, s6, "f1_we1", [128, ND * 4096], BF16)
        we2 = sbuf(nc, s6, "f1_we2", [128, 32 * D], BF16)
        eb1c = sbuf(nc, s6, "f1_eb1c", [128, 32])
        eb2_bc = sbuf(nc, s6, "f1_eb2bc", [128, D])
        hT = sbuf(nc, s6, "f1_hT6", [128, 32 * 512], BF16)
        xlt6 = sbuf(nc, s6, "f1_xlt6", [128, ND * 512], BF16)
        yout = sbuf(nc, s6, "f1_yout6", [128, D])
        hps = [psum(nc, s6, f"f1_hps6{i}", [128, 512]) for i in range(2)]
        yps = [psum(nc, s6, f"f1_yps6{i}", [128, 512]) for i in range(2)]
        last_we = None
        last_hps = [None, None]
        last_yps = [None, None]
        last_xlt6 = None
        last_yout = None
        t_eb2 = None
        t_eb2_reader = None
        for ex in range(8):
            t_we = []
            for ic in range(ND):
                t_we.append(p.dma("gpsimd", we1[:, ic * 4096:(ic + 1) * 4096],
                                  io["ew1"][ex, ic * 128:(ic + 1) * 128, :],
                                  after=[last_we]))
            for jc in range(32):
                t_we.append(p.dma("gpsimd", we2[:, jc * D:(jc + 1) * D],
                                  io["ew2"][ex, jc * 128:(jc + 1) * 128, :],
                                  after=[last_we]))
            t_eb1 = [load_col(p, eb1c[:, j:j + 1], io["eb1"][ex, :], j * 128, 128)
                     for j in range(32)]
            _, t_eb2 = bcast_vec(p, nc, s6, cm, f"f1_eb2_{ex}", io["eb2"][ex, :],
                                      D, yps, war=[last_yps[0], last_yps[1], t_eb2_reader],
                                      tile=eb2_bc)
            t_we_last = t_we[-1]
            for tbk in range(8):
                txl = []
                for ic in range(ND):
                    txl.append(p.dma("sync", xlt6[:, ic * 512:(ic + 1) * 512],
                                     io["nt"][ic, :, tbk * 512:(tbk + 1) * 512],
                                     after=[last_xlt6]))
                th = None
                for j in range(32):
                    psl = j % 2
                    mmt = None
                    for ic in range(ND):
                        mmt = p.op("tensor", (lambda e, o=hps[psl][:],
                                   l=we1[:, ic * 4096 + j * 128: ic * 4096 + j * 128 + 128],
                                   r=xlt6[:, ic * 512:(ic + 1) * 512],
                                   s=(ic == 0), st=(ic == ND - 1):
                                   e.matmul(o, l, r, start=s, stop=st)),
                                   after=[txl[-1], last_hps[psl], t_we_last])
                    th = p.op("scalar", (lambda e, o=hT[:, j * 512:(j + 1) * 512],
                              i=hps[psl][:], b=eb1c[:, j:j + 1]:
                              e.activation(o, i, AF.Relu, bias=b)),
                              after=[mmt, t_eb1[j]])
                    last_hps[psl] = th
                last_xlt6 = th
                for tsub in range(4):
                    ti = tbk * 4 + tsub
                    r0 = ti * 128
                    mmt = [None, None]
                    for oc in range(2):
                        for j in range(32):
                            mmt[oc] = p.op("tensor", (lambda e, o=yps[oc][:],
                                           l=hT[:, j * 512 + tsub * 128: j * 512 + tsub * 128 + 128],
                                           r=we2[:, j * D + oc * 512: j * D + (oc + 1) * 512],
                                           s=(j == 0), st=(j == 31):
                                           e.matmul(o, l, r, start=s, stop=st)),
                                           after=[th, last_yps[oc], t_eb2])
                    tev = []
                    for oc in range(2):
                        t1 = p.op("vector", (lambda e, o=yout[:, oc * 512:(oc + 1) * 512],
                                  i=yps[oc][:], b=eb2_bc[:, oc * 512:(oc + 1) * 512],
                                  m=combs[:, ti * 8 + ex: ti * 8 + ex + 1]:
                                  e.scalar_tensor_tensor(out=o, in0=i, scalar=1.0,
                                      in1=b, op0=ALU.mult, op1=ALU.add)),
                                  after=[mmt[oc], t_eb2, last_yout])
                        t2 = p.op("vector", (lambda e, o=yout[:, oc * 512:(oc + 1) * 512],
                                  i=yout[:, oc * 512:(oc + 1) * 512],
                                  m=combs[:, ti * 8 + ex: ti * 8 + ex + 1]:
                                  e.tensor_scalar_mul(o, i, m)), after=[t1])
                        last_yps[oc] = t2
                        tev.append(t2)
                        t_eb2_reader = t1
                    last_yout = p.dma("gpsimd", io["acc"][r0:r0 + 128, :], yout[:],
                                      after=tev, accum_op=ALU.add)
            last_we = last_yout
        p.barrier()

    with ExitStack() as sF:
        xa = sbuf(nc, sF, "f1_xa", [128, D])
        xb = sbuf(nc, sF, "f1_xb", [128, D])
        xc = sbuf(nc, sF, "f1_xc", [128, D])
        last7 = [None, None]
        lastst = None
        for ti in range(NT):
            r0 = ti * 128
            tu = p.dma("sync", xa[:], io["acc"][r0:r0 + 128, :], after=[last7[0]])
            tx = p.dma("sync", xb[:], io["x"][r0:r0 + 128, :], after=[last7[1]])
            tm, tr = ln_stats(p, xa[:], lns, after=[tu])
            ta = ln_apply(p, xc[:], xa[:], lns, [tm, tr, lastst])
            last7[0] = ta
            tx4 = p.op("vector", (lambda e, o=xc[:], a=xc[:], b=xb[:]:
                       e.tensor_add(o, a, b)), after=[ta, tx])
            last7[1] = tx4
            lastst = p.dma("sync", io["x4"][r0:r0 + 128, :], xc[:], after=[tx4])
        p.barrier()


def build_f1(debug=False):
    nc = bass.Bass("TRN2", target_bir_lowering=False)
    io = {
        "x": _din(nc, "x", [T, D]),
        "norm_weight": _din(nc, "norm_weight", [D]),
        "router_w": _din(nc, "router_w", [8, D]),
        "sw1": _din(nc, "sw1", [D, 4096]),
        "sb1": _din(nc, "sb1", [4096]),
        "sw2": _din(nc, "sw2", [4096, D]),
        "sb2": _din(nc, "sb2", [D]),
        "ew1": _din(nc, "ew1", [8, D, 4096]),
        "eb1": _din(nc, "eb1", [8, 4096]),
        "ew2": _din(nc, "ew2", [8, 4096, D]),
        "eb2": _din(nc, "eb2", [8, D]),
        "iden": _din(nc, "iden", [128, 128], BF16),
        "x4": _dout(nc, "x4", [T, D]),
        "nt": _dtmp(nc, "f1nt", [ND, 128, T], BF16),
        "acc": _dtmp(nc, "f1acc", [T, D]),
    }
    with ExitStack() as ctx:
        p = Prog(nc, ctx)
        cm = Common(p, nc, ctx, io)
        if debug:
            io["dbg_accmid"] = _dout(nc, "dbg_accmid", [T, D])
        emit_f1(p, nc, ctx, cm, io, debug=debug)
        p.barrier()
        if debug:
            for nm, shp, dt in (("nt", [ND, 128, T], BF16), ("acc", [T, D], F32)):
                dbg = _dout(nc, "dbg_" + nm, shp, dt)
                p.dma("sync", dbg, io[nm])
            dbgc = _dout(nc, "dbg_combs", [128, NT * 8])
            p.dma("sync", dbgc, io["_combs"][:])
            p.barrier()
        p.replay()
    return nc


def _f1_inmaps(inputs, x3_shards, cores):
    m = []
    for j, c in enumerate(cores):
        m.append({
            "x": np.ascontiguousarray(x3_shards[j]),
            "norm_weight": inputs["L1_ffn_norm_weight"],
            "router_w": inputs["L1_ffn_router_w"],
            "sw1": inputs["L1_ffn_sw1"], "sb1": inputs["L1_ffn_sb1"],
            "sw2": inputs["L1_ffn_sw2"], "sb2": inputs["L1_ffn_sb2"],
            "ew1": inputs["L1_ffn_ew1"], "eb1": inputs["L1_ffn_eb1"],
            "ew2": inputs["L1_ffn_ew2"], "eb2": inputs["L1_ffn_eb2"],
            "iden": IDEN,
        })
    return m


def build_fused():
    nc = bass.Bass("TRN2", target_bir_lowering=False)
    io = {
        "x": _din(nc, "x", [T, D]),
        "iden": _din(nc, "iden", [128, 128], BF16),
        "x4": _dout(nc, "x4", [T, D]),
        "in_proj_w": _din(nc, "in_proj_w", [3 * D, D]),
        "in_proj_b": _din(nc, "in_proj_b", [3 * D]),
        "out_w": _din(nc, "out_w", [D, D]),
        "out_b": _din(nc, "out_b", [D]),
        "lin1_w": _din(nc, "lin1_w", [4096, D]),
        "lin1_b": _din(nc, "lin1_b", [4096]),
        "lin2_w": _din(nc, "lin2_w", [D, 4096]),
        "lin2_b": _din(nc, "lin2_b", [D]),
        "a0_ln1_g": _din(nc, "a0_ln1_g", [D]),
        "a0_ln1_b": _din(nc, "a0_ln1_b", [D]),
        "a0_ln2_g": _din(nc, "a0_ln2_g", [D]),
        "a0_ln2_b": _din(nc, "a0_ln2_b", [D]),
        "w_qkv": _din(nc, "w_qkv", [D, 3 * D]),
        "w_proj": _din(nc, "w_proj", [D, D]),
        "f0_ln1_w": _din(nc, "f0_ln1_w", [D]),
        "f0_ln1_b": _din(nc, "f0_ln1_b", [D]),
        "f0_ln2_w": _din(nc, "f0_ln2_w", [D]),
        "f0_ln2_b": _din(nc, "f0_ln2_b", [D]),
        "gate_w": _din(nc, "gate_w", [D, 8]),
        "gate_b": _din(nc, "gate_b", [8]),
        "w1": _din(nc, "w1", [8, D, 4096]),
        "w2": _din(nc, "w2", [8, 4096, D]),
        "Wq": _din(nc, "Wq", [D, D]), "bq": _din(nc, "bq", [D]),
        "Wk": _din(nc, "Wk", [D, D]), "bk": _din(nc, "bk", [D]),
        "Wv": _din(nc, "Wv", [D, 2 * D]), "bv": _din(nc, "bv", [2 * D]),
        "Wg": _din(nc, "Wg", [D, 2 * D]), "bg": _din(nc, "bg", [2 * D]),
        "Wo": _din(nc, "Wo", [2 * D, D]), "bo": _din(nc, "bo", [D]),
        "norm_weight": _din(nc, "norm_weight", [D]),
        "router_w": _din(nc, "router_w", [8, D]),
        "sw1": _din(nc, "sw1", [D, 4096]), "sb1": _din(nc, "sb1", [4096]),
        "sw2": _din(nc, "sw2", [4096, D]), "sb2": _din(nc, "sb2", [D]),
        "ew1": _din(nc, "ew1", [8, D, 4096]), "eb1": _din(nc, "eb1", [8, 4096]),
        "ew2": _din(nc, "ew2", [8, 4096, D]), "eb2": _din(nc, "eb2", [8, D]),
        "x1": _dtmp(nc, "x1t", [T, D]),
        "x2": _dtmp(nc, "x2t", [T, D]),
        "x3": _dtmp(nc, "x3t", [T, D]),
        "a0src": _dtmp(nc, "a0src", [T, D]),
        "a0postt": _dtmp(nc, "a0postt", [D, T], BF16),
        "a0xln1": _dtmp(nc, "a0xln1", [T, D]),
        "a0xln1t": _dtmp(nc, "a0xln1t", [ND, 128, T], BF16),
        "a0u2": _dtmp(nc, "a0u2", [T, D]),
        "f0xs": _dtmp(nc, "f0xs", [T, D]),
        "f0postt": _dtmp(nc, "f0postt", [D, T], BF16),
        "f0u2m": _dtmp(nc, "f0u2m", [T, D]),
        "f0xln2t": _dtmp(nc, "f0xln2t", [ND, 128, T], BF16),
        "f0moe": _dtmp(nc, "f0moe", [T, D]),
        "a1xwt": _dtmp(nc, "a1xwt", [ND, 128, T], BF16),
        "a1vr": _dtmp(nc, "a1vr", [T, 16 * 129], BF16),
        "a1sg": _dtmp(nc, "a1sg", [T, 2 * D], BF16),
        "a1ot": _dtmp(nc, "a1ot", [16, 128, T], BF16),
        "f1nt": _dtmp(nc, "f1nt", [ND, 128, T], BF16),
        "f1acc": _dtmp(nc, "f1acc", [T, D]),
    }
    with ExitStack() as ctx:
        p = Prog(nc, ctx)
        cm = Common(p, nc, ctx, io)
        io_a0 = {k: io[v] for k, v in dict(
            x="x", iden="iden", in_proj_w="in_proj_w", in_proj_b="in_proj_b",
            out_w="out_w", out_b="out_b", lin1_w="lin1_w", lin1_b="lin1_b",
            lin2_w="lin2_w", lin2_b="lin2_b", ln1_g="a0_ln1_g", ln1_b="a0_ln1_b",
            ln2_g="a0_ln2_g", ln2_b="a0_ln2_b", x1="x1", src="a0src",
            postt="a0postt", xln1="a0xln1", xln1t="a0xln1t", u2="a0u2").items()}
        with ExitStack() as pc:
            emit_a0(p, nc, pc, cm, io_a0)
        p.barrier()
        io_f0 = {k: io[v] for k, v in dict(
            x="x1", iden="iden", w_qkv="w_qkv", w_proj="w_proj",
            ln1_w="f0_ln1_w", ln1_b="f0_ln1_b", ln2_w="f0_ln2_w", ln2_b="f0_ln2_b",
            gate_w="gate_w", gate_b="gate_b", w1="w1", w2="w2", x2="x2",
            xs="f0xs", postt="f0postt", u2m="f0u2m", xln2t="f0xln2t",
            moe="f0moe").items()}
        with ExitStack() as pc:
            emit_f0(p, nc, pc, cm, io_f0)
        p.barrier()
        io_a1 = {k: io[v] for k, v in dict(
            x="x2", iden="iden", Wq="Wq", bq="bq", Wk="Wk", bk="bk", Wv="Wv",
            bv="bv", Wg="Wg", bg="bg", Wo="Wo", bo="bo", x3="x3",
            xwt="a1xwt", vr="a1vr", sg="a1sg", ot="a1ot").items()}
        with ExitStack() as pc:
            emit_a1(p, nc, pc, cm, io_a1)
        p.barrier()
        io_f1 = {k: io[v] for k, v in dict(
            x="x3", iden="iden", norm_weight="norm_weight", router_w="router_w",
            sw1="sw1", sb1="sb1", sw2="sw2", sb2="sb2", ew1="ew1", eb1="eb1",
            ew2="ew2", eb2="eb2", x4="x4", nt="f1nt", acc="f1acc").items()}
        with ExitStack() as pc:
            emit_f1(p, nc, pc, cm, io_f1)
        p.barrier()
        p.replay()
    return nc


def _fused_inmaps(inputs, cores):
    m = []
    for c in cores:
        m.append({
            "x": np.ascontiguousarray(inputs["L0_attn_src"][c]),
            "iden": IDEN,
            "in_proj_w": inputs["L0_attn_in_proj_w"],
            "in_proj_b": inputs["L0_attn_in_proj_b"],
            "out_w": inputs["L0_attn_out_w"], "out_b": inputs["L0_attn_out_b"],
            "lin1_w": inputs["L0_attn_lin1_w"], "lin1_b": inputs["L0_attn_lin1_b"],
            "lin2_w": inputs["L0_attn_lin2_w"], "lin2_b": inputs["L0_attn_lin2_b"],
            "a0_ln1_g": inputs["L0_attn_ln1_g"], "a0_ln1_b": inputs["L0_attn_ln1_b"],
            "a0_ln2_g": inputs["L0_attn_ln2_g"], "a0_ln2_b": inputs["L0_attn_ln2_b"],
            "w_qkv": inputs["L0_ffn_w_qkv"], "w_proj": inputs["L0_ffn_w_proj"],
            "f0_ln1_w": inputs["L0_ffn_ln1_w"], "f0_ln1_b": inputs["L0_ffn_ln1_b"],
            "f0_ln2_w": inputs["L0_ffn_ln2_w"], "f0_ln2_b": inputs["L0_ffn_ln2_b"],
            "gate_w": inputs["L0_ffn_gate_w"], "gate_b": inputs["L0_ffn_gate_b"],
            "w1": inputs["L0_ffn_w1"], "w2": inputs["L0_ffn_w2"],
            "Wq": inputs["L1_attn_Wq"], "bq": inputs["L1_attn_bq"],
            "Wk": inputs["L1_attn_Wk"], "bk": inputs["L1_attn_bk"],
            "Wv": inputs["L1_attn_Wv"], "bv": inputs["L1_attn_bv"],
            "Wg": inputs["L1_attn_Wg"], "bg": inputs["L1_attn_bg"],
            "Wo": inputs["L1_attn_Wo"], "bo": inputs["L1_attn_bo"],
            "norm_weight": inputs["L1_ffn_norm_weight"],
            "router_w": inputs["L1_ffn_router_w"],
            "sw1": inputs["L1_ffn_sw1"], "sb1": inputs["L1_ffn_sb1"],
            "sw2": inputs["L1_ffn_sw2"], "sb2": inputs["L1_ffn_sb2"],
            "ew1": inputs["L1_ffn_ew1"], "eb1": inputs["L1_ffn_eb1"],
            "ew2": inputs["L1_ffn_ew2"], "eb2": inputs["L1_ffn_eb2"],
        })
    return m


def kernel_fused(**inputs):
    cores = list(range(8))
    nc = build_fused()
    res = run_bass_kernel_spmd(nc, _fused_inmaps(inputs, cores), core_ids=cores)
    return np.stack([res.results[i]["x4"] for i in range(len(cores))])


# version 07
# speedup vs baseline: 1.1036x; 1.1036x over previous
import numpy as np
import ml_dtypes
from contextlib import ExitStack

import concourse.bass as bass
import concourse.mybir as mybir
from concourse.bass_utils import run_bass_kernel_spmd

F32 = mybir.dt.float32
BF16 = mybir.dt.bfloat16
I32 = mybir.dt.int32
U32 = mybir.dt.uint32
AF = mybir.ActivationFunctionType
ALU = mybir.AluOpType
AX = mybir.AxisListType

T = 4096
D = 1024
NT = T // 128
ND = D // 128
LN_EPS = 1e-5

ENGS = ("sync", "gpsimd", "tensor", "vector", "scalar")


class Prog:

    def __init__(self, nc, ctx):
        self.nc = nc
        self.ctx = ctx
        self.items = {e: [] for e in ENGS}
        self.sems = {}
        self.counts = {}
        self.waited = {e: {} for e in ENGS}
        for e in ENGS:
            self._sem("s_" + e)
        self.dma_idx = {}
        for e in ("sync", "gpsimd", "scalar", "vector"):
            for i in range(8):
                self._sem(f"d_{e}{i}")
            self.dma_idx[e] = 0

    def _sem(self, name):
        self.sems[name] = self.ctx.enter_context(self.nc.semaphore(name))
        self.counts[name] = 0

    def wait(self, eng, semname, val):
        if val <= 0:
            return
        w = self.waited[eng]
        if w.get(semname, 0) >= val:
            return
        w[semname] = val
        self.items[eng].append(("w", semname, val))

    def after(self, eng, tags):
        for t in tags:
            if t is None:
                continue
            self.wait(eng, t[0], t[1])

    def op(self, eng, fn, after=()):
        self.after(eng, after)
        sname = "s_" + eng
        self.counts[sname] += 1
        self.items[eng].append(("o", fn, [(sname, 1)]))
        return (sname, self.counts[sname])

    def dma(self, eng, out, in_, after=(), **kw):
        self.after(eng, after)
        idx = self.dma_idx[eng]
        self.dma_idx[eng] = (idx + 1) % 8
        sname = f"d_{eng}{idx}"
        self.counts[sname] += 16
        fn = (lambda e, o=out, i=in_, kw=kw: e.dma_start(out=o, in_=i, **kw))
        self.items[eng].append(("o", fn, [(sname, 16)]))
        return (sname, self.counts[sname])

    def barrier(self):
        snap = [(s, v) for s, v in self.counts.items() if v > 0]
        for e in ENGS:
            for s, v in snap:
                self.wait(e, s, v)

    def replay(self):
        nc = self.nc
        nc.all_engine_barrier()
        ksr = nc._kernel_sem_range
        start = ksr.start + (3 if nc._bir_kernel_barrier_sem is not None else 2)
        start += len(nc._monotonic_sems)
        rng = range(start, ksr.stop)
        nc.gpsimd.dma_reset(rng)
        nc.gpsimd.sem_clear(rng)
        nc.all_engine_barrier()
        with nc.Block() as block:
            def mk(e):
                def emit(eng):
                    for it in self.items[e]:
                        if it[0] == "w":
                            eng.wait_ge(self.sems[it[1]], it[2])
                        else:
                            inst = it[1](eng)
                            for s, n in it[2]:
                                inst.then_inc(self.sems[s], n)
                return emit
            block.sync(mk("sync"))
            block.gpsimd(mk("gpsimd"))
            block.tensor(mk("tensor"))
            block.vector(mk("vector"))
            block.scalar(mk("scalar"))



def sbuf(nc, ctx, name, shape, dtype=F32):
    return ctx.enter_context(nc.sbuf_tensor(name, shape, dtype))


def psum(nc, ctx, name, shape, dtype=F32):
    return ctx.enter_context(nc.psum_tensor(name, shape, dtype))


def load_col(p, dst_col, vec_hbm, off, n):
    return p.dma("scalar", dst_col,
                 vec_hbm[off:off + n].rearrange("(p one) -> p one", one=1))


class Common:

    def __init__(self, p, nc, ctx, io):
        self.iden = sbuf(nc, ctx, "c_iden", [128, 128], BF16)
        self.t_iden = p.dma("sync", self.iden[:], io["iden"])
        self.ones = sbuf(nc, ctx, "c_ones", [1, 128], F32)
        self.t_ones = p.op("vector", (lambda e, o=self.ones[:]: e.memset(o, 1.0)))
        self.row = sbuf(nc, ctx, "c_row", [1, 1024], F32)
        self.row_last = None


def bcast_vec(p, nc, ctx, cm, name, vec_hbm, n, ps_pool, war=(), tile=None):
    row = cm.row
    if tile is None:
        tile = sbuf(nc, ctx, name + "_bc", [128, n])
    t_row = p.dma("scalar", row[0:1, 0:n],
                  vec_hbm[0:n].rearrange("(one n) -> one n", one=1),
                  after=[cm.row_last])
    last = [None] * len(ps_pool)
    tc = None
    for c0 in range(0, n, 512):
        c1 = min(c0 + 512, n)
        idx = (c0 // 512) % len(ps_pool)
        ps = ps_pool[idx]
        tm = p.op("tensor", (lambda e, o=ps[:, 0:c1 - c0], l=cm.ones[:],
                             r=row[0:1, c0:c1]: e.matmul(o, l, r, start=True, stop=True)),
                  after=[t_row, cm.t_ones, last[idx]] + list(war))
        tc = p.op("vector", (lambda e, o=tile[:, c0:c1], i=ps[:, 0:c1 - c0]:
                             e.tensor_copy(o, i)), after=[tm])
        last[idx] = tc
    cm.row_last = tc
    return tile, tc


class LNS:

    def __init__(self, nc, ctx, name, width=D):
        self.width = width
        self.ssum = sbuf(nc, ctx, name + "_ssum", [128, 1])
        self.sqacc = sbuf(nc, ctx, name + "_sqacc", [128, 1])
        self.mean = sbuf(nc, ctx, name + "_mean", [128, 1])
        self.var = sbuf(nc, ctx, name + "_var", [128, 1])
        self.m2 = sbuf(nc, ctx, name + "_m2", [128, 1])
        self.sd = sbuf(nc, ctx, name + "_sd", [128, 1])
        self.rstd = sbuf(nc, ctx, name + "_rstd", [128, 1])
        self.sq = sbuf(nc, ctx, name + "_sq", [128, width])


def ln_stats(p, x_ap, w, after=(), eps=LN_EPS, mean_zero=False):
    width = w.width
    t_sq = p.op("scalar", (lambda e, o=w.sq[:, 0:width], i=x_ap, a=w.sqacc[:]:
                           e.activation(o, i, AF.Square, accum_out=a)), after=after)
    t_v1 = p.op("vector", (lambda e, o=w.var[:], i=w.sqacc[:]:
                           e.tensor_scalar_mul(o, i, 1.0 / width)), after=[t_sq])
    if mean_zero:
        t_v3 = t_v1
        t_mean = None
    else:
        t_sum = p.op("vector", (lambda e, o=w.ssum[:], i=x_ap:
                                e.reduce_sum(o, i, axis=AX.X)), after=after)
        t_mean = p.op("vector", (lambda e, o=w.mean[:], i=w.ssum[:]:
                                 e.tensor_scalar_mul(o, i, 1.0 / width)), after=[t_sum])
        t_v2 = p.op("vector", (lambda e, o=w.m2[:], i=w.mean[:]:
                               e.tensor_tensor(o, i, i, ALU.mult)), after=[t_mean])
        t_v3 = p.op("vector", (lambda e, o=w.var[:], a=w.var[:], b=w.m2[:]:
                               e.tensor_sub(o, a, b)), after=[t_v1, t_v2])
    t_v4 = p.op("vector", (lambda e, o=w.var[:], i=w.var[:], c=float(eps):
                           e.tensor_scalar_add(o, i, c)), after=[t_v3])
    t_v5 = p.op("scalar", (lambda e, o=w.sd[:], i=w.var[:]:
                           e.activation(o, i, AF.Sqrt)), after=[t_v4])
    t_rstd = p.op("vector", (lambda e, o=w.rstd[:], i=w.sd[:]:
                             e.reciprocal(o, i)), after=[t_v5])
    return t_mean, t_rstd


def ln_apply(p, out_ap, x_ap, w, tags):
    return p.op("vector", (lambda e, o=out_ap, i=x_ap, m=w.mean[:], r=w.rstd[:]:
                           e.tensor_scalar(o, i, m, r, ALU.subtract, ALU.mult)),
                after=tags)


class WTrans:

    def __init__(self, p, nc, ctx, cm, name, n_ps=2):
        self.p, self.nc, self.cm = p, nc, cm
        self.raw = [sbuf(nc, ctx, f"{name}_raw{i}", [128, D], BF16) for i in range(2)]
        self.tr = [psum(nc, ctx, f"{name}_tr{i}", [128, 128], BF16) for i in range(n_ps)]
        self.n_ps = n_ps
        self.last_raw = [None, None]
        self.last_tr = [None] * n_ps
        self.k = 0

    def slab(self, w_hbm_rows, in_width, dest_cols):
        p, cm = self.p, self.cm
        sl = self.k % 2
        self.k += 1
        t_load = p.dma("gpsimd", self.raw[sl][:, 0:in_width], w_hbm_rows,
                       after=[self.last_raw[sl]])
        out_tags = []
        for ic in range(in_width // 128):
            psl = (self.k * 8 + ic) % self.n_ps
            t_tr = p.op("tensor", (lambda e, o=self.tr[psl][:],
                                   i=self.raw[sl][:, ic * 128:(ic + 1) * 128],
                                   idn=cm.iden[:]: e.transpose(o, i, idn)),
                        after=[t_load, cm.t_iden, self.last_tr[psl]])
            t_cp = p.op("vector", (lambda e, o=dest_cols[ic], i=self.tr[psl][:]:
                                   e.tensor_copy(o, i)), after=[t_tr])
            self.last_tr[psl] = t_cp
            out_tags.append(t_cp)
        self.last_raw[sl] = out_tags[-1]
        return out_tags


class TrPool:

    def __init__(self, p, nc, ctx, cm, name):
        self.p, self.cm = p, cm
        self.trs = [psum(nc, ctx, f"{name}_tr{i}", [128, 128], BF16) for i in range(2)]
        self.last = [None, None]
        self.k = 0

    def tr(self, in_bf_ap, dest_ap, after=(), dest_eng="vector"):
        p, cm = self.p, self.cm
        sl = self.k % 2
        self.k += 1
        t_tr = p.op("tensor", (lambda e, o=self.trs[sl][:], i=in_bf_ap,
                               idn=cm.iden[:]: e.transpose(o, i, idn)),
                    after=list(after) + [cm.t_iden, self.last[sl]])
        t_cp = p.op(dest_eng, (lambda e, o=dest_ap, i=self.trs[sl][:]:
                               e.tensor_copy(o, i)), after=[t_tr])
        self.last[sl] = t_cp
        return t_cp


class WT2:

    def __init__(self, p, nc, ctx, trp, name, scale=1.0, raw_width=4096):
        self.p, self.trp = p, trp
        self.raw = [sbuf(nc, ctx, f"{name}_raw{i}", [128, raw_width], BF16) for i in range(2)]
        self.last_raw = [None, None]
        self.k = 0
        self.scale = scale

    def slab(self, w_hbm_rows, in_width, dest_cols):
        p = self.p
        sl = self.k % 2
        self.k += 1
        t_load = p.dma("gpsimd", self.raw[sl][:, 0:in_width], w_hbm_rows,
                       after=[self.last_raw[sl]])
        tags = []
        for ic in range(in_width // 128):
            if self.scale == 1.0:
                t = self.trp.tr(self.raw[sl][:, ic * 128:(ic + 1) * 128],
                                dest_cols[ic], after=[t_load])
            else:
                trp, sc = self.trp, self.scale
                psl = trp.k % 2
                trp.k += 1
                t_tr = p.op("tensor", (lambda e, o=trp.trs[psl][:],
                                       i=self.raw[sl][:, ic * 128:(ic + 1) * 128],
                                       idn=trp.cm.iden[:]: e.transpose(o, i, idn)),
                            after=[t_load, trp.cm.t_iden, trp.last[psl]])
                t = p.op("scalar", (lambda e, o=dest_cols[ic], i=trp.trs[psl][:],
                                    s=sc: e.mul(o, i, s)), after=[t_tr])
                trp.last[psl] = t
            tags.append(t)
        self.last_raw[sl] = tags[-1]
        return tags


def emit_a0(p, nc, ctx, cm, io):
    H, DH = 16, 64
    trp = TrPool(p, nc, ctx, cm, "a0trp")

    with ExitStack() as s_pro:
        bqk = sbuf(nc, ctx, "a0_bqk", [128, 16])
        for oc in range(16):
            load_col(p, bqk[:, oc:oc + 1], io["in_proj_b"], oc * 128, 128)
        p.barrier()
        p.op("vector", (lambda e, o=bqk[:, 8:16], i=bqk[:, 8:16]:
                        e.tensor_scalar_mul(o, i, 0.125)))
        p.barrier()

    lns = LNS(nc, ctx, "a0_lns")

    with ExitStack() as sB:
        srcT = sbuf(nc, sB, "a0_srcT", [128, ND * T], BF16)
        s1 = sB.enter_context(ExitStack())
        x_t = [sbuf(nc, s1, f"a0_x{i}", [128, D]) for i in range(2)]
        src_t = [sbuf(nc, s1, f"a0_src{i}", [128, D]) for i in range(2)]
        srcbf = [sbuf(nc, s1, f"a0_srcbf{i}", [128, D], BF16) for i in range(2)]
        lastu = {}
        for ti in range(NT):
            sl = ti % 2
            r0 = ti * 128
            tx = p.dma("sync", x_t[sl][:], io["x"][r0:r0 + 128, :],
                       after=[lastu.get(("x", sl))])
            tm, tr = ln_stats(p, x_t[sl][:], lns, after=[tx])
            ts = ln_apply(p, src_t[sl][:], x_t[sl][:], lns,
                          [tm, tr, lastu.get(("srcdma", sl)), lastu.get(("srcbf", sl))])
            lastu[("x", sl)] = ts
            lastu[("srcdma", sl)] = p.dma("sync", io["src"][r0:r0 + 128, :],
                                          src_t[sl][:], after=[ts])
            tbf = p.op("scalar", (lambda e, o=srcbf[sl][:], i=src_t[sl][:]:
                                  e.copy(o, i)), after=[ts, lastu.get(("sbf", sl))])
            lastu[("srcbf", sl)] = tbf
            tlast = None
            for ic in range(ND):
                tlast = trp.tr(srcbf[sl][:, ic * 128:(ic + 1) * 128],
                               srcT[:, ic * T + r0: ic * T + r0 + 128], after=[tbf])
            lastu[("sbf", sl)] = tlast
        p.barrier()
        s1.close()

        qT = sbuf(nc, sB, "a0_qT", [128, 2 * T], BF16)
        kT = sbuf(nc, sB, "a0_kT", [128, 2 * T], BF16)
        vg = sbuf(nc, sB, "a0_vg", [128, NT * 256], BF16)
        pt = [sbuf(nc, sB, f"a0_pt{i}", [128, T], BF16) for i in range(2)]
        gst = sbuf(nc, sB, "a0_gst", [128, T], BF16)
        lparts = sbuf(nc, sB, "a0_lparts", [128, 4])
        lsum = sbuf(nc, sB, "a0_lsum", [128, 1])
        linv = sbuf(nc, sB, "a0_linv", [128, 1])
        vp = [sbuf(nc, sB, f"a0_vp{i}", [128, 64], BF16) for i in range(2)]
        acc = sbuf(nc, sB, "a0_acc", [128, T])
        wblk = [sbuf(nc, sB, f"a0_wblk{i}", [128, D], BF16) for i in range(2)]
        zA = psum(nc, sB, "a0_zA", [128, 1024])
        zB = psum(nc, sB, "a0_zB", [128, 1024])
        avps = psum(nc, sB, "a0_avps", [128, 1024])
        wt = WT2(p, nc, sB, trp, "a0_wt", raw_width=D)
        vb_bc, t_vb = bcast_vec(p, nc, sB, cm, "a0_vb",
                                io["in_proj_b"][2 * D:3 * D], D, [zA, zB])

        mm_slot = [zA[:, 0:512], zB[:, 0:512]]
        last_slot = [t_vb, t_vb]
        last_wblk = [None, None]
        t_postdma = {}
        lp_last = None
        prev_ac_last = None
        gelu_done = None
        gst_free = None
        za_free = zb_free = None
        pt_free = [None, None]
        vp_free = [None, None]
        av_war = [None, None]

        for g in range(4):
            for part in ("q", "k", "v"):
                base_oc = {"q": 0, "k": 8, "v": 16}[part]
                for oc_loc in range(2):
                    oc = base_oc + 2 * g + oc_loc
                    wsl = oc % 2
                    dcols = [wblk[wsl][:, ic * 128:(ic + 1) * 128] for ic in range(ND)]
                    if last_wblk[wsl] is not None:
                        wt.last_raw[wt.k % 2] = last_wblk[wsl]
                    ttags = wt.slab(io["in_proj_w"][oc * 128:(oc + 1) * 128, :], D, dcols)
                    tw = ttags[-1]
                    if part in ("q", "k"):
                        dest = qT if part == "q" else kT
                        rb = oc_loc
                        mmt = None
                        for tc in range(8):
                            psl = tc % 2
                            for ic in range(ND):
                                mmt = p.op("tensor", (lambda e, o=mm_slot[psl],
                                           l=wblk[wsl][:, ic * 128:(ic + 1) * 128],
                                           r=srcT[:, ic * T + tc * 512: ic * T + (tc + 1) * 512],
                                           s=(ic == 0), st=(ic == ND - 1):
                                           e.matmul(o, l, r, start=s, stop=st)),
                                           after=[tw, last_slot[psl]])
                            tev = p.op("scalar", (lambda e,
                                       o=dest[:, rb * T + tc * 512: rb * T + (tc + 1) * 512],
                                       i=mm_slot[psl], b=bqk[:, oc:oc + 1],
                                       s=(0.125 if part == "k" else 1.0):
                                       e.activation(o, i, AF.Identity, bias=b, scale=s)),
                                       after=[mmt])
                            last_slot[psl] = tev
                            last_wblk[wsl] = tev
                    else:
                        vcol0 = oc_loc * 128
                        mmt = None
                        for ti in range(NT):
                            psl = ti % 2
                            for ic in range(ND):
                                mmt = p.op("tensor", (lambda e,
                                           o=mm_slot[psl][:, 0:128],
                                           l=srcT[:, ic * T + ti * 128: ic * T + ti * 128 + 128],
                                           r=wblk[wsl][:, ic * 128:(ic + 1) * 128],
                                           s=(ic == 0), st=(ic == ND - 1):
                                           e.matmul(o, l, r, start=s, stop=st)),
                                           after=[tw, last_slot[psl]])
                            tev = p.op("vector", (lambda e,
                                       o=vg[:, ti * 256 + vcol0: ti * 256 + vcol0 + 128],
                                       i=mm_slot[psl][:, 0:128],
                                       b=vb_bc[:, g * 256 + vcol0: g * 256 + vcol0 + 128]:
                                       e.scalar_tensor_tensor(out=o, in0=i, scalar=1.0,
                                           in1=b, op0=ALU.mult, op1=ALU.add)),
                                       after=[mmt, t_vb])
                            last_slot[psl] = tev
                            last_wblk[wsl] = tev
            for hq in range(4):
                h = 4 * g + hq
                rb = hq // 2
                pb = 64 * (hq % 2)
                zregs = [(zA, 0), (zA, 512), (zB, 0), (zB, 512),
                         (zA, 0), (zA, 512), (zB, 0), (zB, 512)]
                ac_tags = []
                pend = None
                st = {"prev": prev_ac_last}

                def flush_av(pd):
                    ti_, psl_, ev_, tvp_ = pd
                    new_ac = []
                    for tc in range(8):
                        asl = tc % 2
                        t_av = p.op("tensor", (lambda e,
                                    o=avps[0:64, asl * 512: asl * 512 + 512],
                                    l=vp[psl_][:], r=pt[psl_][:, tc * 512:(tc + 1) * 512]:
                                    e.matmul(o, l, r, start=True, stop=True)),
                                    after=[ev_, tvp_,
                                           new_ac[-2] if len(new_ac) >= 2 else av_war[asl]])
                        if ti_ == 0:
                            t_ac = p.op("vector", (lambda e,
                                        o=acc[pb:pb + 64, tc * 512:(tc + 1) * 512],
                                        i=avps[0:64, asl * 512: asl * 512 + 512]:
                                        e.tensor_copy(o, i)),
                                        after=[t_av, gelu_done, t_postdma.get(h // 2)])
                        else:
                            t_ac = p.op("vector", (lambda e,
                                        o=acc[pb:pb + 64, tc * 512:(tc + 1) * 512],
                                        i=avps[0:64, asl * 512: asl * 512 + 512]:
                                        e.scalar_tensor_tensor(out=o, in0=i, scalar=1.0,
                                            in1=o, op0=ALU.mult, op1=ALU.add)),
                                        after=[t_av, ac_tags[-8] if len(ac_tags) >= 8 else None])
                        new_ac.append(t_ac)
                        ac_tags.append(t_ac)
                    av_war[0] = new_ac[6]
                    av_war[1] = new_ac[7]
                    pt_free[psl_] = t_av
                    vp_free[psl_] = t_av
                    st["prev"] = new_ac[-1]

                for ti in range(NT):
                    s0 = ti * 128
                    psl = ti % 2
                    ztags = {}

                    def zmm(tc, war):
                        zt, zo = zregs[tc]
                        return p.op("tensor", (lambda e, o=zt[:, zo:zo + 512],
                                    l=kT[pb:pb + 64, rb * T + s0: rb * T + s0 + 128],
                                    r=qT[pb:pb + 64, rb * T + tc * 512: rb * T + (tc + 1) * 512]:
                                    e.matmul(o, l, r, start=True, stop=True)),
                                    after=war)
                    evs = []
                    for q4 in range(4):
                        zt, war0 = [(zA, za_free), (zB, zb_free)][q4 % 2]
                        if q4 >= 2:
                            war0 = evs[q4 - 2]
                        ztags[2 * q4] = zmm(2 * q4, [war0] + (
                            [last_slot[0], last_slot[1]] if q4 == 0 else []))
                        ztags[2 * q4 + 1] = zmm(2 * q4 + 1, [])
                        ev = p.op("scalar", (lambda e,
                                  o=pt[psl][:, q4 * 1024:(q4 + 1) * 1024], i=zt[:],
                                  a=lparts[:, q4:q4 + 1]:
                                  e.activation(o, i, AF.Exp, accum_out=a)),
                                  after=[ztags[2 * q4 + 1]] + (
                                      [pt_free[psl], lp_last] if q4 == 0 else []))
                        evs.append(ev)
                    za_free, zb_free = evs[2], evs[3]
                    if pend is not None:
                        flush_av(pend)
                    t_l = p.op("vector", (lambda e, o=lsum[:], i=lparts[:, 0:4]:
                               e.reduce_sum(o, i, axis=AX.X)), after=evs)
                    lp_last = t_l
                    t_li = p.op("vector", (lambda e, o=linv[:], i=lsum[:]:
                                e.reciprocal(o, i)), after=[t_l])
                    t_vp = p.op("vector", (lambda e, o=vp[psl][:],
                                i=vg[:, ti * 256 + hq * 64: ti * 256 + hq * 64 + 64],
                                s=linv[:]: e.tensor_scalar_mul(o, i, s)),
                                after=[t_li, vp_free[psl]])
                    pend = (ti, psl, evs[3], t_vp)
                flush_av(pend)
                prev_ac_last = st["prev"]
                if hq % 2 == 1:
                    gelu_done = p.op("scalar", (lambda e, o=gst[:], i=acc[:]:
                                     e.activation(o, i, AF.Gelu)),
                                     after=[prev_ac_last, gst_free])
                    pr = h // 2
                    t_postdma[pr] = p.dma("sync", io["postt"][pr * 128:(pr + 1) * 128, :],
                                          gst[:], after=[gelu_done])
                    gst_free = t_postdma[pr]
        p.barrier()

    with ExitStack() as sC:
        outwT = sbuf(nc, sC, "a0_outwT", [128, ND * D], BF16)
        wt4 = WT2(p, nc, sC, trp, "a0_wt4", scale=1.0 / 64.0, raw_width=D)
        for oco in range(ND):
            dcols = [outwT[:, ic * D + oco * 128: ic * D + oco * 128 + 128]
                     for ic in range(ND)]
            wt4.slab(io["out_w"][oco * 128:(oco + 1) * 128, :], D, dcols)
        pblk = sbuf(nc, sC, "a0_pblk", [128, ND * 512], BF16)
        stage = sbuf(nc, sC, "a0_stage", [128, ND * 512], BF16)
        src_t4 = sbuf(nc, sC, "a0_src4", [128, D])
        ua = sbuf(nc, sC, "a0_ua", [128, D])
        ub = sbuf(nc, sC, "a0_ub", [128, D])
        xl1 = sbuf(nc, sC, "a0_xl1", [128, D])
        xl1bf = sbuf(nc, sC, "a0_xl1bf", [128, D], BF16)
        at2 = [psum(nc, sC, f"a0_at2{i}", [128, 512]) for i in range(2)]
        outb_bc, t_outb = bcast_vec(p, nc, sC, cm, "a0_outb", io["out_b"], D, at2)
        g1_bc, t_g1 = bcast_vec(p, nc, sC, cm, "a0_g1", io["ln1_g"], D, at2)
        b1_bc, t_b1 = bcast_vec(p, nc, sC, cm, "a0_b1", io["ln1_b"], D, at2)
        p.barrier()
        last_ev4 = [None, None]
        last_stage = None
        last_pblk = None
        last_src4 = None
        last_xl1 = []
        for tb in range(8):
            tpb = []
            for ic in range(ND):
                tpb.append(p.dma("sync", pblk[:, ic * 512:(ic + 1) * 512],
                                 io["postt"][ic * 128:(ic + 1) * 128,
                                             tb * 512:(tb + 1) * 512],
                                 after=[last_pblk, t_postdma.get(ic)]))
            tstg = []
            for tsub in range(4):
                r0 = tb * 512 + tsub * 128
                tsrc = p.dma("sync", src_t4[:], io["src"][r0:r0 + 128, :],
                             after=[last_src4])
                mmt = [None, None]
                for oc in range(2):
                    for ic in range(ND):
                        mmt[oc] = p.op("tensor", (lambda e, o=at2[oc][:],
                                       l=pblk[:, ic * 512 + tsub * 128: ic * 512 + tsub * 128 + 128],
                                       r=outwT[:, ic * D + oc * 512: ic * D + (oc + 1) * 512],
                                       s=(ic == 0), st=(ic == ND - 1):
                                       e.matmul(o, l, r, start=s, stop=st)),
                                       after=[tpb[-1], last_ev4[oc]])
                tu = []
                for oc in range(2):
                    t1 = p.op("vector", (lambda e, o=ua[:, oc * 512:(oc + 1) * 512],
                              i=at2[oc][:], b=outb_bc[:, oc * 512:(oc + 1) * 512]:
                              e.scalar_tensor_tensor(out=o, in0=i, scalar=1.0, in1=b,
                                  op0=ALU.mult, op1=ALU.add)), after=[mmt[oc], t_outb])
                    t2 = p.op("vector", (lambda e, o=ub[:, oc * 512:(oc + 1) * 512],
                              a=ua[:, oc * 512:(oc + 1) * 512],
                              b=src_t4[:, oc * 512:(oc + 1) * 512]:
                              e.tensor_add(o, a, b)), after=[t1, tsrc])
                    last_ev4[oc] = t2
                    tu.append(t2)
                last_src4 = tu[-1]
                tm, tr = ln_stats(p, ub[:], lns, after=tu)
                ta = ln_apply(p, ua[:], ub[:], lns, [tm, tr])
                tg = p.op("vector", (lambda e, o=ub[:], a=ua[:], b=g1_bc[:]:
                          e.tensor_mul(o, a, b)), after=[ta, t_g1])
                tb2 = p.op("vector", (lambda e, o=xl1[:], a=ub[:], b=b1_bc[:]:
                           e.tensor_add(o, a, b)), after=[tg, t_b1] + last_xl1)
                t_xd = p.dma("sync", io["xln1"][r0:r0 + 128, :], xl1[:], after=[tb2])
                tbf = p.op("scalar", (lambda e, o=xl1bf[:], i=xl1[:]: e.copy(o, i)),
                           after=[tb2])
                last_xl1 = [t_xd, tbf]
                for ic in range(ND):
                    tstg.append(trp.tr(xl1bf[:, ic * 128:(ic + 1) * 128],
                                       stage[:, ic * 512 + tsub * 128: ic * 512 + tsub * 128 + 128],
                                       after=[tbf, last_stage]))
            xlt_dma = []
            for ic in range(ND):
                xlt_dma.append(p.dma("sync", io["xln1t"][ic, :, tb * 512:(tb + 1) * 512],
                                     stage[:, ic * 512:(ic + 1) * 512], after=[tstg[-1]]))
            last_stage = xlt_dma[-1]
            last_pblk = mmt[1]
        p.barrier()

    with ExitStack() as sD:
        s5 = sD.enter_context(ExitStack())
        lin1T = sbuf(nc, s5, "a0_lin1T", [128, ND * 4096], BF16)
        lin2T = sbuf(nc, s5, "a0_lin2T", [128, 32 * D], BF16)
        wt5 = WT2(p, nc, s5, trp, "a0_wt5", raw_width=2048)
        for j in range(32):
            dcols = [lin1T[:, ic * 4096 + j * 128: ic * 4096 + j * 128 + 128]
                     for ic in range(ND)]
            wt5.slab(io["lin1_w"][j * 128:(j + 1) * 128, :], D, dcols)
        for oro in range(ND):
            dcols = [lin2T[:, jc * D + oro * 128: jc * D + oro * 128 + 128]
                     for jc in range(32)]
            wt5.slab(io["lin2_w"][oro * 128:(oro + 1) * 128, 0:2048], 2048, dcols[:16])
            wt5.slab(io["lin2_w"][oro * 128:(oro + 1) * 128, 2048:4096], 2048, dcols[16:])
        b1c = sbuf(nc, s5, "a0_b1c", [128, 32])
        for j in range(32):
            load_col(p, b1c[:, j:j + 1], io["lin1_b"], j * 128, 128)
        hT = sbuf(nc, s5, "a0_hT", [128, 32 * 512], BF16)
        xlt_in = [sbuf(nc, s5, "a0_xlt0", [128, ND * 512], BF16)]
        xl1_t5 = sbuf(nc, s5, "a0_xl1t5", [128, D])
        ya = sbuf(nc, s5, "a0_ya", [128, D])
        u2 = sbuf(nc, s5, "a0_u2", [128, D])
        hps = [psum(nc, sD, f"a0_hps{i}", [128, 512]) for i in range(2)]
        yps = [psum(nc, sD, f"a0_yps{i}", [128, 512]) for i in range(2)]
        l2b_bc, t_l2b = bcast_vec(p, nc, s5, cm, "a0_l2b", io["lin2_b"], D, yps)
        p.barrier()
        last_hps = [None, None]
        last_yps = [None, None]
        last_xlt = [None, None]
        last_hT_reader = None
        last_xl5 = None
        last_u2d = None
        for tbk in range(8):
            xsl = 0
            txl = []
            for ic in range(ND):
                txl.append(p.dma("sync", xlt_in[xsl][:, ic * 512:(ic + 1) * 512],
                                 io["xln1t"][ic, :, tbk * 512:(tbk + 1) * 512],
                                 after=[last_xlt[xsl]]))
            th = {}
            for j in range(32):
                psl = j % 2
                mmt = None
                for ic in range(ND):
                    mmt = p.op("tensor", (lambda e, o=hps[psl][:],
                               l=lin1T[:, ic * 4096 + j * 128: ic * 4096 + j * 128 + 128],
                               r=xlt_in[xsl][:, ic * 512:(ic + 1) * 512],
                               s=(ic == 0), st=(ic == ND - 1):
                               e.matmul(o, l, r, start=s, stop=st)),
                               after=[txl[-1], last_hps[psl], last_hT_reader])
                th[j] = p.op("scalar", (lambda e, o=hT[:, j * 512:(j + 1) * 512],
                             i=hps[psl][:], b=b1c[:, j:j + 1]:
                             e.activation(o, i, AF.Gelu, bias=b)), after=[mmt])
                last_hps[psl] = th[j]
            last_xlt[xsl] = th[31]
            hT_readers = []
            for tsub in range(4):
                r0 = tbk * 512 + tsub * 128
                txl5 = p.dma("sync", xl1_t5[:], io["xln1"][r0:r0 + 128, :],
                             after=[last_xl5])
                mmt = [None, None]
                for oc in range(2):
                    for j in range(32):
                        mmt[oc] = p.op("tensor", (lambda e, o=yps[oc][:],
                                       l=hT[:, j * 512 + tsub * 128: j * 512 + tsub * 128 + 128],
                                       r=lin2T[:, j * D + oc * 512: j * D + (oc + 1) * 512],
                                       s=(j == 0), st=(j == 31):
                                       e.matmul(o, l, r, start=s, stop=st)),
                                       after=[th[31], last_yps[oc]])
                    hT_readers.append(mmt[oc])
                tev = []
                for oc in range(2):
                    t1 = p.op("vector", (lambda e, o=ya[:, oc * 512:(oc + 1) * 512],
                              i=yps[oc][:], b=l2b_bc[:, oc * 512:(oc + 1) * 512]:
                              e.scalar_tensor_tensor(out=o, in0=i, scalar=1.0, in1=b,
                                  op0=ALU.mult, op1=ALU.add)), after=[mmt[oc], t_l2b])
                    t2 = p.op("vector", (lambda e, o=u2[:, oc * 512:(oc + 1) * 512],
                              a=ya[:, oc * 512:(oc + 1) * 512],
                              b=xl1_t5[:, oc * 512:(oc + 1) * 512]:
                              e.tensor_add(o, a, b)), after=[t1, txl5, last_u2d])
                    last_yps[oc] = t2
                    tev.append(t2)
                last_xl5 = tev[-1]
                last_u2d = p.dma("sync", io["u2"][r0:r0 + 128, :], u2[:], after=tev)
            last_hT_reader = hT_readers[-1]
        p.barrier()
        s5.close()
        g2_bc, t_g2 = bcast_vec(p, nc, sD, cm, "a0_g2", io["ln2_g"], D, yps)
        b2_bc, t_b2 = bcast_vec(p, nc, sD, cm, "a0_b2", io["ln2_b"], D, yps)

        xa = sbuf(nc, sD, "a0_xa", [128, D])
        xb = sbuf(nc, sD, "a0_xb", [128, D])
        xc = sbuf(nc, sD, "a0_xc", [128, D])
        last6 = [None, None]
        last6st = None
        for ti in range(NT):
            r0 = ti * 128
            tu2 = p.dma("sync", xa[:], io["u2"][r0:r0 + 128, :],
                        after=[last6[0], last6st])
            tx = p.dma("sync", xb[:], io["x"][r0:r0 + 128, :], after=[last6[1]])
            tm, tr = ln_stats(p, xa[:], lns, after=[tu2])
            ta = ln_apply(p, xc[:], xa[:], lns, [tm, tr])
            tg = p.op("vector", (lambda e, o=xc[:], a=xc[:], b=g2_bc[:]:
                      e.tensor_mul(o, a, b)), after=[ta, t_g2])
            tb2 = p.op("vector", (lambda e, o=xa[:], a=xc[:], b=b2_bc[:]:
                       e.tensor_add(o, a, b)), after=[tg, t_b2, last6st])
            tm2, tr2 = ln_stats(p, xa[:], lns, after=[tb2])
            ta2 = ln_apply(p, xc[:], xa[:], lns, [tm2, tr2])
            last6[0] = ta2
            tx1 = p.op("vector", (lambda e, o=xa[:], a=xc[:], b=xb[:]:
                       e.tensor_add(o, a, b)), after=[ta2, tx])
            last6[1] = tx1
            last6st = p.dma("sync", io["x1"][r0:r0 + 128, :], xa[:], after=[tx1])
        p.barrier()



def _din(nc, name, shape, dtype=F32):
    return nc.dram_tensor(name, shape, dtype, kind="ExternalInput").ap()


def _dout(nc, name, shape, dtype=F32):
    return nc.dram_tensor(name, shape, dtype, kind="ExternalOutput").ap()


def _dtmp(nc, name, shape, dtype=F32):
    return nc.dram_tensor(name, shape, dtype, kind="Internal").ap()


def build_a0(debug=False):
    nc = bass.Bass("TRN2", target_bir_lowering=False)
    io = {
        "x": _din(nc, "x", [T, D]),
        "in_proj_w": _din(nc, "in_proj_w", [3 * D, D]),
        "in_proj_b": _din(nc, "in_proj_b", [3 * D]),
        "out_w": _din(nc, "out_w", [D, D]),
        "out_b": _din(nc, "out_b", [D]),
        "lin1_w": _din(nc, "lin1_w", [4096, D]),
        "lin1_b": _din(nc, "lin1_b", [4096]),
        "lin2_w": _din(nc, "lin2_w", [D, 4096]),
        "lin2_b": _din(nc, "lin2_b", [D]),
        "ln1_g": _din(nc, "ln1_g", [D]),
        "ln1_b": _din(nc, "ln1_b", [D]),
        "ln2_g": _din(nc, "ln2_g", [D]),
        "ln2_b": _din(nc, "ln2_b", [D]),
        "iden": _din(nc, "iden", [128, 128], BF16),
        "x1": _dout(nc, "x1", [T, D]),
        "src": _dtmp(nc, "a0src", [T, D]),
        "postt": _dtmp(nc, "a0postt", [D, T], BF16),
        "xln1": _dtmp(nc, "a0xln1", [T, D]),
        "xln1t": _dtmp(nc, "a0xln1t", [ND, 128, T], BF16),
        "u2": _dtmp(nc, "a0u2", [T, D]),
    }
    with ExitStack() as ctx:
        p = Prog(nc, ctx)
        cm = Common(p, nc, ctx, io)
        emit_a0(p, nc, ctx, cm, io)
        p.barrier()
        if debug:
            for nm, shp, dt in (("src", [T, D], F32), ("postt", [D, T], BF16),
                                ("xln1", [T, D], F32), ("u2", [T, D], F32),
                                ("xln1t", [ND, 128, T], BF16)):
                dbg = _dout(nc, "dbg_" + nm, shp, dt)
                p.dma("sync", dbg, io[nm])
            p.barrier()
        p.replay()
    return nc


IDEN = np.eye(128, dtype=np.float32).astype(ml_dtypes.bfloat16)


def _a0_inmaps(inputs, cores):
    m = []
    for c in cores:
        m.append({
            "x": np.ascontiguousarray(inputs["L0_attn_src"][c]),
            "in_proj_w": inputs["L0_attn_in_proj_w"],
            "in_proj_b": inputs["L0_attn_in_proj_b"],
            "out_w": inputs["L0_attn_out_w"],
            "out_b": inputs["L0_attn_out_b"],
            "lin1_w": inputs["L0_attn_lin1_w"],
            "lin1_b": inputs["L0_attn_lin1_b"],
            "lin2_w": inputs["L0_attn_lin2_w"],
            "lin2_b": inputs["L0_attn_lin2_b"],
            "ln1_g": inputs["L0_attn_ln1_g"],
            "ln1_b": inputs["L0_attn_ln1_b"],
            "ln2_g": inputs["L0_attn_ln2_g"],
            "ln2_b": inputs["L0_attn_ln2_b"],
            "iden": IDEN,
        })
    return m


def kernel(**inputs):
    return kernel_fused(**inputs)


def emit_f0(p, nc, ctx, cm, io):
    H, DH = 16, 64
    trp = TrPool(p, nc, ctx, cm, "f0trp")
    lns = LNS(nc, ctx, "f0_lns")

    with ExitStack() as sB:
        u1T = sbuf(nc, sB, "f0_u1T", [128, ND * T], BF16)
        zA = psum(nc, sB, "f0_zA", [128, 1024])
        zB = psum(nc, sB, "f0_zB", [128, 1024])
        avp = psum(nc, sB, "f0_avp", [128, 512])
        lbp = psum(nc, sB, "f0_lbp", [128, 512])
        g1_bc, t_g1 = bcast_vec(p, nc, sB, cm, "f0_g1", io["ln1_w"], D, [zA, zB])
        b1_bc, t_b1 = bcast_vec(p, nc, sB, cm, "f0_b1", io["ln1_b"], D, [zA, zB])
        s1 = sB.enter_context(ExitStack())
        x_t = [sbuf(nc, s1, f"f0_x{i}", [128, D]) for i in range(2)]
        xs_t = [sbuf(nc, s1, f"f0_xs{i}", [128, D]) for i in range(2)]
        u1_t = [sbuf(nc, s1, f"f0_u1{i}", [128, D]) for i in range(2)]
        u1bf = [sbuf(nc, s1, f"f0_u1bf{i}", [128, D], BF16) for i in range(2)]
        lastu = {}
        for ti in range(NT):
            sl = ti % 2
            r0 = ti * 128
            tx = p.dma("sync", x_t[sl][:], io["x"][r0:r0 + 128, :],
                       after=[lastu.get(("x", sl))])
            tm, tr = ln_stats(p, x_t[sl][:], lns, after=[tx])
            ts = ln_apply(p, xs_t[sl][:], x_t[sl][:], lns,
                          [tm, tr, lastu.get(("xsd", sl)), lastu.get(("bf", sl))])
            lastu[("x", sl)] = ts
            lastu[("xsd", sl)] = p.dma("sync", io["xs"][r0:r0 + 128, :],
                                       xs_t[sl][:], after=[ts])
            tm2, tr2 = ln_stats(p, xs_t[sl][:], lns, after=[ts])
            ta = ln_apply(p, u1_t[sl][:], xs_t[sl][:], lns, [tm2, tr2])
            tg = p.op("vector", (lambda e, o=u1_t[sl][:], a=u1_t[sl][:], b=g1_bc[:]:
                      e.tensor_mul(o, a, b)), after=[ta, t_g1])
            tb = p.op("vector", (lambda e, o=u1_t[sl][:], a=u1_t[sl][:], b=b1_bc[:]:
                      e.tensor_add(o, a, b)), after=[tg, t_b1])
            tbf = p.op("scalar", (lambda e, o=u1bf[sl][:], i=u1_t[sl][:]:
                                  e.copy(o, i)), after=[tb, lastu.get(("bf", sl))])
            tlast = None
            for ic in range(ND):
                tlast = trp.tr(u1bf[sl][:, ic * 128:(ic + 1) * 128],
                               u1T[:, ic * T + r0: ic * T + r0 + 128], after=[tbf])
            lastu[("bf", sl)] = tlast
        p.barrier()
        s1.close()

        wqkv = sbuf(nc, sB, "f0_wqkv", [128, ND * 3 * D], BF16)
        t_w = []
        for ic in range(ND):
            t_w.append(p.dma("gpsimd", wqkv[:, ic * 3 * D:(ic + 1) * 3 * D],
                             io["w_qkv"][ic * 128:(ic + 1) * 128, :]))
        qT = sbuf(nc, sB, "f0_qT", [128, 2 * T], BF16)
        kT = sbuf(nc, sB, "f0_kT", [128, 2 * T], BF16)
        VW = 260
        vg = sbuf(nc, sB, "f0_vg", [128, NT * VW], BF16)
        ptile = [sbuf(nc, sB, f"f0_pt{i}", [128, 1024], BF16) for i in range(2)]
        postT = sbuf(nc, sB, "f0_postT", [128, 512])
        linv_r = sbuf(nc, sB, "f0_linv", [1, 512])
        lbc_sb = sbuf(nc, sB, "f0_lbc", [128, 512])
        postbf = sbuf(nc, sB, "f0_postbf", [128, 512], BF16)

        mm_slot = [zA[:, 0:512], zB[:, 0:512]]
        last_slot = [t_g1, t_b1]
        t_vones = p.op("vector", (lambda e, o=vg[:]: e.memset(o, 1.0)))
        t_postdma = {}
        pt_free = [None, None]
        last_avp = None
        last_lbp = None
        last_postT = None
        last_pbf = None

        for g in range(4):
            for part in ("q", "k"):
                woff = (0 if part == "q" else D) + g * 256
                dest = qT if part == "q" else kT
                for rb in range(2):
                    for tc in range(8):
                        psl = tc % 2
                        mmt = None
                        for ic in range(ND):
                            mmt = p.op("tensor", (lambda e, o=mm_slot[psl],
                                       l=wqkv[:, ic * 3 * D + woff + rb * 128:
                                              ic * 3 * D + woff + (rb + 1) * 128],
                                       r=u1T[:, ic * T + tc * 512: ic * T + (tc + 1) * 512],
                                       s=(ic == 0), st=(ic == ND - 1):
                                       e.matmul(o, l, r, start=s, stop=st)),
                                       after=[t_w[ic], last_slot[psl]])
                        tev = p.op("scalar", (lambda e,
                                   o=dest[:, rb * T + tc * 512: rb * T + (tc + 1) * 512],
                                   i=mm_slot[psl], s=(0.125 if part == "k" else 1.0):
                                   e.activation(o, i, AF.Identity, scale=s)),
                                   after=[mmt])
                        last_slot[psl] = tev
            woff = 2 * D + g * 256
            for ti in range(NT):
                for hh in range(2):
                    psl = hh % 2
                    mmt = None
                    for ic in range(ND):
                        mmt = p.op("tensor", (lambda e, o=mm_slot[psl][:, 0:128],
                                   l=u1T[:, ic * T + ti * 128: ic * T + ti * 128 + 128],
                                   r=wqkv[:, ic * 3 * D + woff + hh * 128:
                                          ic * 3 * D + woff + (hh + 1) * 128],
                                   s=(ic == 0), st=(ic == ND - 1):
                                   e.matmul(o, l, r, start=s, stop=st)),
                                   after=[t_w[ic], last_slot[psl]])
                    base = ti * VW + (2 * hh) * 65
                    tev = p.op("vector", (lambda e,
                               o=vg[:, base:base + 130].rearrange("p (s c) -> p s c", c=65)[:, :, 0:64],
                               i=mm_slot[psl][:, 0:128].rearrange("p (s c) -> p s c", c=64):
                               e.tensor_copy(o, i)), after=[mmt, t_vones])
                    last_slot[psl] = tev
            for hq in range(4):
                h = 4 * g + hq
                rb = hq // 2
                pb = 64 * (hq % 2)
                for tc in range(8):
                    avtags = []
                    pend = None

                    def flush_av(pd):
                        k2, tex = pd
                        for kk in range(2):
                            sj = 2 * k2 + kk
                            ta = p.op("tensor", (lambda e, o=avp[0:65, :],
                                      l=vg[:, sj * VW + hq * 65: sj * VW + hq * 65 + 65],
                                      r=ptile[k2 % 2][:, kk * 512:(kk + 1) * 512],
                                      s=(sj == 0), st=(sj == NT - 1):
                                      e.matmul(o, l, r, start=s, stop=st)),
                                      after=[tex, last_avp if sj == 0 else None])
                            avtags.append(ta)
                        pt_free[k2 % 2] = avtags[-1]

                    for k2 in range(NT // 2):
                        zsl = k2 % 2
                        zt = [zA, zB][zsl]
                        for kk in range(2):
                            si = 2 * k2 + kk
                            tz = p.op("tensor", (lambda e, o=zt[:, kk * 512:(kk + 1) * 512],
                                      l=kT[pb:pb + 64, rb * T + si * 128: rb * T + si * 128 + 128],
                                      r=qT[pb:pb + 64, rb * T + tc * 512: rb * T + (tc + 1) * 512]:
                                      e.matmul(o, l, r, start=True, stop=True)),
                                      after=[last_slot[zsl]] if kk == 0 else [])
                        if pend is not None:
                            flush_av(pend)
                        tex = p.op("scalar", (lambda e, o=ptile[k2 % 2][:], i=zt[:],
                                   : e.activation(o, i, AF.Exp)),
                                   after=[tz, pt_free[k2 % 2]])
                        last_slot[zsl] = tex
                        pend = (k2, tex)
                    flush_av(pend)
                    t_li = p.op("vector", (lambda e, o=linv_r[:], i=avp[64:65, :]:
                                e.reciprocal(o, i)), after=[avtags[-1]])
                    t_lb = p.op("tensor", (lambda e, o=lbp[0:64, :], l=cm.ones[0:1, 0:64],
                                r=linv_r[:]: e.matmul(o, l, r, start=True, stop=True)),
                                after=[t_li, cm.t_ones, last_lbp])
                    t_lbc = p.op("vector", (lambda e, o=lbc_sb[0:64, :], i=lbp[0:64, :]:
                                 e.tensor_copy(o, i)), after=[t_lb])
                    last_lbp = t_lbc
                    t_pn = p.op("vector", (lambda e, o=postbf[0:64, :], a=avp[0:64, :],
                                b=lbc_sb[0:64, :]: e.tensor_mul(o, a, b)),
                                after=[t_lbc, last_pbf])
                    last_avp = t_pn
                    t_pd = p.dma("sync", io["postt"][h * 64:(h + 1) * 64,
                                                     tc * 512:(tc + 1) * 512],
                                 postbf[0:64, :], after=[t_pn])
                    last_pbf = t_pd
        p.barrier()

    with ExitStack() as sC:
        wpr = sbuf(nc, sC, "f0_wpr", [128, ND * D], BF16)
        t_wp = []
        for ic in range(ND):
            t_wp.append(p.dma("gpsimd", wpr[:, ic * D:(ic + 1) * D],
                              io["w_proj"][ic * 128:(ic + 1) * 128, :]))
        pblk = sbuf(nc, sC, "f0_pblk", [128, ND * 512], BF16)
        stage = sbuf(nc, sC, "f0_stage", [128, ND * 512], BF16)
        xs_t4 = sbuf(nc, sC, "f0_xs4", [128, D])
        ua = sbuf(nc, sC, "f0_ua", [128, D])
        ub = sbuf(nc, sC, "f0_ub", [128, D])
        xl2 = sbuf(nc, sC, "f0_xl2", [128, D])
        xl2bf = sbuf(nc, sC, "f0_xl2bf", [128, D], BF16)
        at2 = [psum(nc, sC, f"f0_at2{i}", [128, 512]) for i in range(2)]
        g2_bc, t_g2 = bcast_vec(p, nc, sC, cm, "f0_g2", io["ln2_w"], D, at2)
        b2_bc, t_b2 = bcast_vec(p, nc, sC, cm, "f0_b2", io["ln2_b"], D, at2)
        p.barrier()
        last_ev4 = [None, None]
        last_stage = None
        last_pblk = None
        last_xs4 = None
        last_u2md = None
        last_tbf = None
        for tb in range(8):
            tpb = []
            for ic in range(ND):
                tpb.append(p.dma("sync", pblk[:, ic * 512:(ic + 1) * 512],
                                 io["postt"][ic * 128:(ic + 1) * 128,
                                             tb * 512:(tb + 1) * 512],
                                 after=[last_pblk]))
            tstg = []
            for tsub in range(4):
                r0 = tb * 512 + tsub * 128
                txs = p.dma("sync", xs_t4[:], io["xs"][r0:r0 + 128, :],
                            after=[last_xs4])
                mmt = [None, None]
                for oc in range(2):
                    for ic in range(ND):
                        mmt[oc] = p.op("tensor", (lambda e, o=at2[oc][:],
                                       l=pblk[:, ic * 512 + tsub * 128: ic * 512 + tsub * 128 + 128],
                                       r=wpr[:, ic * D + oc * 512: ic * D + (oc + 1) * 512],
                                       s=(ic == 0), st=(ic == ND - 1):
                                       e.matmul(o, l, r, start=s, stop=st)),
                                       after=[tpb[-1], last_ev4[oc], t_wp[ic]])
                tu = []
                for oc in range(2):
                    t2 = p.op("vector", (lambda e, o=ub[:, oc * 512:(oc + 1) * 512],
                              a=at2[oc][:], b=xs_t4[:, oc * 512:(oc + 1) * 512]:
                              e.tensor_add(o, a, b)), after=[mmt[oc], txs, last_u2md])
                    last_ev4[oc] = t2
                    tu.append(t2)
                last_xs4 = tu[-1]
                last_u2md = p.dma("sync", io["u2m"][r0:r0 + 128, :], ub[:], after=tu)
                tm, tr = ln_stats(p, ub[:], lns, after=tu)
                ta = ln_apply(p, ua[:], ub[:], lns, [tm, tr])
                tg = p.op("vector", (lambda e, o=ua[:], a=ua[:], b=g2_bc[:]:
                          e.tensor_mul(o, a, b)), after=[ta, t_g2])
                tb2 = p.op("vector", (lambda e, o=xl2[:], a=ua[:], b=b2_bc[:]:
                           e.tensor_add(o, a, b)), after=[tg, t_b2, last_tbf])
                tbf = p.op("scalar", (lambda e, o=xl2bf[:], i=xl2[:]: e.copy(o, i)),
                           after=[tb2])
                last_tbf = tbf
                for ic in range(ND):
                    tstg.append(trp.tr(xl2bf[:, ic * 128:(ic + 1) * 128],
                                       stage[:, ic * 512 + tsub * 128: ic * 512 + tsub * 128 + 128],
                                       after=[tbf, last_stage]))
            xlt_dma = []
            for ic in range(ND):
                xlt_dma.append(p.dma("sync", io["xln2t"][ic, :, tb * 512:(tb + 1) * 512],
                                     stage[:, ic * 512:(ic + 1) * 512], after=[tstg[-1]]))
            last_stage = xlt_dma[-1]
            last_pblk = mmt[1]
        p.barrier()

    with ExitStack() as sE:
        gw = sbuf(nc, sE, "f0_gw", [128, ND * 8], BF16)
        t_gw = []
        for ic in range(ND):
            t_gw.append(p.dma("gpsimd", gw[:, ic * 8:(ic + 1) * 8],
                              io["gate_w"][ic * 128:(ic + 1) * 128, :]))
        xlt_in = sbuf(nc, sE, "f0_xlt5", [128, ND * 128], BF16)
        glog = sbuf(nc, sE, "f0_glog", [128, 8])
        gmax = sbuf(nc, sE, "f0_gmax", [128, 8])
        gidx = sbuf(nc, sE, "f0_gidx", [128, 8], U32)
        gidxf = sbuf(nc, sE, "f0_gidxf", [128, 8])
        iota8i = sbuf(nc, sE, "f0_iota8i", [128, 8], I32)
        iota8 = sbuf(nc, sE, "f0_iota8", [128, 8])
        masks = sbuf(nc, sE, "f0_masks", [128, NT * 8])
        gps = psum(nc, sE, "f0_gps", [128, 512])
        gb_bc, t_gb = bcast_vec(p, nc, sE, cm, "f0_gb", io["gate_b"], 8, [gps])
        t_io = p.op("gpsimd", (lambda e, o=iota8i[:]:
                    e.iota(o, [[1, 8]], channel_multiplier=0)))
        t_iof = p.op("vector", (lambda e, o=iota8[:], i=iota8i[:]:
                     e.tensor_copy(o, i)), after=[t_io])
        last_xlt = None
        last_gps = t_gb
        for ti in range(NT):
            r0 = ti * 128
            txl = []
            for ic in range(ND):
                txl.append(p.dma("sync", xlt_in[:, ic * 128:(ic + 1) * 128],
                                 io["xln2t"][ic, :, r0:r0 + 128], after=[last_xlt]))
            mmt = None
            for ic in range(ND):
                mmt = p.op("tensor", (lambda e, o=gps[:, 0:8],
                           l=xlt_in[:, ic * 128:(ic + 1) * 128],
                           r=gw[:, ic * 8:(ic + 1) * 8],
                           s=(ic == 0), st=(ic == ND - 1):
                           e.matmul(o, l, r, start=s, stop=st)),
                           after=[txl[-1], t_gw[ic], last_gps])
            t_lg = p.op("vector", (lambda e, o=glog[:], a=gps[:, 0:8], b=gb_bc[:]:
                        e.tensor_add(o, a, b)), after=[mmt, t_gb])
            last_gps = t_lg
            t_mx = p.op("vector", (lambda e, o=gmax[:], i=glog[:]: e.max(o, i)),
                        after=[t_lg])
            t_ix = p.op("vector", (lambda e, o=gidx[:], m=gmax[:], i=glog[:]:
                        e.max_index(o, m, i)), after=[t_mx])
            t_if = p.op("vector", (lambda e, o=gidxf[:], i=gidx[:]:
                        e.tensor_copy(o, i)), after=[t_ix])
            t_m = p.op("vector", (lambda e, o=masks[:, ti * 8:(ti + 1) * 8],
                       i=iota8[:], s=gidxf[:, 0:1]:
                       e.tensor_single_scalar(o, i, s, ALU.is_equal)),
                       after=[t_if, t_iof])
            last_xlt = t_m
        p.barrier()

        with ExitStack() as s6:
            we1 = sbuf(nc, s6, "f0_we1", [128, ND * 4096], BF16)
            we2 = sbuf(nc, s6, "f0_we2", [128, 32 * D], BF16)
            hT = sbuf(nc, s6, "f0_hT", [128, 32 * 512], BF16)
            xlt6 = sbuf(nc, s6, "f0_xlt6", [128, ND * 512], BF16)
            yout = sbuf(nc, s6, "f0_yout", [128, D])
            hps = [psum(nc, s6, f"f0_hps{i}", [128, 512]) for i in range(2)]
            yps = [psum(nc, s6, f"f0_yps{i}", [128, 512]) for i in range(2)]
            last_we = None
            last_hps = [None, None]
            last_yps = [None, None]
            last_xlt6 = None
            last_yout = None
            for ex in range(8):
                t_we = []
                for ic in range(ND):
                    t_we.append(p.dma("gpsimd", we1[:, ic * 4096:(ic + 1) * 4096],
                                      io["w1"][ex, ic * 128:(ic + 1) * 128, :],
                                      after=[last_we]))
                for jc in range(32):
                    t_we.append(p.dma("gpsimd", we2[:, jc * D:(jc + 1) * D],
                                      io["w2"][ex, jc * 128:(jc + 1) * 128, :],
                                      after=[last_we]))
                t_we_last = t_we[-1]
                for tbk in range(8):
                    txl = []
                    for ic in range(ND):
                        txl.append(p.dma("sync", xlt6[:, ic * 512:(ic + 1) * 512],
                                         io["xln2t"][ic, :, tbk * 512:(tbk + 1) * 512],
                                         after=[last_xlt6]))
                    th = None
                    for j in range(32):
                        psl = j % 2
                        mmt = None
                        for ic in range(ND):
                            mmt = p.op("tensor", (lambda e, o=hps[psl][:],
                                       l=we1[:, ic * 4096 + j * 128: ic * 4096 + j * 128 + 128],
                                       r=xlt6[:, ic * 512:(ic + 1) * 512],
                                       s=(ic == 0), st=(ic == ND - 1):
                                       e.matmul(o, l, r, start=s, stop=st)),
                                       after=[txl[-1], last_hps[psl], t_we_last])
                        th = p.op("scalar", (lambda e, o=hT[:, j * 512:(j + 1) * 512],
                                  i=hps[psl][:]:
                                  e.activation(o, i, AF.Gelu_apprx_tanh)), after=[mmt])
                        last_hps[psl] = th
                    last_xlt6 = th
                    for tsub in range(4):
                        ti = tbk * 4 + tsub
                        r0 = ti * 128
                        mmt = [None, None]
                        for oc in range(2):
                            for j in range(32):
                                mmt[oc] = p.op("tensor", (lambda e, o=yps[oc][:],
                                               l=hT[:, j * 512 + tsub * 128: j * 512 + tsub * 128 + 128],
                                               r=we2[:, j * D + oc * 512: j * D + (oc + 1) * 512],
                                               s=(j == 0), st=(j == 31):
                                               e.matmul(o, l, r, start=s, stop=st)),
                                               after=[th, last_yps[oc]])
                        tev = []
                        for oc in range(2):
                            t1 = p.op("vector", (lambda e, o=yout[:, oc * 512:(oc + 1) * 512],
                                      i=yps[oc][:], m=masks[:, ti * 8 + ex: ti * 8 + ex + 1]:
                                      e.tensor_scalar_mul(o, i, m)),
                                      after=[mmt[oc], last_yout])
                            last_yps[oc] = t1
                            tev.append(t1)
                        last_yout = p.dma("gpsimd", io["moe"][r0:r0 + 128, :], yout[:],
                                          after=tev,
                                          **({} if ex == 0 else {"accum_op": ALU.add}))
                last_we = last_yout
            p.barrier()

    with ExitStack() as sF:
        xa = sbuf(nc, sF, "f0_xa", [128, D])
        xb = sbuf(nc, sF, "f0_xb", [128, D])
        xc = sbuf(nc, sF, "f0_xc", [128, D])
        last7 = [None, None, None]
        lastst = None
        for ti in range(NT):
            r0 = ti * 128
            tu = p.dma("sync", xa[:], io["u2m"][r0:r0 + 128, :], after=[last7[0]])
            tm_ = p.dma("sync", xb[:], io["moe"][r0:r0 + 128, :], after=[last7[1]])
            tadd = p.op("vector", (lambda e, o=xa[:], a=xa[:], b=xb[:]:
                        e.tensor_add(o, a, b)), after=[tu, tm_])
            last7[0] = tadd
            tx = p.dma("sync", xb[:], io["x"][r0:r0 + 128, :], after=[tadd])
            tm, tr = ln_stats(p, xa[:], lns, after=[tadd])
            ta = ln_apply(p, xc[:], xa[:], lns, [tm, tr, lastst])
            tx2 = p.op("vector", (lambda e, o=xc[:], a=xc[:], b=xb[:]:
                       e.tensor_add(o, a, b)), after=[ta, tx])
            last7[1] = tx2
            last7[2] = tx2
            lastst = p.dma("sync", io["x2"][r0:r0 + 128, :], xc[:], after=[tx2])
        p.barrier()


def build_f0():
    nc = bass.Bass("TRN2", target_bir_lowering=False)
    io = {
        "x": _din(nc, "x", [T, D]),
        "w_qkv": _din(nc, "w_qkv", [D, 3 * D]),
        "w_proj": _din(nc, "w_proj", [D, D]),
        "ln1_w": _din(nc, "ln1_w", [D]),
        "ln1_b": _din(nc, "ln1_b", [D]),
        "ln2_w": _din(nc, "ln2_w", [D]),
        "ln2_b": _din(nc, "ln2_b", [D]),
        "gate_w": _din(nc, "gate_w", [D, 8]),
        "gate_b": _din(nc, "gate_b", [8]),
        "w1": _din(nc, "w1", [8, D, 4096]),
        "w2": _din(nc, "w2", [8, 4096, D]),
        "iden": _din(nc, "iden", [128, 128], BF16),
        "x2": _dout(nc, "x2", [T, D]),
        "xs": _dtmp(nc, "f0xs", [T, D]),
        "postt": _dtmp(nc, "f0postt", [D, T], BF16),
        "u2m": _dtmp(nc, "f0u2m", [T, D]),
        "xln2t": _dtmp(nc, "f0xln2t", [ND, 128, T], BF16),
        "moe": _dtmp(nc, "f0moe", [T, D]),
    }
    with ExitStack() as ctx:
        p = Prog(nc, ctx)
        cm = Common(p, nc, ctx, io)
        emit_f0(p, nc, ctx, cm, io)
        p.barrier()
        p.replay()
    return nc


def _f0_inmaps(inputs, x1_shards, cores):
    m = []
    for j, c in enumerate(cores):
        m.append({
            "x": np.ascontiguousarray(x1_shards[j]),
            "w_qkv": inputs["L0_ffn_w_qkv"],
            "w_proj": inputs["L0_ffn_w_proj"],
            "ln1_w": inputs["L0_ffn_ln1_w"],
            "ln1_b": inputs["L0_ffn_ln1_b"],
            "ln2_w": inputs["L0_ffn_ln2_w"],
            "ln2_b": inputs["L0_ffn_ln2_b"],
            "gate_w": inputs["L0_ffn_gate_w"],
            "gate_b": inputs["L0_ffn_gate_b"],
            "w1": inputs["L0_ffn_w1"],
            "w2": inputs["L0_ffn_w2"],
            "iden": IDEN,
        })
    return m


def emit_a1(p, nc, ctx, cm, io):
    H = 16
    trp = TrPool(p, nc, ctx, cm, "a1trp")
    lns = LNS(nc, ctx, "a1_lns")

    DEC = [float(np.log(1.0 - 2.0 ** (-5.0 - h))) for h in range(H)]

    tab = ExitStack()
    ctx.enter_context(tab)
    i32r = sbuf(nc, ctx, "a1_i32r", [1, 128], I32)
    posr = sbuf(nc, ctx, "a1_posr", [1, 128])
    ang64 = sbuf(nc, ctx, "a1_ang64", [1, 64])
    angrep = sbuf(nc, ctx, "a1_angrep", [1, D])
    tmp32 = sbuf(nc, ctx, "a1_tmp32", [1, 32])
    t_i = p.op("gpsimd", (lambda e, o=i32r[:]: e.iota(o, [[1, 128]], channel_multiplier=0)))
    t_p = p.op("vector", (lambda e, o=posr[:], i=i32r[:]: e.tensor_copy(o, i)), after=[t_i])
    t_a1 = p.op("vector", (lambda e, o=tmp32[:], i=posr[0:1, 0:32]:
                 e.tensor_scalar_mul(o, i, float(-np.log(10000.0) / 31.0))), after=[t_p])
    t_a2 = p.op("scalar", (lambda e, o=tmp32[:], i=tmp32[:]:
                 e.activation(o, i, AF.Exp)), after=[t_a1])
    t_a3 = p.op("vector", (lambda e, o=ang64[0:1, 0:64:2], i=tmp32[:]:
                 e.tensor_copy(o, i)), after=[t_a2])
    t_a4 = p.op("vector", (lambda e, o=ang64[0:1, 1:64:2], i=tmp32[:]:
                 e.tensor_copy(o, i)), after=[t_a2, t_a3])
    t_ar = []
    for rr in range(16):
        t_ar.append(p.op("vector", (lambda e, o=angrep[0:1, rr * 64:(rr + 1) * 64],
                    i=ang64[:]: e.tensor_copy(o, i)), after=[t_a4, t_a3]))
    t_ar_last = t_ar[-1]

    with ExitStack() as s1:
        x_t = [sbuf(nc, s1, f"a1_x{i}", [128, D]) for i in range(2)]
        w_t = [sbuf(nc, s1, f"a1_w{i}", [128, D]) for i in range(2)]
        wbf = [sbuf(nc, s1, f"a1_wbf{i}", [128, D], BF16) for i in range(2)]
        stage1 = sbuf(nc, s1, "a1_stage1", [128, ND * 512], BF16)
        lastu = {}
        last_st1 = None
        for tb in range(8):
            tstg = []
            for tsub in range(4):
                ti = tb * 4 + tsub
                sl = ti % 2
                r0 = ti * 128
                tx = p.dma("sync", x_t[sl][:], io["x"][r0:r0 + 128, :],
                           after=[lastu.get(("x", sl))])
                tm, tr = ln_stats(p, x_t[sl][:], lns, after=[tx])
                ts = ln_apply(p, w_t[sl][:], x_t[sl][:], lns,
                              [tm, tr, lastu.get(("wbf", sl))])
                lastu[("x", sl)] = ts
                tbf = p.op("scalar", (lambda e, o=wbf[sl][:], i=w_t[sl][:]:
                                      e.copy(o, i)), after=[ts, lastu.get(("bf", sl))])
                lastu[("wbf", sl)] = tbf
                for ic in range(ND):
                    tstg.append(trp.tr(wbf[sl][:, ic * 128:(ic + 1) * 128],
                                       stage1[:, ic * 512 + tsub * 128: ic * 512 + tsub * 128 + 128],
                                       after=[tbf, last_st1]))
                lastu[("bf", sl)] = tstg[-1]
            dmas = []
            for ic in range(ND):
                dmas.append(p.dma("sync", io["xwt"][ic, :, tb * 512:(tb + 1) * 512],
                                  stage1[:, ic * 512:(ic + 1) * 512], after=[tstg[-1]]))
            last_st1 = dmas[-1]
        p.barrier()

    qrT = sbuf(nc, ctx, "a1_qrT", [128, 8 * T], BF16)
    krT = sbuf(nc, ctx, "a1_krT", [128, 8 * T], BF16)
    io["_qrT"] = qrT
    io["_krT"] = krT
    with ExitStack() as s2:
        xwt_in = sbuf(nc, s2, "a1_xwt2", [128, ND * 128], BF16)
        qps = [psum(nc, s2, f"a1_qps{i}", [128, 512]) for i in range(2)]
        cps = psum(nc, s2, "a1_cps", [128, 1024])
        dk_r = sbuf(nc, s2, "a1_dkr", [1, 16])
        omr_r = sbuf(nc, s2, "a1_omr", [1, 16])
        t_d1 = p.op("vector", (lambda e, o=dk_r[:], i=posr[0:1, 0:16]:
                    e.tensor_scalar(o, i, float(-np.log(2.0)), float(-5.0 * np.log(2.0)),
                                    ALU.mult, ALU.add)), after=[t_p])
        t_d2 = p.op("scalar", (lambda e, o=omr_r[:], i=dk_r[:]:
                    e.activation(o, i, AF.Exp)), after=[t_d1])
        t_d3 = p.op("vector", (lambda e, o=omr_r[:], i=omr_r[:]:
                    e.tensor_scalar(o, i, -1.0, 1.0, ALU.mult, ALU.add)), after=[t_d2])
        t_d4 = p.op("scalar", (lambda e, o=dk_r[:], i=omr_r[:]:
                    e.activation(o, i, AF.Ln)), after=[t_d3])
        t_d5 = p.op("vector", (lambda e, o=omr_r[:], i=omr_r[:]:
                    e.tensor_scalar(o, i, -1.0, 1.0, ALU.mult, ALU.add)), after=[t_d4])
        t_d6 = p.op("vector", (lambda e, o=omr_r[:], i=omr_r[:]:
                    e.reciprocal(o, i)), after=[t_d5])
        omr_bc = sbuf(nc, s2, "a1_omrbc", [128, 16])
        t_ob = p.op("tensor", (lambda e, o=cps[:, 0:16], l=cm.ones[:], r=omr_r[:]:
                    e.matmul(o, l, r, start=True, stop=True)), after=[t_d6, cm.t_ones])
        t_ob2 = p.op("vector", (lambda e, o=omr_bc[:], i=cps[:, 0:16]:
                     e.tensor_copy(o, i)), after=[t_ob])
        dk_bc = sbuf(nc, s2, "a1_dkbc", [128, 16])
        t_db = p.op("tensor", (lambda e, o=cps[:, 16:32], l=cm.ones[:], r=dk_r[:]:
                    e.matmul(o, l, r, start=True, stop=True)), after=[t_d4, cm.t_ones, t_ob2])
        t_db2 = p.op("vector", (lambda e, o=dk_bc[:], i=cps[:, 16:32]:
                     e.tensor_copy(o, i)), after=[t_db])
        ep_t = sbuf(nc, s2, "a1_ept", [128, 16])
        t_e1 = p.op("tensor", (lambda e, o=cps[:, 32:48], l=posr[:], r=dk_r[:]:
                    e.matmul(o, l, r, start=True, stop=True)), after=[t_d4, t_p, t_db2])
        t_e2 = p.op("vector", (lambda e, o=ep_t[:], i=cps[:, 32:48]:
                    e.tensor_scalar_mul(o, i, -1.0)), after=[t_e1])
        t_e3 = p.op("scalar", (lambda e, o=ep_t[:], i=ep_t[:]:
                    e.activation(o, i, AF.Exp)), after=[t_e2])
        u_ps = sbuf(nc, s2, "a1_ups", [128, D])
        prow = sbuf(nc, s2, "a1_prow", [1, 128])
        s2a = s2.enter_context(ExitStack())
        wq = sbuf(nc, s2a, "a1_wq", [128, ND * D], BF16)
        wk = wq
        bq_bc, t_bq = bcast_vec(p, nc, s2a, cm, "a1_bq", io["bq"], D, qps)
        bk_bc, t_bk = bcast_vec(p, nc, s2a, cm, "a1_bk", io["bk"], D, qps)
        t_bk = p.op("vector", (lambda e, o=bk_bc[:], i=bk_bc[:]:
                    e.tensor_scalar_mul(o, i, 0.125)), after=[t_bk])
        cosf = sbuf(nc, s2a, "a1_cosf", [128, D])
        sinf = sbuf(nc, s2a, "a1_sinf", [128, D])
        rri = sbuf(nc, s2a, "a1_rri", [128, D], I32)
        rrf = sbuf(nc, s2a, "a1_rrf", [128, D])
        q_f = sbuf(nc, s2a, "a1_qf", [128, D])
        t1_f = sbuf(nc, s2a, "a1_t1f", [128, D])
        t2_f = sbuf(nc, s2a, "a1_t2f", [128, D])
        qrbf = sbuf(nc, s2a, "a1_qrbf", [128, D], BF16)
        ap_t = sbuf(nc, s2a, "a1_apt", [128, 16])
        rr_t = sbuf(nc, s2a, "a1_rrt", [128, 16])
        sc1 = sbuf(nc, s2a, "a1_sc1", [128, 1])
        sc2 = sbuf(nc, s2a, "a1_sc2", [128, 1])
        TWO_PI = float(2.0 * np.pi)

        def build_trig(r0, after):
            tpr = p.op("vector", (lambda e, o=prow[:], i=posr[:], c=float(r0):
                       e.tensor_scalar_add(o, i, c)),
                       after=list(after) + [last_q.get("trigmm")])
            war = last_q.get("trig")
            tm1 = p.op("tensor", (lambda e, o=cps[:, 0:512], l=prow[:], r=angrep[0:1, 0:512]:
                       e.matmul(o, l, r, start=True, stop=True)), after=[tpr, t_ar_last, war])
            tm2 = p.op("tensor", (lambda e, o=cps[:, 512:1024], l=prow[:],
                       r=angrep[0:1, 512:1024]:
                       e.matmul(o, l, r, start=True, stop=True)), after=[tpr, t_ar_last, war])
            last_q["trigmm"] = tm2
            tags = []
            for fn, dest, shift in (("sin", sinf, 0.0), ("cos", cosf, float(np.pi / 2))):
                tu = p.op("vector", (lambda e, o=u_ps[:], i=cps[:], sh=shift:
                          e.tensor_scalar(o, i, 1.0 / TWO_PI, sh / TWO_PI,
                                          ALU.mult, ALU.add)), after=[tm1, tm2, tags[-1] if tags else None])
                ti1 = p.op("vector", (lambda e, o=rri[:], i=u_ps[:]:
                           e.tensor_copy(o, i)), after=[tu])
                ti2 = p.op("vector", (lambda e, o=rrf[:], i=rri[:]:
                           e.tensor_copy(o, i)), after=[ti1])
                ts_ = p.op("vector", (lambda e, o=u_ps[:], a=u_ps[:], b=rrf[:]:
                           e.tensor_sub(o, a, b)), after=[ti2])
                tsc = p.op("vector", (lambda e, o=u_ps[:], i=u_ps[:]:
                           e.tensor_scalar_mul(o, i, TWO_PI)), after=[ts_])
                tsn = p.op("scalar", (lambda e, o=dest[:], i=u_ps[:]:
                           e.activation(o, i, AF.Sin)), after=[tsc])
                tags.append(tsn)
            last_q["trig"] = tags[-1]
            return tags

        def build_ap(r0, after):
            s0 = r0 % 512
            tpr = p.op("vector", (lambda e, o=prow[:], i=posr[:]:
                       e.tensor_scalar_add(o, i, float(r0 + 1))), after=after)
            tm = p.op("tensor", (lambda e, o=cps[:, 0:16], l=prow[:], r=dk_r[:]:
                      e.matmul(o, l, r, start=True, stop=True)), after=[tpr, t_d4])
            te = p.op("scalar", (lambda e, o=rr_t[:], i=cps[:, 0:16]:
                      e.activation(o, i, AF.Exp)), after=[tm])
            t1 = p.op("vector", (lambda e, o=rr_t[:], i=rr_t[:]:
                      e.tensor_scalar(o, i, -1.0, 1.0, ALU.mult, ALU.add)), after=[te])
            t2 = p.op("vector", (lambda e, o=rr_t[:], a=rr_t[:], b=omr_bc[:]:
                      e.tensor_mul(o, a, b)), after=[t1, t_ob2])
            t3 = p.op("scalar", (lambda e, o=rr_t[:], i=rr_t[:]:
                      e.activation(o, i, AF.Ln)), after=[t2])
            t4 = p.op("scalar", (lambda e, o=rr_t[:], i=rr_t[:]:
                      e.activation(o, i, AF.Exp, scale=-0.5)), after=[t3])
            t5 = p.op("vector", (lambda e, o=ap_t[:], i=dk_bc[:]:
                      e.tensor_scalar_mul(o, i, float(s0))), after=[t_db2, t4])
            return t4, t5

        pos_col = sbuf(nc, s2a, "a1_poscol", [128, 1])
        t_pc = p.op("gpsimd", (lambda e, o=rri[:, 0:1]:
                    e.iota(o, [[1, 1]], channel_multiplier=1)))
        t_pc2 = p.op("vector", (lambda e, o=pos_col[:], i=rri[:, 0:1]:
                     e.tensor_copy(o, i)), after=[t_pc])

        last_q = {}
        for part, wsb, bias_bc, t_bias, destT in (
                ("q", wq, bq_bc, t_bq, qrT), ("k", wk, bk_bc, t_bk, krT)):
            twl = []
            for ic in range(ND):
                twl.append(p.dma("gpsimd", wsb[:, ic * D:(ic + 1) * D],
                                 io["Wq" if part == "q" else "Wk"][ic * 128:(ic + 1) * 128, :],
                                 after=[last_q.get("wconsume")]))
            for ti in range(NT):
                r0 = ti * 128
                txw = []
                for ic in range(ND):
                    txw.append(p.dma("sync", xwt_in[:, ic * 128:(ic + 1) * 128],
                                     io["xwt"][ic, :, r0:r0 + 128],
                                     after=[last_q.get("xwt")]))
                mmt = [None, None]
                for oc in range(2):
                    for ic in range(ND):
                        mmt[oc] = p.op("tensor", (lambda e, o=qps[oc][:],
                                       l=xwt_in[:, ic * 128:(ic + 1) * 128],
                                       r=wsb[:, ic * D + oc * 512: ic * D + (oc + 1) * 512],
                                       s=(ic == 0), st=(ic == ND - 1):
                                       e.matmul(o, l, r, start=s, stop=st)),
                                       after=[txw[-1], twl[ic], last_q.get(("ev", oc))])
                sc_k = 0.125 if part == "k" else 1.0
                tq = []
                for oc in range(2):
                    tq.append(p.op("vector", (lambda e, o=q_f[:, oc * 512:(oc + 1) * 512],
                              i=qps[oc][:], b=bias_bc[:, oc * 512:(oc + 1) * 512], s=sc_k:
                              e.scalar_tensor_tensor(out=o, in0=i, scalar=s, in1=b,
                                  op0=ALU.mult, op1=ALU.add)),
                              after=[mmt[oc], t_bias, last_q.get("qf")]))
                    last_q[("ev", oc)] = tq[-1]
                ttr = build_trig(r0, tq)
                tr1 = p.op("vector", (lambda e, o=t1_f[:, 0:D:2], i=q_f[:, 1:D:2]:
                           e.tensor_scalar_mul(o, i, -1.0)), after=tq + [last_q.get("t1")])
                tr2 = p.op("vector", (lambda e, o=t1_f[:, 1:D:2], i=q_f[:, 0:D:2]:
                           e.tensor_copy(o, i)), after=tq + [tr1])
                tm1 = p.op("vector", (lambda e, o=t2_f[:], a=q_f[:], b=cosf[:]:
                           e.tensor_mul(o, a, b)), after=tq + ttr + [last_q.get("t2")])
                tm2 = p.op("vector", (lambda e, o=t1_f[:], a=t1_f[:], b=sinf[:]:
                           e.tensor_mul(o, a, b)), after=[tr1, tr2] + ttr)
                tad = p.op("vector", (lambda e, o=t2_f[:], a=t2_f[:], b=t1_f[:]:
                           e.tensor_add(o, a, b)), after=[tm1, tm2])
                last_q["t1"] = tad
                last_q["qf"] = tad
                if part == "q":
                    s0 = r0 % 512
                    tp1 = p.op("vector", (lambda e, o=sc1[:], i=pos_col[:], c=float(s0):
                               e.tensor_scalar_add(o, i, c)), after=[t_pc2, last_q.get("sc1")])
                    tp2 = p.op("vector", (lambda e, o=rr_t[:], a=dk_bc[:], b=sc1[:]:
                               e.tensor_scalar_mul(o, a, b)), after=[tp1, t_db2, last_q.get("rrt")])
                    last_q["sc1"] = tp2
                    tp3 = p.op("scalar", (lambda e, o=rr_t[:], i=rr_t[:]:
                               e.activation(o, i, AF.Exp)), after=[tp2])
                    tp4 = p.op("vector", (lambda e, o=sc2[:], i=pos_col[:], c=float(r0 + 1):
                               e.tensor_scalar_add(o, i, c)), after=[t_pc2, last_q.get("sc2")])
                    tp5 = p.op("vector", (lambda e, o=ap_t[:], a=dk_bc[:], b=sc2[:]:
                               e.tensor_scalar_mul(o, a, b)), after=[tp4, last_q.get("apt")])
                    last_q["sc2"] = tp5
                    tp6 = p.op("scalar", (lambda e, o=ap_t[:], i=ap_t[:]:
                               e.activation(o, i, AF.Exp)), after=[tp5])
                    tp7 = p.op("vector", (lambda e, o=ap_t[:], i=ap_t[:]:
                               e.tensor_scalar(o, i, -1.0, 1.0, ALU.mult, ALU.add)), after=[tp6])
                    tp8 = p.op("vector", (lambda e, o=ap_t[:], a=ap_t[:], b=omr_bc[:]:
                               e.tensor_mul(o, a, b)), after=[tp7, t_ob2])
                    tp9 = p.op("scalar", (lambda e, o=ap_t[:], i=ap_t[:]:
                               e.activation(o, i, AF.Ln)), after=[tp8])
                    tp10 = p.op("scalar", (lambda e, o=ap_t[:], i=ap_t[:]:
                                e.activation(o, i, AF.Exp, scale=-0.5)), after=[tp9])
                    tfold = p.op("vector", (lambda e, o=rr_t[:], a=rr_t[:], b=ap_t[:]:
                                 e.tensor_mul(o, a, b)), after=[tp3, tp10])
                    last_q["apt"] = tfold
                    last_q["rrt"] = tfold
                    ftab, t_ft = rr_t, tfold
                else:
                    ftab, t_ft = ep_t, t_e3
                tcast = []
                for h in range(H):
                    tcast.append(p.op("vector", (lambda e,
                                 o=qrbf[:, h * 64:(h + 1) * 64],
                                 i=t2_f[:, h * 64:(h + 1) * 64],
                                 s=ftab[:, h:h + 1]:
                                 e.tensor_scalar_mul(o, i, s)),
                                 after=[tad, t_ft, last_q.get("qrbf")]))
                last_q["t2"] = tcast[-1]
                ttp = None
                for ic in range(ND):
                    ttp = trp.tr(qrbf[:, ic * 128:(ic + 1) * 128],
                                 destT[:, ic * T + r0: ic * T + r0 + 128],
                                 after=[tcast[-1]])
                last_q["xwt"] = ttp
                last_q["qrbf"] = ttp
                last_q["wconsume"] = tq[-1]
        p.barrier()
        s2a.close()

        wv = sbuf(nc, s2, "a1_wv", [128, ND * 2048], BF16)
        vstage = sbuf(nc, s2, "a1_vstage", [128, 16 * 129], BF16)
        gstage = sbuf(nc, s2, "a1_gstage", [128, 2048], BF16)
        bv_bc1, t_bv1 = bcast_vec(p, nc, s2, cm, "a1_bv1", io["bv"][0:1024], D, qps)
        bv_bc2, t_bv2 = bcast_vec(p, nc, s2, cm, "a1_bv2", io["bv"][1024:2048], D, qps)
        t_vone = p.op("vector", (lambda e, o=vstage[:]: e.memset(o, 1.0)))
        for part in ("v", "g"):
            t_wv = []
            for ic in range(ND):
                t_wv.append(p.dma("gpsimd", wv[:, ic * 2048:(ic + 1) * 2048],
                                  io["Wv" if part == "v" else "Wg"][ic * 128:(ic + 1) * 128, :],
                                  after=[last_q.get("vst")]))
            if part == "g":
                bv_bc1, t_bv1 = bcast_vec(p, nc, s2, cm, "a1_bg1", io["bg"][0:1024], D, qps)
                bv_bc2, t_bv2 = bcast_vec(p, nc, s2, cm, "a1_bg2", io["bg"][1024:2048], D, qps)
            for ti in range(NT):
                r0 = ti * 128
                txw = []
                for ic in range(ND):
                    txw.append(p.dma("sync", xwt_in[:, ic * 128:(ic + 1) * 128],
                                     io["xwt"][ic, :, r0:r0 + 128],
                                     after=[last_q.get("xwt")]))
                for half in range(2):
                    bbc = [bv_bc1, bv_bc2][half]
                    tbb = [t_bv1, t_bv2][half]
                    for oc in range(2):
                        mmt = None
                        for ic in range(ND):
                            mmt = p.op("tensor", (lambda e, o=qps[oc][:],
                                       l=xwt_in[:, ic * 128:(ic + 1) * 128],
                                       r=wv[:, ic * 2048 + half * 1024 + oc * 512:
                                            ic * 2048 + half * 1024 + (oc + 1) * 512],
                                       s=(ic == 0), st=(ic == ND - 1):
                                       e.matmul(o, l, r, start=s, stop=st)),
                                       after=[txw[-1], t_wv[ic], last_q.get(("pv", oc))])
                        co = half * 1024 + oc * 512
                        if part == "v":
                            base = (co // 128) * 129
                            tev = p.op("vector", (lambda e,
                                       o=vstage[:, base:base + 4 * 129].rearrange(
                                           "p (s c) -> p s c", c=129)[:, :, 0:128],
                                       i=qps[oc][:].rearrange("p (s c) -> p s c", c=128),
                                       b=bbc[:, (co % 1024):(co % 1024) + 512].rearrange(
                                           "p (s c) -> p s c", c=128):
                                       e.scalar_tensor_tensor(out=o, in0=i, scalar=1.0,
                                           in1=b, op0=ALU.mult, op1=ALU.add)),
                                       after=[mmt, tbb, t_vone, last_q.get("vst")])
                            last_q["vst"] = tev
                        else:
                            t_g1 = p.op("vector", (lambda e, o=u_ps[:, 0:512],
                                        i=qps[oc][:], b=bbc[:, (co % 1024):(co % 1024) + 512]:
                                        e.tensor_add(o, i, b)),
                                        after=[mmt, tbb, last_q.get("gs1")])
                            tev = p.op("scalar", (lambda e, o=gstage[:, co:co + 512],
                                       i=u_ps[:, 0:512]:
                                       e.activation(o, i, AF.Silu)),
                                       after=[t_g1, last_q.get("vst")])
                            last_q["vst"] = tev
                            last_q["gs1"] = tev
                        last_q[("pv", oc)] = tev
                if part == "v":
                    tdm = p.dma("sync", io["vr"][r0:r0 + 128, :], vstage[:],
                                after=[last_q["vst"]])
                else:
                    tdm = p.dma("sync", io["sg"][r0:r0 + 128, :], gstage[:],
                                after=[last_q["vst"]])
                last_q["vst"] = tdm
                last_q["gs1"] = tdm
        p.barrier()

    with ExitStack() as s3:
        lns128 = LNS(nc, s3, "a1_lns128", width=128)
        zpp = [psum(nc, s3, f"a1_zpp{i}", [128, 512]) for i in range(2)]
        ops = [psum(nc, s3, f"a1_ops{i}", [128, 129]) for i in range(4)]
        vr_in = [sbuf(nc, s3, f"a1_vrin{i}", [128, 129], BF16) for i in range(2)]
        qkbf = sbuf(nc, s3, "a1_qkbf", [128, 512], BF16)
        cmask = sbuf(nc, s3, "a1_cmask", [128, 4 * 512])
        grid = sbuf(nc, s3, "a1_grid", [128, 512])
        i512 = sbuf(nc, s3, "a1_i512", [1, 512], I32)
        f512 = sbuf(nc, s3, "a1_f512", [1, 512])
        pcol = sbuf(nc, s3, "a1_pcol", [128, 1])
        den = sbuf(nc, s3, "a1_den", [128, 1])
        sgt = sbuf(nc, s3, "a1_sgt", [128, 128], BF16)
        ret = sbuf(nc, s3, "a1_ret", [128, 128])
        obf = sbuf(nc, s3, "a1_obf", [128, 128], BF16)
        stage3 = sbuf(nc, s3, "a1_stage3", [128, 512], BF16)
        t_i5 = p.op("gpsimd", (lambda e, o=i512[:]: e.iota(o, [[1, 512]], channel_multiplier=0)))
        t_f5 = p.op("vector", (lambda e, o=f512[:], i=i512[:]: e.tensor_copy(o, i)), after=[t_i5])
        t_pc3 = p.op("gpsimd", (lambda e, o=i512[0:1, 0:1]: 0) if False else
                     (lambda e, o=i512[0:1, 0:1]: e.iota(o, [[1, 1]], channel_multiplier=0)))
        icol = sbuf(nc, s3, "a1_icol", [128, 1], I32)
        t_ic = p.op("gpsimd", (lambda e, o=icol[:]: e.iota(o, [[1, 1]], channel_multiplier=1)))
        t_ic2 = p.op("vector", (lambda e, o=pcol[:], i=icol[:]: e.tensor_copy(o, i)), after=[t_ic])
        t_gr = p.op("tensor", (lambda e, o=zpp[0][:], l=cm.ones[:], r=f512[:]:
                    e.matmul(o, l, r, start=True, stop=True)), after=[t_f5, cm.t_ones])
        t_gr2 = p.op("vector", (lambda e, o=grid[:], i=zpp[0][:]: e.tensor_copy(o, i)), after=[t_gr])
        tmsk = []
        for j in range(4):
            tj = p.op("vector", (lambda e, o=den[:], i=pcol[:], c=float(128 * j):
                      e.tensor_scalar_add(o, i, c)), after=[t_ic2, tmsk[-1] if tmsk else None])
            tmsk.append(p.op("vector", (lambda e, o=cmask[:, j * 512:(j + 1) * 512],
                        i=grid[:], s=den[:]:
                        e.tensor_single_scalar(o, i, s, ALU.is_ge)), after=[tj, t_gr2]))
        t_cm = tmsk[-1]
        p.barrier()

        last3 = {}
        for h in range(H):
            hp, pb = h // 2, 64 * (h % 2)
            dec = DEC[h]
            for sc in range(8):
                av_started = False
                avt = None
                for tt in range(4 * sc + 4):
                    tvr = p.dma("scalar", vr_in[tt % 2][:],
                                io["vr"][tt * 128:(tt + 1) * 128, h * 129:(h + 1) * 129],
                                after=[last3.get(("vr", tt % 2))])
                    zsl = tt % 2
                    tz = p.op("tensor", (lambda e, o=zpp[zsl][:],
                              l=krT[pb:pb + 64, hp * T + tt * 128: hp * T + tt * 128 + 128],
                              r=qrT[pb:pb + 64, hp * T + sc * 512: hp * T + (sc + 1) * 512]:
                              e.matmul(o, l, r, start=True, stop=True)),
                              after=[last3.get(("z", zsl))])
                    f_tile = float(np.exp(dec * (512 * sc - 128 * tt)))
                    diag = (tt >= 4 * sc)
                    if diag:
                        jj = tt - 4 * sc
                        ta_ = p.op("scalar", (lambda e, o=grid[:], i=zpp[zsl][:], f=f_tile:
                                   e.activation(o, i, AF.Identity, scale=f)),
                                   after=[tz, last3.get("qk"), last3.get("qk2"), t_cm,
                                          last3.get("grd")])
                        tev = p.op("vector", (lambda e, o=qkbf[:], a=grid[:],
                                   b=cmask[:, jj * 512:(jj + 1) * 512]:
                                   e.tensor_mul(o, a, b)), after=[ta_, last3.get("qkb")])
                        last3[("z", zsl)] = ta_
                        last3["grd"] = tev
                    else:
                        tev = p.op("scalar", (lambda e, o=qkbf[:], i=zpp[zsl][:], f=f_tile:
                                   e.activation(o, i, AF.Identity, scale=f)),
                                   after=[tz, last3.get("qk"), last3.get("qkb"),
                                          last3.get("qk2")])
                        last3[("z", zsl)] = tev
                    last3["qkb"] = tev
                    last3["qk"] = tev
                    newav = []
                    for sub in range(4):
                        avt = p.op("tensor", (lambda e, o=ops[sub][0:128, :],
                                   l=qkbf[:, sub * 128:(sub + 1) * 128],
                                   r=vr_in[tt % 2][:],
                                   s=(not av_started), st=(tt == 4 * sc + 3):
                                   e.matmul(o, l, r, start=s, stop=st)),
                                   after=[tev, tvr, last3.get(("ev", sub))])
                        newav.append(avt)
                    av_started = True
                    last3[("vr", tt % 2)] = newav[-1]
                    last3["qk2"] = newav[-1]
                for sub in range(4):
                    r0 = sc * 512 + sub * 128
                    t_d = p.op("vector", (lambda e, o=den[:], i=ops[sub][:, 128:129]:
                               e.tensor_scalar_mul(o, i, -1.0)),
                               after=[newav[-1], last3.get("den")])
                    t_d1 = p.op("vector", (lambda e, o=den[:], a=ops[sub][:, 128:129], b=den[:]:
                                e.tensor_max(o, a, b)), after=[t_d])
                    t_d2 = p.op("vector", (lambda e, o=den[:], i=den[:]:
                                e.tensor_scalar_max(o, i, 1.0)), after=[t_d1])
                    t_d3 = p.op("vector", (lambda e, o=den[:], i=den[:]:
                                e.reciprocal(o, i)), after=[t_d2])
                    t_r = p.op("vector", (lambda e, o=ret[:], i=ops[sub][:, 0:128],
                               s=den[:]: e.tensor_scalar_mul(o, i, s)),
                               after=[t_d3, last3.get("ret")])
                    last3[("ev", sub)] = t_r
                    last3["den"] = t_r
                    tm, tr = ln_stats(p, ret[:], lns128, after=[t_r])
                    ta = ln_apply(p, ret[:], ret[:], lns128, [tm, tr])
                    tsg = p.dma("gpsimd", sgt[:],
                                io["sg"][r0:r0 + 128, h * 128:(h + 1) * 128],
                                after=[last3.get("sgt")])
                    tob = p.op("vector", (lambda e, o=obf[:], a=ret[:], b=sgt[:]:
                               e.tensor_mul(o, a, b)), after=[ta, tsg, last3.get("obf")])
                    last3["sgt"] = tob
                    last3["ret"] = tob
                    tst = trp.tr(obf[:], stage3[:, sub * 128:(sub + 1) * 128],
                                 after=[tob, last3.get("st3")])
                    last3["obf"] = tst
                    last3["st3d"] = tst
                tdm = p.dma("sync", io["ot"][h, :, sc * 512:(sc + 1) * 512], stage3[:],
                            after=[last3["st3d"]])
                last3["st3"] = tdm
        p.barrier()

    with ExitStack() as sC:
        wo = sbuf(nc, sC, "a1_wo", [128, 16 * D], BF16)
        t_wo = []
        for jc in range(16):
            t_wo.append(p.dma("gpsimd", wo[:, jc * D:(jc + 1) * D],
                              io["Wo"][jc * 128:(jc + 1) * 128, :]))
        otb = sbuf(nc, sC, "a1_otb", [128, 16 * 512], BF16)
        xb_ = sbuf(nc, sC, "a1_xb", [128, D])
        ya_ = sbuf(nc, sC, "a1_ya", [128, D])
        yb_ = sbuf(nc, sC, "a1_yb", [128, D])
        at2 = [psum(nc, sC, f"a1_at2{i}", [128, 512]) for i in range(2)]
        bo_bc, t_bo = bcast_vec(p, nc, sC, cm, "a1_bo", io["bo"], D, at2)
        p.barrier()
        last4 = {}
        for tb in range(8):
            tob_ = []
            for jc in range(16):
                tob_.append(p.dma("sync", otb[:, jc * 512:(jc + 1) * 512],
                                  io["ot"][jc, :, tb * 512:(tb + 1) * 512],
                                  after=[last4.get("otb")]))
            for tsub in range(4):
                r0 = tb * 512 + tsub * 128
                mmt = [None, None]
                for oc in range(2):
                    for jc in range(16):
                        mmt[oc] = p.op("tensor", (lambda e, o=at2[oc][:],
                                       l=otb[:, jc * 512 + tsub * 128: jc * 512 + tsub * 128 + 128],
                                       r=wo[:, jc * D + oc * 512: jc * D + (oc + 1) * 512],
                                       s=(jc == 0), st=(jc == 15):
                                       e.matmul(o, l, r, start=s, stop=st)),
                                       after=[tob_[-1], t_wo[jc], last4.get(("ev", oc))])
                tev = []
                for oc in range(2):
                    t1 = p.op("vector", (lambda e, o=ya_[:, oc * 512:(oc + 1) * 512],
                              i=at2[oc][:], b=bo_bc[:, oc * 512:(oc + 1) * 512]:
                              e.scalar_tensor_tensor(out=o, in0=i, scalar=1.0, in1=b,
                                  op0=ALU.mult, op1=ALU.add)),
                              after=[mmt[oc], t_bo, last4.get("ya")])
                    last4[("ev", oc)] = t1
                    tev.append(t1)
                last4["ya"] = tev[-1]
                tx = p.dma("sync", xb_[:], io["x"][r0:r0 + 128, :], after=[last4.get("xb")])
                tm, tr = ln_stats(p, ya_[:], lns, after=tev)
                ta = ln_apply(p, yb_[:], ya_[:], lns, [tm, tr, last4.get("x3d")])
                tx3 = p.op("vector", (lambda e, o=yb_[:], a=yb_[:], b=xb_[:]:
                           e.tensor_add(o, a, b)), after=[ta, tx])
                last4["xb"] = tx3
                last4["x3d"] = p.dma("sync", io["x3"][r0:r0 + 128, :], yb_[:],
                                     after=[tx3])
            last4["otb"] = mmt[1]
        p.barrier()


def build_a1(debug=False):
    nc = bass.Bass("TRN2", target_bir_lowering=False)
    io = {
        "x": _din(nc, "x", [T, D]),
        "Wq": _din(nc, "Wq", [D, D]),
        "bq": _din(nc, "bq", [D]),
        "Wk": _din(nc, "Wk", [D, D]),
        "bk": _din(nc, "bk", [D]),
        "Wv": _din(nc, "Wv", [D, 2 * D]),
        "bv": _din(nc, "bv", [2 * D]),
        "Wg": _din(nc, "Wg", [D, 2 * D]),
        "bg": _din(nc, "bg", [2 * D]),
        "Wo": _din(nc, "Wo", [2 * D, D]),
        "bo": _din(nc, "bo", [D]),
        "iden": _din(nc, "iden", [128, 128], BF16),
        "x3": _dout(nc, "x3", [T, D]),
        "xwt": _dtmp(nc, "a1xwt", [ND, 128, T], BF16),
        "vr": _dtmp(nc, "a1vr", [T, 16 * 129], BF16),
        "sg": _dtmp(nc, "a1sg", [T, 2 * D], BF16),
        "ot": _dtmp(nc, "a1ot", [16, 128, T], BF16),
    }
    with ExitStack() as ctx:
        p = Prog(nc, ctx)
        cm = Common(p, nc, ctx, io)
        emit_a1(p, nc, ctx, cm, io)
        p.barrier()
        if debug:
            for nm, shp, dt in (("vr", [T, 16 * 129], BF16), ("sg", [T, 2 * D], BF16),
                                 ("ot", [16, 128, T], BF16), ("xwt", [ND, 128, T], BF16)):
                dbg = _dout(nc, "dbg_" + nm, shp, dt)
                p.dma("sync", dbg, io[nm])
            dbq = _dout(nc, "dbg_qrT", [128, 8 * T], BF16)
            dbk = _dout(nc, "dbg_krT", [128, 8 * T], BF16)
            p.dma("sync", dbq, io["_qrT"][:])
            p.dma("sync", dbk, io["_krT"][:])
            p.barrier()
        p.replay()
    return nc


def _a1_inmaps(inputs, x2_shards, cores):
    m = []
    for j, c in enumerate(cores):
        m.append({
            "x": np.ascontiguousarray(x2_shards[j]),
            "Wq": inputs["L1_attn_Wq"], "bq": inputs["L1_attn_bq"],
            "Wk": inputs["L1_attn_Wk"], "bk": inputs["L1_attn_bk"],
            "Wv": inputs["L1_attn_Wv"], "bv": inputs["L1_attn_bv"],
            "Wg": inputs["L1_attn_Wg"], "bg": inputs["L1_attn_bg"],
            "Wo": inputs["L1_attn_Wo"], "bo": inputs["L1_attn_bo"],
            "iden": IDEN,
        })
    return m


def emit_f1(p, nc, ctx, cm, io, debug=False):
    trp = TrPool(p, nc, ctx, cm, "f1trp")
    lns = LNS(nc, ctx, "f1_lns")

    with ExitStack() as s1:
        nw_ps = [psum(nc, s1, f"f1_nwps{i}", [128, 512]) for i in range(2)]
        nw_bc, t_nw = bcast_vec(p, nc, s1, cm, "f1_nw", io["norm_weight"], D, nw_ps)
        x_t = [sbuf(nc, s1, f"f1_x{i}", [128, D]) for i in range(2)]
        w_t = [sbuf(nc, s1, f"f1_w{i}", [128, D]) for i in range(2)]
        wbf = [sbuf(nc, s1, f"f1_wbf{i}", [128, D], BF16) for i in range(2)]
        stage1 = sbuf(nc, s1, "f1_stage1", [128, ND * 512], BF16)
        lastu = {}
        last_st1 = None
        for tb in range(8):
            tstg = []
            for tsub in range(4):
                ti = tb * 4 + tsub
                sl = ti % 2
                r0 = ti * 128
                tx = p.dma("sync", x_t[sl][:], io["x"][r0:r0 + 128, :],
                           after=[lastu.get(("x", sl))])
                tm, tr = ln_stats(p, x_t[sl][:], lns, after=[tx])
                ts = ln_apply(p, w_t[sl][:], x_t[sl][:], lns, [tm, tr])
                lastu[("x", sl)] = ts
                tm2, tr2 = ln_stats(p, w_t[sl][:], lns, after=[ts],
                                    eps=1e-8, mean_zero=True)
                ta = p.op("vector", (lambda e, o=w_t[sl][:], i=w_t[sl][:],
                          r=lns.rstd[:]: e.tensor_scalar_mul(o, i, r)),
                          after=[tr2, lastu.get(("wbf", sl))])
                tg = p.op("vector", (lambda e, o=w_t[sl][:], a=w_t[sl][:], b=nw_bc[:]:
                          e.tensor_mul(o, a, b)), after=[ta, t_nw])
                tbf = p.op("scalar", (lambda e, o=wbf[sl][:], i=w_t[sl][:]:
                                      e.copy(o, i)), after=[tg, lastu.get(("bf", sl))])
                lastu[("wbf", sl)] = tbf
                for ic in range(ND):
                    tstg.append(trp.tr(wbf[sl][:, ic * 128:(ic + 1) * 128],
                                       stage1[:, ic * 512 + tsub * 128: ic * 512 + tsub * 128 + 128],
                                       after=[tbf, last_st1]))
                lastu[("bf", sl)] = tstg[-1]
            dmas = []
            for ic in range(ND):
                dmas.append(p.dma("sync", io["nt"][ic, :, tb * 512:(tb + 1) * 512],
                                  stage1[:, ic * 512:(ic + 1) * 512], after=[tstg[-1]]))
            last_st1 = dmas[-1]
        p.barrier()

    combs = sbuf(nc, ctx, "f1_combs", [128, NT * 8])
    io["_combs"] = combs
    with ExitStack() as s2:
        rwT = sbuf(nc, s2, "f1_rwT", [128, ND * 8])
        rwbf = sbuf(nc, s2, "f1_rwbf", [128, ND * 8], BF16)
        t_rw = []
        for ic in range(ND):
            for ex in range(8):
                t_rw.append(load_col(p, rwT[:, ic * 8 + ex: ic * 8 + ex + 1],
                                     io["router_w"][ex, :], ic * 128, 128))
        t_rwb = p.op("vector", (lambda e, o=rwbf[:], i=rwT[:]: e.tensor_copy(o, i)),
                     after=[t_rw[-1]])
        xlt_in = sbuf(nc, s2, "f1_xlt2", [128, ND * 128], BF16)
        probs = sbuf(nc, s2, "f1_probs", [128, 8])
        gmax = sbuf(nc, s2, "f1_gmax", [128, 8])
        gidx = sbuf(nc, s2, "f1_gidx", [128, 8], U32)
        gidxf = sbuf(nc, s2, "f1_gidxf", [128, 8])
        iota8i = sbuf(nc, s2, "f1_iota8i", [128, 8], I32)
        iota8 = sbuf(nc, s2, "f1_iota8", [128, 8])
        wsum = sbuf(nc, s2, "f1_wsum", [128, 1])
        wn = sbuf(nc, s2, "f1_wn", [128, 2])
        eq2 = sbuf(nc, s2, "f1_eq2", [128, 8])
        gps = psum(nc, s2, "f1_gps", [128, 512])
        t_io = p.op("gpsimd", (lambda e, o=iota8i[:]:
                    e.iota(o, [[1, 8]], channel_multiplier=0)))
        t_iof = p.op("vector", (lambda e, o=iota8[:], i=iota8i[:]:
                     e.tensor_copy(o, i)), after=[t_io])
        last_xlt = None
        last_gps = None
        for ti in range(NT):
            r0 = ti * 128
            txl = []
            for ic in range(ND):
                txl.append(p.dma("sync", xlt_in[:, ic * 128:(ic + 1) * 128],
                                 io["nt"][ic, :, r0:r0 + 128], after=[last_xlt]))
            mmt = None
            for ic in range(ND):
                mmt = p.op("tensor", (lambda e, o=gps[:, 0:8],
                           l=xlt_in[:, ic * 128:(ic + 1) * 128],
                           r=rwbf[:, ic * 8:(ic + 1) * 8],
                           s=(ic == 0), st=(ic == ND - 1):
                           e.matmul(o, l, r, start=s, stop=st)),
                           after=[txl[-1], t_rwb, last_gps])
            t_pr = p.op("scalar", (lambda e, o=probs[:], i=gps[:, 0:8]:
                        e.activation(o, i, AF.Sigmoid)), after=[mmt])
            last_gps = t_pr
            t_mx = p.op("vector", (lambda e, o=gmax[:], i=probs[:]: e.max(o, i)),
                        after=[t_pr])
            t_ix = p.op("vector", (lambda e, o=gidx[:], m=gmax[:], i=probs[:]:
                        e.max_index(o, m, i)), after=[t_mx])
            t_if = p.op("vector", (lambda e, o=gidxf[:], i=gidx[:]:
                        e.tensor_copy(o, i)), after=[t_ix])
            t_ws = p.op("vector", (lambda e, o=wsum[:], i=gmax[:, 0:2]:
                        e.reduce_sum(o, i, axis=AX.X)), after=[t_mx])
            t_w2 = p.op("vector", (lambda e, o=wsum[:], i=wsum[:]:
                        e.tensor_scalar_add(o, i, 1e-6)), after=[t_ws])
            t_w3 = p.op("vector", (lambda e, o=wsum[:], i=wsum[:]:
                        e.reciprocal(o, i)), after=[t_w2])
            t_w4 = p.op("vector", (lambda e, o=wn[:], i=gmax[:, 0:2], s=wsum[:]:
                        e.tensor_scalar_mul(o, i, s)), after=[t_w3])
            t_c1 = p.op("vector", (lambda e, o=combs[:, ti * 8:(ti + 1) * 8],
                        i=iota8[:], s1=gidxf[:, 0:1], s2=wn[:, 0:1]:
                        e.tensor_scalar(o, i, s1, s2, ALU.is_equal, ALU.mult)),
                        after=[t_if, t_w4, t_iof])
            t_c2 = p.op("vector", (lambda e, o=eq2[:],
                        i=iota8[:], s1=gidxf[:, 1:2], s2=wn[:, 1:2]:
                        e.tensor_scalar(o, i, s1, s2, ALU.is_equal, ALU.mult)),
                        after=[t_c1])
            t_c3 = p.op("vector", (lambda e, o=combs[:, ti * 8:(ti + 1) * 8],
                        a=combs[:, ti * 8:(ti + 1) * 8], b=eq2[:]:
                        e.tensor_add(o, a, b)), after=[t_c2])
            last_xlt = t_c3
        p.barrier()

    with ExitStack() as s3:
        sw1 = sbuf(nc, s3, "f1_sw1", [128, ND * 4096], BF16)
        sw2 = sbuf(nc, s3, "f1_sw2", [128, 32 * D], BF16)
        t_w1, t_w2_ = [], []
        for ic in range(ND):
            t_w1.append(p.dma("gpsimd", sw1[:, ic * 4096:(ic + 1) * 4096],
                              io["sw1"][ic * 128:(ic + 1) * 128, :]))
        for jc in range(32):
            t_w2_.append(p.dma("gpsimd", sw2[:, jc * D:(jc + 1) * D],
                               io["sw2"][jc * 128:(jc + 1) * 128, :]))
        sb1c = sbuf(nc, s3, "f1_sb1c", [128, 32])
        t_b1 = [load_col(p, sb1c[:, j:j + 1], io["sb1"], j * 128, 128)
                for j in range(32)]
        hT = sbuf(nc, s3, "f1_hT", [128, 32 * 512], BF16)
        xlt3 = sbuf(nc, s3, "f1_xlt3", [128, ND * 512], BF16)
        yout = sbuf(nc, s3, "f1_yout", [128, D])
        hps = [psum(nc, s3, f"f1_hps{i}", [128, 512]) for i in range(2)]
        yps = [psum(nc, s3, f"f1_yps{i}", [128, 512]) for i in range(2)]
        sb2_bc, t_b2 = bcast_vec(p, nc, s3, cm, "f1_sb2", io["sb2"], D, yps)
        last_hps = [None, None]
        last_yps = [t_b2, t_b2]
        last_xlt3 = None
        last_yout = None
        for tbk in range(8):
            txl = []
            for ic in range(ND):
                txl.append(p.dma("sync", xlt3[:, ic * 512:(ic + 1) * 512],
                                 io["nt"][ic, :, tbk * 512:(tbk + 1) * 512],
                                 after=[last_xlt3]))
            th = None
            for j in range(32):
                psl = j % 2
                mmt = None
                for ic in range(ND):
                    mmt = p.op("tensor", (lambda e, o=hps[psl][:],
                               l=sw1[:, ic * 4096 + j * 128: ic * 4096 + j * 128 + 128],
                               r=xlt3[:, ic * 512:(ic + 1) * 512],
                               s=(ic == 0), st=(ic == ND - 1):
                               e.matmul(o, l, r, start=s, stop=st)),
                               after=[txl[-1], t_w1[ic], last_hps[psl]])
                th = p.op("scalar", (lambda e, o=hT[:, j * 512:(j + 1) * 512],
                          i=hps[psl][:], b=sb1c[:, j:j + 1]:
                          e.activation(o, i, AF.Relu, bias=b)),
                          after=[mmt, t_b1[j]])
                last_hps[psl] = th
            last_xlt3 = th
            for tsub in range(4):
                r0 = tbk * 512 + tsub * 128
                mmt = [None, None]
                for oc in range(2):
                    for j in range(32):
                        mmt[oc] = p.op("tensor", (lambda e, o=yps[oc][:],
                                       l=hT[:, j * 512 + tsub * 128: j * 512 + tsub * 128 + 128],
                                       r=sw2[:, j * D + oc * 512: j * D + (oc + 1) * 512],
                                       s=(j == 0), st=(j == 31):
                                       e.matmul(o, l, r, start=s, stop=st)),
                                       after=[th, t_w2_[-1], last_yps[oc]])
                tev = []
                for oc in range(2):
                    t1 = p.op("vector", (lambda e, o=yout[:, oc * 512:(oc + 1) * 512],
                              i=yps[oc][:], b=sb2_bc[:, oc * 512:(oc + 1) * 512]:
                              e.scalar_tensor_tensor(out=o, in0=i, scalar=1.0, in1=b,
                                  op0=ALU.mult, op1=ALU.add)),
                              after=[mmt[oc], t_b2, last_yout])
                    last_yps[oc] = t1
                    tev.append(t1)
                last_yout = p.dma("sync", io["acc"][r0:r0 + 128, :], yout[:], after=tev)
        p.barrier()
        if debug:
            p.dma("sync", io["dbg_accmid"], io["acc"])
            p.barrier()

    with ExitStack() as s6:
        we1 = sbuf(nc, s6, "f1_we1", [128, ND * 4096], BF16)
        we2 = sbuf(nc, s6, "f1_we2", [128, 32 * D], BF16)
        eb1c = sbuf(nc, s6, "f1_eb1c", [128, 32])
        eb2_bc = sbuf(nc, s6, "f1_eb2bc", [128, D])
        hT = sbuf(nc, s6, "f1_hT6", [128, 32 * 512], BF16)
        xlt6 = sbuf(nc, s6, "f1_xlt6", [128, ND * 512], BF16)
        yout = sbuf(nc, s6, "f1_yout6", [128, D])
        hps = [psum(nc, s6, f"f1_hps6{i}", [128, 512]) for i in range(2)]
        yps = [psum(nc, s6, f"f1_yps6{i}", [128, 512]) for i in range(2)]
        last_we = None
        last_hps = [None, None]
        last_yps = [None, None]
        last_xlt6 = None
        last_yout = None
        t_eb2 = None
        t_eb2_reader = None
        for ex in range(8):
            t_we = []
            for ic in range(ND):
                t_we.append(p.dma("gpsimd", we1[:, ic * 4096:(ic + 1) * 4096],
                                  io["ew1"][ex, ic * 128:(ic + 1) * 128, :],
                                  after=[last_we]))
            for jc in range(32):
                t_we.append(p.dma("gpsimd", we2[:, jc * D:(jc + 1) * D],
                                  io["ew2"][ex, jc * 128:(jc + 1) * 128, :],
                                  after=[last_we]))
            t_eb1 = [load_col(p, eb1c[:, j:j + 1], io["eb1"][ex, :], j * 128, 128)
                     for j in range(32)]
            _, t_eb2 = bcast_vec(p, nc, s6, cm, f"f1_eb2_{ex}", io["eb2"][ex, :],
                                      D, yps, war=[last_yps[0], last_yps[1], t_eb2_reader],
                                      tile=eb2_bc)
            t_we_last = t_we[-1]
            for tbk in range(8):
                txl = []
                for ic in range(ND):
                    txl.append(p.dma("sync", xlt6[:, ic * 512:(ic + 1) * 512],
                                     io["nt"][ic, :, tbk * 512:(tbk + 1) * 512],
                                     after=[last_xlt6]))
                th = None
                for j in range(32):
                    psl = j % 2
                    mmt = None
                    for ic in range(ND):
                        mmt = p.op("tensor", (lambda e, o=hps[psl][:],
                                   l=we1[:, ic * 4096 + j * 128: ic * 4096 + j * 128 + 128],
                                   r=xlt6[:, ic * 512:(ic + 1) * 512],
                                   s=(ic == 0), st=(ic == ND - 1):
                                   e.matmul(o, l, r, start=s, stop=st)),
                                   after=[txl[-1], last_hps[psl], t_we_last])
                    th = p.op("scalar", (lambda e, o=hT[:, j * 512:(j + 1) * 512],
                              i=hps[psl][:], b=eb1c[:, j:j + 1]:
                              e.activation(o, i, AF.Relu, bias=b)),
                              after=[mmt, t_eb1[j]])
                    last_hps[psl] = th
                last_xlt6 = th
                for tsub in range(4):
                    ti = tbk * 4 + tsub
                    r0 = ti * 128
                    mmt = [None, None]
                    for oc in range(2):
                        for j in range(32):
                            mmt[oc] = p.op("tensor", (lambda e, o=yps[oc][:],
                                           l=hT[:, j * 512 + tsub * 128: j * 512 + tsub * 128 + 128],
                                           r=we2[:, j * D + oc * 512: j * D + (oc + 1) * 512],
                                           s=(j == 0), st=(j == 31):
                                           e.matmul(o, l, r, start=s, stop=st)),
                                           after=[th, last_yps[oc], t_eb2])
                    tev = []
                    for oc in range(2):
                        t1 = p.op("vector", (lambda e, o=yout[:, oc * 512:(oc + 1) * 512],
                                  i=yps[oc][:], b=eb2_bc[:, oc * 512:(oc + 1) * 512],
                                  m=combs[:, ti * 8 + ex: ti * 8 + ex + 1]:
                                  e.scalar_tensor_tensor(out=o, in0=i, scalar=1.0,
                                      in1=b, op0=ALU.mult, op1=ALU.add)),
                                  after=[mmt[oc], t_eb2, last_yout])
                        t2 = p.op("vector", (lambda e, o=yout[:, oc * 512:(oc + 1) * 512],
                                  i=yout[:, oc * 512:(oc + 1) * 512],
                                  m=combs[:, ti * 8 + ex: ti * 8 + ex + 1]:
                                  e.tensor_scalar_mul(o, i, m)), after=[t1])
                        last_yps[oc] = t2
                        tev.append(t2)
                        t_eb2_reader = t1
                    last_yout = p.dma("gpsimd", io["acc"][r0:r0 + 128, :], yout[:],
                                      after=tev, accum_op=ALU.add)
            last_we = last_yout
        p.barrier()

    with ExitStack() as sF:
        xa = sbuf(nc, sF, "f1_xa", [128, D])
        xb = sbuf(nc, sF, "f1_xb", [128, D])
        xc = sbuf(nc, sF, "f1_xc", [128, D])
        last7 = [None, None]
        lastst = None
        for ti in range(NT):
            r0 = ti * 128
            tu = p.dma("sync", xa[:], io["acc"][r0:r0 + 128, :], after=[last7[0]])
            tx = p.dma("sync", xb[:], io["x"][r0:r0 + 128, :], after=[last7[1]])
            tm, tr = ln_stats(p, xa[:], lns, after=[tu])
            ta = ln_apply(p, xc[:], xa[:], lns, [tm, tr, lastst])
            last7[0] = ta
            tx4 = p.op("vector", (lambda e, o=xc[:], a=xc[:], b=xb[:]:
                       e.tensor_add(o, a, b)), after=[ta, tx])
            last7[1] = tx4
            lastst = p.dma("sync", io["x4"][r0:r0 + 128, :], xc[:], after=[tx4])
        p.barrier()


def build_f1(debug=False):
    nc = bass.Bass("TRN2", target_bir_lowering=False)
    io = {
        "x": _din(nc, "x", [T, D]),
        "norm_weight": _din(nc, "norm_weight", [D]),
        "router_w": _din(nc, "router_w", [8, D]),
        "sw1": _din(nc, "sw1", [D, 4096]),
        "sb1": _din(nc, "sb1", [4096]),
        "sw2": _din(nc, "sw2", [4096, D]),
        "sb2": _din(nc, "sb2", [D]),
        "ew1": _din(nc, "ew1", [8, D, 4096]),
        "eb1": _din(nc, "eb1", [8, 4096]),
        "ew2": _din(nc, "ew2", [8, 4096, D]),
        "eb2": _din(nc, "eb2", [8, D]),
        "iden": _din(nc, "iden", [128, 128], BF16),
        "x4": _dout(nc, "x4", [T, D]),
        "nt": _dtmp(nc, "f1nt", [ND, 128, T], BF16),
        "acc": _dtmp(nc, "f1acc", [T, D]),
    }
    with ExitStack() as ctx:
        p = Prog(nc, ctx)
        cm = Common(p, nc, ctx, io)
        if debug:
            io["dbg_accmid"] = _dout(nc, "dbg_accmid", [T, D])
        emit_f1(p, nc, ctx, cm, io, debug=debug)
        p.barrier()
        if debug:
            for nm, shp, dt in (("nt", [ND, 128, T], BF16), ("acc", [T, D], F32)):
                dbg = _dout(nc, "dbg_" + nm, shp, dt)
                p.dma("sync", dbg, io[nm])
            dbgc = _dout(nc, "dbg_combs", [128, NT * 8])
            p.dma("sync", dbgc, io["_combs"][:])
            p.barrier()
        p.replay()
    return nc


def _f1_inmaps(inputs, x3_shards, cores):
    m = []
    for j, c in enumerate(cores):
        m.append({
            "x": np.ascontiguousarray(x3_shards[j]),
            "norm_weight": inputs["L1_ffn_norm_weight"],
            "router_w": inputs["L1_ffn_router_w"],
            "sw1": inputs["L1_ffn_sw1"], "sb1": inputs["L1_ffn_sb1"],
            "sw2": inputs["L1_ffn_sw2"], "sb2": inputs["L1_ffn_sb2"],
            "ew1": inputs["L1_ffn_ew1"], "eb1": inputs["L1_ffn_eb1"],
            "ew2": inputs["L1_ffn_ew2"], "eb2": inputs["L1_ffn_eb2"],
            "iden": IDEN,
        })
    return m


def build_fused():
    nc = bass.Bass("TRN2", target_bir_lowering=False)
    io = {
        "x": _din(nc, "x", [T, D]),
        "iden": _din(nc, "iden", [128, 128], BF16),
        "x4": _dout(nc, "x4", [T, D]),
        "in_proj_w": _din(nc, "in_proj_w", [3 * D, D]),
        "in_proj_b": _din(nc, "in_proj_b", [3 * D]),
        "out_w": _din(nc, "out_w", [D, D]),
        "out_b": _din(nc, "out_b", [D]),
        "lin1_w": _din(nc, "lin1_w", [4096, D]),
        "lin1_b": _din(nc, "lin1_b", [4096]),
        "lin2_w": _din(nc, "lin2_w", [D, 4096]),
        "lin2_b": _din(nc, "lin2_b", [D]),
        "a0_ln1_g": _din(nc, "a0_ln1_g", [D]),
        "a0_ln1_b": _din(nc, "a0_ln1_b", [D]),
        "a0_ln2_g": _din(nc, "a0_ln2_g", [D]),
        "a0_ln2_b": _din(nc, "a0_ln2_b", [D]),
        "w_qkv": _din(nc, "w_qkv", [D, 3 * D]),
        "w_proj": _din(nc, "w_proj", [D, D]),
        "f0_ln1_w": _din(nc, "f0_ln1_w", [D]),
        "f0_ln1_b": _din(nc, "f0_ln1_b", [D]),
        "f0_ln2_w": _din(nc, "f0_ln2_w", [D]),
        "f0_ln2_b": _din(nc, "f0_ln2_b", [D]),
        "gate_w": _din(nc, "gate_w", [D, 8]),
        "gate_b": _din(nc, "gate_b", [8]),
        "w1": _din(nc, "w1", [8, D, 4096]),
        "w2": _din(nc, "w2", [8, 4096, D]),
        "Wq": _din(nc, "Wq", [D, D]), "bq": _din(nc, "bq", [D]),
        "Wk": _din(nc, "Wk", [D, D]), "bk": _din(nc, "bk", [D]),
        "Wv": _din(nc, "Wv", [D, 2 * D]), "bv": _din(nc, "bv", [2 * D]),
        "Wg": _din(nc, "Wg", [D, 2 * D]), "bg": _din(nc, "bg", [2 * D]),
        "Wo": _din(nc, "Wo", [2 * D, D]), "bo": _din(nc, "bo", [D]),
        "norm_weight": _din(nc, "norm_weight", [D]),
        "router_w": _din(nc, "router_w", [8, D]),
        "sw1": _din(nc, "sw1", [D, 4096]), "sb1": _din(nc, "sb1", [4096]),
        "sw2": _din(nc, "sw2", [4096, D]), "sb2": _din(nc, "sb2", [D]),
        "ew1": _din(nc, "ew1", [8, D, 4096]), "eb1": _din(nc, "eb1", [8, 4096]),
        "ew2": _din(nc, "ew2", [8, 4096, D]), "eb2": _din(nc, "eb2", [8, D]),
        "x1": _dtmp(nc, "x1t", [T, D]),
        "x2": _dtmp(nc, "x2t", [T, D]),
        "x3": _dtmp(nc, "x3t", [T, D]),
        "a0src": _dtmp(nc, "a0src", [T, D]),
        "a0postt": _dtmp(nc, "a0postt", [D, T], BF16),
        "a0xln1": _dtmp(nc, "a0xln1", [T, D]),
        "a0xln1t": _dtmp(nc, "a0xln1t", [ND, 128, T], BF16),
        "a0u2": _dtmp(nc, "a0u2", [T, D]),
        "f0xs": _dtmp(nc, "f0xs", [T, D]),
        "f0postt": _dtmp(nc, "f0postt", [D, T], BF16),
        "f0u2m": _dtmp(nc, "f0u2m", [T, D]),
        "f0xln2t": _dtmp(nc, "f0xln2t", [ND, 128, T], BF16),
        "f0moe": _dtmp(nc, "f0moe", [T, D]),
        "a1xwt": _dtmp(nc, "a1xwt", [ND, 128, T], BF16),
        "a1vr": _dtmp(nc, "a1vr", [T, 16 * 129], BF16),
        "a1sg": _dtmp(nc, "a1sg", [T, 2 * D], BF16),
        "a1ot": _dtmp(nc, "a1ot", [16, 128, T], BF16),
        "f1nt": _dtmp(nc, "f1nt", [ND, 128, T], BF16),
        "f1acc": _dtmp(nc, "f1acc", [T, D]),
    }
    with ExitStack() as ctx:
        p = Prog(nc, ctx)
        cm = Common(p, nc, ctx, io)
        io_a0 = {k: io[v] for k, v in dict(
            x="x", iden="iden", in_proj_w="in_proj_w", in_proj_b="in_proj_b",
            out_w="out_w", out_b="out_b", lin1_w="lin1_w", lin1_b="lin1_b",
            lin2_w="lin2_w", lin2_b="lin2_b", ln1_g="a0_ln1_g", ln1_b="a0_ln1_b",
            ln2_g="a0_ln2_g", ln2_b="a0_ln2_b", x1="x1", src="a0src",
            postt="a0postt", xln1="a0xln1", xln1t="a0xln1t", u2="a0u2").items()}
        with ExitStack() as pc:
            emit_a0(p, nc, pc, cm, io_a0)
        p.barrier()
        io_f0 = {k: io[v] for k, v in dict(
            x="x1", iden="iden", w_qkv="w_qkv", w_proj="w_proj",
            ln1_w="f0_ln1_w", ln1_b="f0_ln1_b", ln2_w="f0_ln2_w", ln2_b="f0_ln2_b",
            gate_w="gate_w", gate_b="gate_b", w1="w1", w2="w2", x2="x2",
            xs="f0xs", postt="f0postt", u2m="f0u2m", xln2t="f0xln2t",
            moe="f0moe").items()}
        with ExitStack() as pc:
            emit_f0(p, nc, pc, cm, io_f0)
        p.barrier()
        io_a1 = {k: io[v] for k, v in dict(
            x="x2", iden="iden", Wq="Wq", bq="bq", Wk="Wk", bk="bk", Wv="Wv",
            bv="bv", Wg="Wg", bg="bg", Wo="Wo", bo="bo", x3="x3",
            xwt="a1xwt", vr="a1vr", sg="a1sg", ot="a1ot").items()}
        with ExitStack() as pc:
            emit_a1(p, nc, pc, cm, io_a1)
        p.barrier()
        io_f1 = {k: io[v] for k, v in dict(
            x="x3", iden="iden", norm_weight="norm_weight", router_w="router_w",
            sw1="sw1", sb1="sb1", sw2="sw2", sb2="sb2", ew1="ew1", eb1="eb1",
            ew2="ew2", eb2="eb2", x4="x4", nt="f1nt", acc="f1acc").items()}
        with ExitStack() as pc:
            emit_f1(p, nc, pc, cm, io_f1)
        p.barrier()
        p.replay()
    return nc


def _fused_inmaps(inputs, cores):
    m = []
    for c in cores:
        m.append({
            "x": np.ascontiguousarray(inputs["L0_attn_src"][c]),
            "iden": IDEN,
            "in_proj_w": inputs["L0_attn_in_proj_w"],
            "in_proj_b": inputs["L0_attn_in_proj_b"],
            "out_w": inputs["L0_attn_out_w"], "out_b": inputs["L0_attn_out_b"],
            "lin1_w": inputs["L0_attn_lin1_w"], "lin1_b": inputs["L0_attn_lin1_b"],
            "lin2_w": inputs["L0_attn_lin2_w"], "lin2_b": inputs["L0_attn_lin2_b"],
            "a0_ln1_g": inputs["L0_attn_ln1_g"], "a0_ln1_b": inputs["L0_attn_ln1_b"],
            "a0_ln2_g": inputs["L0_attn_ln2_g"], "a0_ln2_b": inputs["L0_attn_ln2_b"],
            "w_qkv": inputs["L0_ffn_w_qkv"], "w_proj": inputs["L0_ffn_w_proj"],
            "f0_ln1_w": inputs["L0_ffn_ln1_w"], "f0_ln1_b": inputs["L0_ffn_ln1_b"],
            "f0_ln2_w": inputs["L0_ffn_ln2_w"], "f0_ln2_b": inputs["L0_ffn_ln2_b"],
            "gate_w": inputs["L0_ffn_gate_w"], "gate_b": inputs["L0_ffn_gate_b"],
            "w1": inputs["L0_ffn_w1"], "w2": inputs["L0_ffn_w2"],
            "Wq": inputs["L1_attn_Wq"], "bq": inputs["L1_attn_bq"],
            "Wk": inputs["L1_attn_Wk"], "bk": inputs["L1_attn_bk"],
            "Wv": inputs["L1_attn_Wv"], "bv": inputs["L1_attn_bv"],
            "Wg": inputs["L1_attn_Wg"], "bg": inputs["L1_attn_bg"],
            "Wo": inputs["L1_attn_Wo"], "bo": inputs["L1_attn_bo"],
            "norm_weight": inputs["L1_ffn_norm_weight"],
            "router_w": inputs["L1_ffn_router_w"],
            "sw1": inputs["L1_ffn_sw1"], "sb1": inputs["L1_ffn_sb1"],
            "sw2": inputs["L1_ffn_sw2"], "sb2": inputs["L1_ffn_sb2"],
            "ew1": inputs["L1_ffn_ew1"], "eb1": inputs["L1_ffn_eb1"],
            "ew2": inputs["L1_ffn_ew2"], "eb2": inputs["L1_ffn_eb2"],
        })
    return m


def kernel_fused(**inputs):
    cores = list(range(8))
    nc = build_fused()
    res = run_bass_kernel_spmd(nc, _fused_inmaps(inputs, cores), core_ids=cores)
    return np.stack([res.results[i]["x4"] for i in range(len(cores))])


# version 09
# speedup vs baseline: 1.2322x; 1.1164x over previous
import numpy as np
import ml_dtypes
from contextlib import ExitStack

import concourse.bass as bass
import concourse.mybir as mybir
from concourse.bass_utils import run_bass_kernel_spmd

F32 = mybir.dt.float32
BF16 = mybir.dt.bfloat16
I32 = mybir.dt.int32
U32 = mybir.dt.uint32
AF = mybir.ActivationFunctionType
ALU = mybir.AluOpType
AX = mybir.AxisListType

T = 4096
D = 1024
NT = T // 128
ND = D // 128
LN_EPS = 1e-5

ENGS = ("sync", "gpsimd", "tensor", "vector", "scalar")


class Prog:

    def __init__(self, nc, ctx):
        self.nc = nc
        self.ctx = ctx
        self.items = {e: [] for e in ENGS}
        self.sems = {}
        self.counts = {}
        self.waited = {e: {} for e in ENGS}
        for e in ENGS:
            self._sem("s_" + e)
        self.dma_idx = {}
        for e in ("sync", "gpsimd", "scalar", "vector"):
            for i in range(8):
                self._sem(f"d_{e}{i}")
            self.dma_idx[e] = 0

    def _sem(self, name):
        self.sems[name] = self.ctx.enter_context(self.nc.semaphore(name))
        self.counts[name] = 0

    def wait(self, eng, semname, val):
        if val <= 0:
            return
        w = self.waited[eng]
        if w.get(semname, 0) >= val:
            return
        w[semname] = val
        self.items[eng].append(("w", semname, val))

    def after(self, eng, tags):
        for t in tags:
            if t is None:
                continue
            self.wait(eng, t[0], t[1])

    def op(self, eng, fn, after=()):
        self.after(eng, after)
        sname = "s_" + eng
        self.counts[sname] += 1
        self.items[eng].append(("o", fn, [(sname, 1)]))
        return (sname, self.counts[sname])

    def dma(self, eng, out, in_, after=(), **kw):
        self.after(eng, after)
        idx = self.dma_idx[eng]
        self.dma_idx[eng] = (idx + 1) % 8
        sname = f"d_{eng}{idx}"
        self.counts[sname] += 16
        fn = (lambda e, o=out, i=in_, kw=kw: e.dma_start(out=o, in_=i, **kw))
        self.items[eng].append(("o", fn, [(sname, 16)]))
        return (sname, self.counts[sname])

    def barrier(self):
        snap = [(s, v) for s, v in self.counts.items() if v > 0]
        for e in ENGS:
            for s, v in snap:
                self.wait(e, s, v)

    def replay(self):
        nc = self.nc
        nc.all_engine_barrier()
        ksr = nc._kernel_sem_range
        start = ksr.start + (3 if nc._bir_kernel_barrier_sem is not None else 2)
        start += len(nc._monotonic_sems)
        rng = range(start, ksr.stop)
        nc.gpsimd.dma_reset(rng)
        nc.gpsimd.sem_clear(rng)
        nc.all_engine_barrier()
        with nc.Block() as block:
            def mk(e):
                def emit(eng):
                    for it in self.items[e]:
                        if it[0] == "w":
                            eng.wait_ge(self.sems[it[1]], it[2])
                        else:
                            inst = it[1](eng)
                            for s, n in it[2]:
                                inst.then_inc(self.sems[s], n)
                return emit
            block.sync(mk("sync"))
            block.gpsimd(mk("gpsimd"))
            block.tensor(mk("tensor"))
            block.vector(mk("vector"))
            block.scalar(mk("scalar"))



def sbuf(nc, ctx, name, shape, dtype=F32):
    return ctx.enter_context(nc.sbuf_tensor(name, shape, dtype))


def psum(nc, ctx, name, shape, dtype=F32):
    return ctx.enter_context(nc.psum_tensor(name, shape, dtype))


def load_col(p, dst_col, vec_hbm, off, n):
    return p.dma("scalar", dst_col,
                 vec_hbm[off:off + n].rearrange("(p one) -> p one", one=1))


class Common:

    def __init__(self, p, nc, ctx, io):
        self.iden = sbuf(nc, ctx, "c_iden", [128, 128], BF16)
        self.t_iden = p.dma("sync", self.iden[:], io["iden"])
        self.ones = sbuf(nc, ctx, "c_ones", [1, 128], F32)
        self.t_ones = p.op("vector", (lambda e, o=self.ones[:]: e.memset(o, 1.0)))
        self.row = sbuf(nc, ctx, "c_row", [1, 1024], F32)
        self.row_last = None


def bcast_vec(p, nc, ctx, cm, name, vec_hbm, n, ps_pool, war=(), tile=None,
              dtype=F32):
    row = cm.row
    if tile is None:
        tile = sbuf(nc, ctx, name + "_bc", [128, n], dtype)
    t_row = p.dma("scalar", row[0:1, 0:n],
                  vec_hbm[0:n].rearrange("(one n) -> one n", one=1),
                  after=[cm.row_last])
    last = [None] * len(ps_pool)
    tc = None
    for c0 in range(0, n, 512):
        c1 = min(c0 + 512, n)
        idx = (c0 // 512) % len(ps_pool)
        ps = ps_pool[idx]
        tm = p.op("tensor", (lambda e, o=ps[:, 0:c1 - c0], l=cm.ones[:],
                             r=row[0:1, c0:c1]: e.matmul(o, l, r, start=True, stop=True)),
                  after=[t_row, cm.t_ones, last[idx]] + list(war))
        tc = p.op("vector", (lambda e, o=tile[:, c0:c1], i=ps[:, 0:c1 - c0]:
                             e.tensor_copy(o, i)), after=[tm])
        last[idx] = tc
    cm.row_last = tc
    return tile, tc


class LNS:

    def __init__(self, nc, ctx, name, width=D):
        self.width = width
        self.ssum = sbuf(nc, ctx, name + "_ssum", [128, 1])
        self.sqacc = sbuf(nc, ctx, name + "_sqacc", [128, 1])
        self.mean = sbuf(nc, ctx, name + "_mean", [128, 1])
        self.var = sbuf(nc, ctx, name + "_var", [128, 1])
        self.m2 = sbuf(nc, ctx, name + "_m2", [128, 1])
        self.sd = sbuf(nc, ctx, name + "_sd", [128, 1])
        self.rstd = sbuf(nc, ctx, name + "_rstd", [128, 1])
        self.sq = sbuf(nc, ctx, name + "_sq", [128, width])


def ln_stats(p, x_ap, w, after=(), eps=LN_EPS, mean_zero=False):
    width = w.width
    t_sq = p.op("scalar", (lambda e, o=w.sq[:, 0:width], i=x_ap, a=w.sqacc[:]:
                           e.activation(o, i, AF.Square, accum_out=a)), after=after)
    t_v1 = p.op("vector", (lambda e, o=w.var[:], i=w.sqacc[:]:
                           e.tensor_scalar_mul(o, i, 1.0 / width)), after=[t_sq])
    if mean_zero:
        t_v3 = t_v1
        t_mean = None
    else:
        t_sum = p.op("vector", (lambda e, o=w.ssum[:], i=x_ap:
                                e.reduce_sum(o, i, axis=AX.X)), after=after)
        t_mean = p.op("vector", (lambda e, o=w.mean[:], i=w.ssum[:]:
                                 e.tensor_scalar_mul(o, i, 1.0 / width)), after=[t_sum])
        t_v2 = p.op("vector", (lambda e, o=w.m2[:], i=w.mean[:]:
                               e.tensor_tensor(o, i, i, ALU.mult)), after=[t_mean])
        t_v3 = p.op("vector", (lambda e, o=w.var[:], a=w.var[:], b=w.m2[:]:
                               e.tensor_sub(o, a, b)), after=[t_v1, t_v2])
    t_v4 = p.op("vector", (lambda e, o=w.var[:], i=w.var[:], c=float(eps):
                           e.tensor_scalar_add(o, i, c)), after=[t_v3])
    t_v5 = p.op("scalar", (lambda e, o=w.sd[:], i=w.var[:]:
                           e.activation(o, i, AF.Sqrt)), after=[t_v4])
    t_rstd = p.op("vector", (lambda e, o=w.rstd[:], i=w.sd[:]:
                             e.reciprocal(o, i)), after=[t_v5])
    return t_mean, t_rstd


def ln_apply(p, out_ap, x_ap, w, tags):
    return p.op("vector", (lambda e, o=out_ap, i=x_ap, m=w.mean[:], r=w.rstd[:]:
                           e.tensor_scalar(o, i, m, r, ALU.subtract, ALU.mult)),
                after=tags)


class WTrans:

    def __init__(self, p, nc, ctx, cm, name, n_ps=2):
        self.p, self.nc, self.cm = p, nc, cm
        self.raw = [sbuf(nc, ctx, f"{name}_raw{i}", [128, D], BF16) for i in range(2)]
        self.tr = [psum(nc, ctx, f"{name}_tr{i}", [128, 128], BF16) for i in range(n_ps)]
        self.n_ps = n_ps
        self.last_raw = [None, None]
        self.last_tr = [None] * n_ps
        self.k = 0

    def slab(self, w_hbm_rows, in_width, dest_cols):
        p, cm = self.p, self.cm
        sl = self.k % 2
        self.k += 1
        t_load = p.dma("gpsimd", self.raw[sl][:, 0:in_width], w_hbm_rows,
                       after=[self.last_raw[sl]])
        out_tags = []
        for ic in range(in_width // 128):
            psl = (self.k * 8 + ic) % self.n_ps
            t_tr = p.op("tensor", (lambda e, o=self.tr[psl][:],
                                   i=self.raw[sl][:, ic * 128:(ic + 1) * 128],
                                   idn=cm.iden[:]: e.transpose(o, i, idn)),
                        after=[t_load, cm.t_iden, self.last_tr[psl]])
            t_cp = p.op("vector", (lambda e, o=dest_cols[ic], i=self.tr[psl][:]:
                                   e.tensor_copy(o, i)), after=[t_tr])
            self.last_tr[psl] = t_cp
            out_tags.append(t_cp)
        self.last_raw[sl] = out_tags[-1]
        return out_tags


class TrPool:

    def __init__(self, p, nc, ctx, cm, name):
        self.p, self.cm = p, cm
        self.trs = [psum(nc, ctx, f"{name}_tr{i}", [128, 128], BF16) for i in range(2)]
        self.last = [None, None]
        self.k = 0

    def tr(self, in_bf_ap, dest_ap, after=(), dest_eng="vector"):
        p, cm = self.p, self.cm
        sl = self.k % 2
        self.k += 1
        t_tr = p.op("tensor", (lambda e, o=self.trs[sl][:], i=in_bf_ap,
                               idn=cm.iden[:]: e.transpose(o, i, idn)),
                    after=list(after) + [cm.t_iden, self.last[sl]])
        t_cp = p.op(dest_eng, (lambda e, o=dest_ap, i=self.trs[sl][:]:
                               e.tensor_copy(o, i)), after=[t_tr])
        self.last[sl] = t_cp
        return t_cp


class WT2:

    def __init__(self, p, nc, ctx, trp, name, scale=1.0, raw_width=4096):
        self.p, self.trp = p, trp
        self.raw = [sbuf(nc, ctx, f"{name}_raw{i}", [128, raw_width], BF16) for i in range(2)]
        self.last_raw = [None, None]
        self.k = 0
        self.scale = scale

    def slab(self, w_hbm_rows, in_width, dest_cols):
        p = self.p
        sl = self.k % 2
        self.k += 1
        t_load = p.dma("gpsimd", self.raw[sl][:, 0:in_width], w_hbm_rows,
                       after=[self.last_raw[sl]])
        tags = []
        for ic in range(in_width // 128):
            if self.scale == 1.0:
                t = self.trp.tr(self.raw[sl][:, ic * 128:(ic + 1) * 128],
                                dest_cols[ic], after=[t_load])
            else:
                trp, sc = self.trp, self.scale
                psl = trp.k % 2
                trp.k += 1
                t_tr = p.op("tensor", (lambda e, o=trp.trs[psl][:],
                                       i=self.raw[sl][:, ic * 128:(ic + 1) * 128],
                                       idn=trp.cm.iden[:]: e.transpose(o, i, idn)),
                            after=[t_load, trp.cm.t_iden, trp.last[psl]])
                t = p.op("scalar", (lambda e, o=dest_cols[ic], i=trp.trs[psl][:],
                                    s=sc: e.mul(o, i, s)), after=[t_tr])
                trp.last[psl] = t
            tags.append(t)
        self.last_raw[sl] = tags[-1]
        return tags


def emit_a0(p, nc, ctx, cm, io):
    H, DH = 16, 64
    trp = TrPool(p, nc, ctx, cm, "a0trp")

    with ExitStack() as s_pro:
        bqk = sbuf(nc, ctx, "a0_bqk", [128, 16])
        for oc in range(16):
            load_col(p, bqk[:, oc:oc + 1], io["in_proj_b"], oc * 128, 128)
        p.barrier()
        p.op("vector", (lambda e, o=bqk[:, 8:16], i=bqk[:, 8:16]:
                        e.tensor_scalar_mul(o, i, 0.125)))
        p.barrier()

    lns = LNS(nc, ctx, "a0_lns")

    with ExitStack() as sB:
        srcT = sbuf(nc, sB, "a0_srcT", [128, ND * T], BF16)
        s1 = sB.enter_context(ExitStack())
        x_t = [sbuf(nc, s1, f"a0_x{i}", [128, D]) for i in range(2)]
        src_t = [sbuf(nc, s1, f"a0_src{i}", [128, D]) for i in range(2)]
        srcbf = [sbuf(nc, s1, f"a0_srcbf{i}", [128, D], BF16) for i in range(2)]
        lastu = {}
        for ti in range(NT):
            sl = ti % 2
            r0 = ti * 128
            tx = p.dma("sync", x_t[sl][:], io["x"][r0:r0 + 128, :],
                       after=[lastu.get(("x", sl))])
            tm, tr = ln_stats(p, x_t[sl][:], lns, after=[tx])
            ts = ln_apply(p, src_t[sl][:], x_t[sl][:], lns,
                          [tm, tr, lastu.get(("srcdma", sl)), lastu.get(("srcbf", sl))])
            lastu[("x", sl)] = ts
            lastu[("srcdma", sl)] = p.dma("sync", io["src"][r0:r0 + 128, :],
                                          src_t[sl][:], after=[ts])
            tbf = p.op("scalar", (lambda e, o=srcbf[sl][:], i=src_t[sl][:]:
                                  e.copy(o, i)), after=[ts, lastu.get(("sbf", sl))])
            lastu[("srcbf", sl)] = tbf
            tlast = None
            for ic in range(ND):
                tlast = trp.tr(srcbf[sl][:, ic * 128:(ic + 1) * 128],
                               srcT[:, ic * T + r0: ic * T + r0 + 128], after=[tbf])
            lastu[("sbf", sl)] = tlast
        p.barrier()
        s1.close()

        qT = sbuf(nc, sB, "a0_qT", [128, 2 * T], BF16)
        kT = sbuf(nc, sB, "a0_kT", [128, 2 * T], BF16)
        vg = sbuf(nc, sB, "a0_vg", [128, NT * 256], BF16)
        pt = [sbuf(nc, sB, f"a0_pt{i}", [128, T], BF16) for i in range(2)]
        gst = sbuf(nc, sB, "a0_gst", [128, T], BF16)
        lparts = sbuf(nc, sB, "a0_lparts", [128, 4])
        lsum = sbuf(nc, sB, "a0_lsum", [128, 1])
        linv = sbuf(nc, sB, "a0_linv", [128, 1])
        vp = [sbuf(nc, sB, f"a0_vp{i}", [128, 64], BF16) for i in range(2)]
        acc = sbuf(nc, sB, "a0_acc", [128, T])
        wblk = [sbuf(nc, sB, f"a0_wblk{i}", [128, D], BF16) for i in range(2)]
        zA = psum(nc, sB, "a0_zA", [128, 1024])
        zB = psum(nc, sB, "a0_zB", [128, 1024])
        avps = psum(nc, sB, "a0_avps", [128, 1024])
        wt = WT2(p, nc, sB, trp, "a0_wt", raw_width=D)
        vb_bc, t_vb = bcast_vec(p, nc, sB, cm, "a0_vb",
                                io["in_proj_b"][2 * D:3 * D], D, [zA, zB])

        mm_slot = [zA[:, 0:512], zB[:, 0:512]]
        last_slot = [t_vb, t_vb]
        last_wblk = [None, None]
        t_postdma = {}
        lp_last = None
        prev_ac_last = None
        gelu_done = None
        gst_free = None
        za_free = zb_free = None
        pt_free = [None, None]
        vp_free = [None, None]
        av_war = [None, None]

        for g in range(4):
            for part in ("q", "k", "v"):
                base_oc = {"q": 0, "k": 8, "v": 16}[part]
                for oc_loc in range(2):
                    oc = base_oc + 2 * g + oc_loc
                    wsl = oc % 2
                    dcols = [wblk[wsl][:, ic * 128:(ic + 1) * 128] for ic in range(ND)]
                    if last_wblk[wsl] is not None:
                        wt.last_raw[wt.k % 2] = last_wblk[wsl]
                    ttags = wt.slab(io["in_proj_w"][oc * 128:(oc + 1) * 128, :], D, dcols)
                    tw = ttags[-1]
                    if part in ("q", "k"):
                        dest = qT if part == "q" else kT
                        rb = oc_loc
                        mmt = None
                        for tc in range(8):
                            psl = tc % 2
                            for ic in range(ND):
                                mmt = p.op("tensor", (lambda e, o=mm_slot[psl],
                                           l=wblk[wsl][:, ic * 128:(ic + 1) * 128],
                                           r=srcT[:, ic * T + tc * 512: ic * T + (tc + 1) * 512],
                                           s=(ic == 0), st=(ic == ND - 1):
                                           e.matmul(o, l, r, start=s, stop=st)),
                                           after=[tw, last_slot[psl]])
                            tev = p.op("scalar", (lambda e,
                                       o=dest[:, rb * T + tc * 512: rb * T + (tc + 1) * 512],
                                       i=mm_slot[psl], b=bqk[:, oc:oc + 1],
                                       s=(0.125 if part == "k" else 1.0):
                                       e.activation(o, i, AF.Identity, bias=b, scale=s)),
                                       after=[mmt])
                            last_slot[psl] = tev
                            last_wblk[wsl] = tev
                    else:
                        vcol0 = oc_loc * 128
                        mmt = None
                        for ti in range(NT):
                            psl = ti % 2
                            for ic in range(ND):
                                mmt = p.op("tensor", (lambda e,
                                           o=mm_slot[psl][:, 0:128],
                                           l=srcT[:, ic * T + ti * 128: ic * T + ti * 128 + 128],
                                           r=wblk[wsl][:, ic * 128:(ic + 1) * 128],
                                           s=(ic == 0), st=(ic == ND - 1):
                                           e.matmul(o, l, r, start=s, stop=st)),
                                           after=[tw, last_slot[psl]])
                            tev = p.op("vector", (lambda e,
                                       o=vg[:, ti * 256 + vcol0: ti * 256 + vcol0 + 128],
                                       i=mm_slot[psl][:, 0:128],
                                       b=vb_bc[:, g * 256 + vcol0: g * 256 + vcol0 + 128]:
                                       e.scalar_tensor_tensor(out=o, in0=i, scalar=1.0,
                                           in1=b, op0=ALU.mult, op1=ALU.add)),
                                       after=[mmt, t_vb])
                            last_slot[psl] = tev
                            last_wblk[wsl] = tev
            for hq in range(4):
                h = 4 * g + hq
                rb = hq // 2
                pb = 64 * (hq % 2)
                zregs = [(zA, 0), (zA, 512), (zB, 0), (zB, 512),
                         (zA, 0), (zA, 512), (zB, 0), (zB, 512)]
                ac_tags = []
                pend = None
                st = {"prev": prev_ac_last}

                def flush_av(pd):
                    ti_, psl_, ev_, tvp_ = pd
                    new_ac = []
                    for tc in range(8):
                        asl = tc % 2
                        t_av = p.op("tensor", (lambda e,
                                    o=avps[0:64, asl * 512: asl * 512 + 512],
                                    l=vp[psl_][:], r=pt[psl_][:, tc * 512:(tc + 1) * 512]:
                                    e.matmul(o, l, r, start=True, stop=True)),
                                    after=[ev_, tvp_,
                                           new_ac[-2] if len(new_ac) >= 2 else av_war[asl]])
                        if ti_ == 0:
                            t_ac = p.op("vector", (lambda e,
                                        o=acc[pb:pb + 64, tc * 512:(tc + 1) * 512],
                                        i=avps[0:64, asl * 512: asl * 512 + 512]:
                                        e.tensor_copy(o, i)),
                                        after=[t_av, gelu_done, t_postdma.get(h // 2)])
                        else:
                            t_ac = p.op("vector", (lambda e,
                                        o=acc[pb:pb + 64, tc * 512:(tc + 1) * 512],
                                        i=avps[0:64, asl * 512: asl * 512 + 512]:
                                        e.scalar_tensor_tensor(out=o, in0=i, scalar=1.0,
                                            in1=o, op0=ALU.mult, op1=ALU.add)),
                                        after=[t_av, ac_tags[-8] if len(ac_tags) >= 8 else None])
                        new_ac.append(t_ac)
                        ac_tags.append(t_ac)
                    av_war[0] = new_ac[6]
                    av_war[1] = new_ac[7]
                    pt_free[psl_] = t_av
                    vp_free[psl_] = t_av
                    st["prev"] = new_ac[-1]

                for ti in range(NT):
                    s0 = ti * 128
                    psl = ti % 2
                    ztags = {}

                    def zmm(tc, war):
                        zt, zo = zregs[tc]
                        return p.op("tensor", (lambda e, o=zt[:, zo:zo + 512],
                                    l=kT[pb:pb + 64, rb * T + s0: rb * T + s0 + 128],
                                    r=qT[pb:pb + 64, rb * T + tc * 512: rb * T + (tc + 1) * 512]:
                                    e.matmul(o, l, r, start=True, stop=True)),
                                    after=war)
                    evs = []
                    for q4 in range(4):
                        zt, war0 = [(zA, za_free), (zB, zb_free)][q4 % 2]
                        if q4 >= 2:
                            war0 = evs[q4 - 2]
                        ztags[2 * q4] = zmm(2 * q4, [war0] + (
                            [last_slot[0], last_slot[1]] if q4 == 0 else []))
                        ztags[2 * q4 + 1] = zmm(2 * q4 + 1, [])
                        ev = p.op("scalar", (lambda e,
                                  o=pt[psl][:, q4 * 1024:(q4 + 1) * 1024], i=zt[:],
                                  a=lparts[:, q4:q4 + 1]:
                                  e.activation(o, i, AF.Exp, accum_out=a)),
                                  after=[ztags[2 * q4 + 1]] + (
                                      [pt_free[psl], lp_last] if q4 == 0 else []))
                        evs.append(ev)
                    za_free, zb_free = evs[2], evs[3]
                    if pend is not None:
                        flush_av(pend)
                    t_l = p.op("vector", (lambda e, o=lsum[:], i=lparts[:, 0:4]:
                               e.reduce_sum(o, i, axis=AX.X)), after=evs)
                    lp_last = t_l
                    t_li = p.op("vector", (lambda e, o=linv[:], i=lsum[:]:
                                e.reciprocal(o, i)), after=[t_l])
                    t_vp = p.op("vector", (lambda e, o=vp[psl][:],
                                i=vg[:, ti * 256 + hq * 64: ti * 256 + hq * 64 + 64],
                                s=linv[:]: e.tensor_scalar_mul(o, i, s)),
                                after=[t_li, vp_free[psl]])
                    pend = (ti, psl, evs[3], t_vp)
                flush_av(pend)
                prev_ac_last = st["prev"]
                if hq % 2 == 1:
                    gelu_done = p.op("scalar", (lambda e, o=gst[:], i=acc[:]:
                                     e.activation(o, i, AF.Gelu)),
                                     after=[prev_ac_last, gst_free])
                    pr = h // 2
                    t_postdma[pr] = p.dma("sync", io["postt"][pr * 128:(pr + 1) * 128, :],
                                          gst[:], after=[gelu_done])
                    gst_free = t_postdma[pr]
        p.barrier()

    with ExitStack() as sC:
        outwT = sbuf(nc, sC, "a0_outwT", [128, ND * D], BF16)
        wt4 = WT2(p, nc, sC, trp, "a0_wt4", scale=1.0 / 64.0, raw_width=D)
        for oco in range(ND):
            dcols = [outwT[:, ic * D + oco * 128: ic * D + oco * 128 + 128]
                     for ic in range(ND)]
            wt4.slab(io["out_w"][oco * 128:(oco + 1) * 128, :], D, dcols)
        pblk = sbuf(nc, sC, "a0_pblk", [128, ND * 512], BF16)
        stage = sbuf(nc, sC, "a0_stage", [128, ND * 512], BF16)
        src_t4 = sbuf(nc, sC, "a0_src4", [128, D])
        ua = sbuf(nc, sC, "a0_ua", [128, D])
        ub = sbuf(nc, sC, "a0_ub", [128, D])
        xl1 = sbuf(nc, sC, "a0_xl1", [128, D])
        xl1bf = sbuf(nc, sC, "a0_xl1bf", [128, D], BF16)
        at2 = [psum(nc, sC, f"a0_at2{i}", [128, 512]) for i in range(2)]
        outb_bc, t_outb = bcast_vec(p, nc, sC, cm, "a0_outb", io["out_b"], D, at2)
        g1_bc, t_g1 = bcast_vec(p, nc, sC, cm, "a0_g1", io["ln1_g"], D, at2)
        b1_bc, t_b1 = bcast_vec(p, nc, sC, cm, "a0_b1", io["ln1_b"], D, at2)
        p.barrier()
        last_ev4 = [None, None]
        last_stage = None
        last_pblk = None
        last_src4 = None
        last_xl1 = []
        for tb in range(8):
            tpb = []
            for ic in range(ND):
                tpb.append(p.dma("sync", pblk[:, ic * 512:(ic + 1) * 512],
                                 io["postt"][ic * 128:(ic + 1) * 128,
                                             tb * 512:(tb + 1) * 512],
                                 after=[last_pblk, t_postdma.get(ic)]))
            tstg = []
            for tsub in range(4):
                r0 = tb * 512 + tsub * 128
                tsrc = p.dma("sync", src_t4[:], io["src"][r0:r0 + 128, :],
                             after=[last_src4])
                mmt = [None, None]
                for oc in range(2):
                    for ic in range(ND):
                        mmt[oc] = p.op("tensor", (lambda e, o=at2[oc][:],
                                       l=pblk[:, ic * 512 + tsub * 128: ic * 512 + tsub * 128 + 128],
                                       r=outwT[:, ic * D + oc * 512: ic * D + (oc + 1) * 512],
                                       s=(ic == 0), st=(ic == ND - 1):
                                       e.matmul(o, l, r, start=s, stop=st)),
                                       after=[tpb[-1], last_ev4[oc]])
                tu = []
                for oc in range(2):
                    t1 = p.op("vector", (lambda e, o=ua[:, oc * 512:(oc + 1) * 512],
                              i=at2[oc][:], b=outb_bc[:, oc * 512:(oc + 1) * 512]:
                              e.scalar_tensor_tensor(out=o, in0=i, scalar=1.0, in1=b,
                                  op0=ALU.mult, op1=ALU.add)), after=[mmt[oc], t_outb])
                    t2 = p.op("vector", (lambda e, o=ub[:, oc * 512:(oc + 1) * 512],
                              a=ua[:, oc * 512:(oc + 1) * 512],
                              b=src_t4[:, oc * 512:(oc + 1) * 512]:
                              e.tensor_add(o, a, b)), after=[t1, tsrc])
                    last_ev4[oc] = t2
                    tu.append(t2)
                last_src4 = tu[-1]
                tm, tr = ln_stats(p, ub[:], lns, after=tu)
                ta = ln_apply(p, ua[:], ub[:], lns, [tm, tr])
                tg = p.op("vector", (lambda e, o=ub[:], a=ua[:], b=g1_bc[:]:
                          e.tensor_mul(o, a, b)), after=[ta, t_g1])
                tb2 = p.op("vector", (lambda e, o=xl1[:], a=ub[:], b=b1_bc[:]:
                           e.tensor_add(o, a, b)), after=[tg, t_b1] + last_xl1)
                t_xd = p.dma("sync", io["xln1"][r0:r0 + 128, :], xl1[:], after=[tb2])
                tbf = p.op("scalar", (lambda e, o=xl1bf[:], i=xl1[:]: e.copy(o, i)),
                           after=[tb2])
                last_xl1 = [t_xd, tbf]
                for ic in range(ND):
                    tstg.append(trp.tr(xl1bf[:, ic * 128:(ic + 1) * 128],
                                       stage[:, ic * 512 + tsub * 128: ic * 512 + tsub * 128 + 128],
                                       after=[tbf, last_stage]))
            xlt_dma = []
            for ic in range(ND):
                xlt_dma.append(p.dma("sync", io["xln1t"][ic, :, tb * 512:(tb + 1) * 512],
                                     stage[:, ic * 512:(ic + 1) * 512], after=[tstg[-1]]))
            last_stage = xlt_dma[-1]
            last_pblk = mmt[1]
        p.barrier()

    with ExitStack() as sD:
        s5 = sD.enter_context(ExitStack())
        lin1T = sbuf(nc, s5, "a0_lin1T", [128, ND * 4096], BF16)
        lin2T = sbuf(nc, s5, "a0_lin2T", [128, 32 * D], BF16)
        wt5 = WT2(p, nc, s5, trp, "a0_wt5", raw_width=2048)
        for j in range(32):
            dcols = [lin1T[:, ic * 4096 + j * 128: ic * 4096 + j * 128 + 128]
                     for ic in range(ND)]
            wt5.slab(io["lin1_w"][j * 128:(j + 1) * 128, :], D, dcols)
        for oro in range(ND):
            dcols = [lin2T[:, jc * D + oro * 128: jc * D + oro * 128 + 128]
                     for jc in range(32)]
            wt5.slab(io["lin2_w"][oro * 128:(oro + 1) * 128, 0:2048], 2048, dcols[:16])
            wt5.slab(io["lin2_w"][oro * 128:(oro + 1) * 128, 2048:4096], 2048, dcols[16:])
        b1c = sbuf(nc, s5, "a0_b1c", [128, 32])
        for j in range(32):
            load_col(p, b1c[:, j:j + 1], io["lin1_b"], j * 128, 128)
        hT = sbuf(nc, s5, "a0_hT", [128, 32 * 512], BF16)
        xlt_in = [sbuf(nc, s5, "a0_xlt0", [128, ND * 512], BF16)]
        xl1_t5 = sbuf(nc, s5, "a0_xl1t5", [128, D])
        ya = sbuf(nc, s5, "a0_ya", [128, D])
        u2 = sbuf(nc, s5, "a0_u2", [128, D])
        hps = [psum(nc, sD, f"a0_hps{i}", [128, 512]) for i in range(2)]
        yps = [psum(nc, sD, f"a0_yps{i}", [128, 512]) for i in range(2)]
        l2b_bc, t_l2b = bcast_vec(p, nc, s5, cm, "a0_l2b", io["lin2_b"], D, yps)
        p.barrier()
        last_hps = [None, None]
        last_yps = [None, None]
        last_xlt = [None, None]
        last_hT_reader = None
        last_xl5 = None
        last_u2d = None
        for tbk in range(8):
            xsl = 0
            txl = []
            for ic in range(ND):
                txl.append(p.dma("sync", xlt_in[xsl][:, ic * 512:(ic + 1) * 512],
                                 io["xln1t"][ic, :, tbk * 512:(tbk + 1) * 512],
                                 after=[last_xlt[xsl]]))
            th = {}
            for j in range(32):
                psl = j % 2
                mmt = None
                for ic in range(ND):
                    mmt = p.op("tensor", (lambda e, o=hps[psl][:],
                               l=lin1T[:, ic * 4096 + j * 128: ic * 4096 + j * 128 + 128],
                               r=xlt_in[xsl][:, ic * 512:(ic + 1) * 512],
                               s=(ic == 0), st=(ic == ND - 1):
                               e.matmul(o, l, r, start=s, stop=st)),
                               after=[txl[-1], last_hps[psl], last_hT_reader])
                th[j] = p.op("scalar", (lambda e, o=hT[:, j * 512:(j + 1) * 512],
                             i=hps[psl][:], b=b1c[:, j:j + 1]:
                             e.activation(o, i, AF.Gelu, bias=b)), after=[mmt])
                last_hps[psl] = th[j]
            last_xlt[xsl] = th[31]
            hT_readers = []
            for tsub in range(4):
                r0 = tbk * 512 + tsub * 128
                txl5 = p.dma("sync", xl1_t5[:], io["xln1"][r0:r0 + 128, :],
                             after=[last_xl5])
                mmt = [None, None]
                for oc in range(2):
                    for j in range(32):
                        mmt[oc] = p.op("tensor", (lambda e, o=yps[oc][:],
                                       l=hT[:, j * 512 + tsub * 128: j * 512 + tsub * 128 + 128],
                                       r=lin2T[:, j * D + oc * 512: j * D + (oc + 1) * 512],
                                       s=(j == 0), st=(j == 31):
                                       e.matmul(o, l, r, start=s, stop=st)),
                                       after=[th[31], last_yps[oc]])
                    hT_readers.append(mmt[oc])
                tev = []
                for oc in range(2):
                    t1 = p.op("vector", (lambda e, o=ya[:, oc * 512:(oc + 1) * 512],
                              i=yps[oc][:], b=l2b_bc[:, oc * 512:(oc + 1) * 512]:
                              e.scalar_tensor_tensor(out=o, in0=i, scalar=1.0, in1=b,
                                  op0=ALU.mult, op1=ALU.add)), after=[mmt[oc], t_l2b])
                    t2 = p.op("vector", (lambda e, o=u2[:, oc * 512:(oc + 1) * 512],
                              a=ya[:, oc * 512:(oc + 1) * 512],
                              b=xl1_t5[:, oc * 512:(oc + 1) * 512]:
                              e.tensor_add(o, a, b)), after=[t1, txl5, last_u2d])
                    last_yps[oc] = t2
                    tev.append(t2)
                last_xl5 = tev[-1]
                last_u2d = p.dma("sync", io["u2"][r0:r0 + 128, :], u2[:], after=tev)
            last_hT_reader = hT_readers[-1]
        p.barrier()
        s5.close()
        g2_bc, t_g2 = bcast_vec(p, nc, sD, cm, "a0_g2", io["ln2_g"], D, yps)
        b2_bc, t_b2 = bcast_vec(p, nc, sD, cm, "a0_b2", io["ln2_b"], D, yps)

        xa = sbuf(nc, sD, "a0_xa", [128, D])
        xb = sbuf(nc, sD, "a0_xb", [128, D])
        xc = sbuf(nc, sD, "a0_xc", [128, D])
        last6 = [None, None]
        last6st = None
        for ti in range(NT):
            r0 = ti * 128
            tu2 = p.dma("sync", xa[:], io["u2"][r0:r0 + 128, :],
                        after=[last6[0], last6st])
            tx = p.dma("sync", xb[:], io["x"][r0:r0 + 128, :], after=[last6[1]])
            tm, tr = ln_stats(p, xa[:], lns, after=[tu2])
            ta = ln_apply(p, xc[:], xa[:], lns, [tm, tr])
            tg = p.op("vector", (lambda e, o=xc[:], a=xc[:], b=g2_bc[:]:
                      e.tensor_mul(o, a, b)), after=[ta, t_g2])
            tb2 = p.op("vector", (lambda e, o=xa[:], a=xc[:], b=b2_bc[:]:
                       e.tensor_add(o, a, b)), after=[tg, t_b2, last6st])
            tm2, tr2 = ln_stats(p, xa[:], lns, after=[tb2])
            ta2 = ln_apply(p, xc[:], xa[:], lns, [tm2, tr2])
            last6[0] = ta2
            tx1 = p.op("vector", (lambda e, o=xa[:], a=xc[:], b=xb[:]:
                       e.tensor_add(o, a, b)), after=[ta2, tx])
            last6[1] = tx1
            last6st = p.dma("sync", io["x1"][r0:r0 + 128, :], xa[:], after=[tx1])
        p.barrier()



def _din(nc, name, shape, dtype=F32):
    return nc.dram_tensor(name, shape, dtype, kind="ExternalInput").ap()


def _dout(nc, name, shape, dtype=F32):
    return nc.dram_tensor(name, shape, dtype, kind="ExternalOutput").ap()


def _dtmp(nc, name, shape, dtype=F32):
    return nc.dram_tensor(name, shape, dtype, kind="Internal").ap()


def build_a0(debug=False):
    nc = bass.Bass("TRN2", target_bir_lowering=False)
    io = {
        "x": _din(nc, "x", [T, D]),
        "in_proj_w": _din(nc, "in_proj_w", [3 * D, D]),
        "in_proj_b": _din(nc, "in_proj_b", [3 * D]),
        "out_w": _din(nc, "out_w", [D, D]),
        "out_b": _din(nc, "out_b", [D]),
        "lin1_w": _din(nc, "lin1_w", [4096, D]),
        "lin1_b": _din(nc, "lin1_b", [4096]),
        "lin2_w": _din(nc, "lin2_w", [D, 4096]),
        "lin2_b": _din(nc, "lin2_b", [D]),
        "ln1_g": _din(nc, "ln1_g", [D]),
        "ln1_b": _din(nc, "ln1_b", [D]),
        "ln2_g": _din(nc, "ln2_g", [D]),
        "ln2_b": _din(nc, "ln2_b", [D]),
        "iden": _din(nc, "iden", [128, 128], BF16),
        "x1": _dout(nc, "x1", [T, D]),
        "src": _dtmp(nc, "a0src", [T, D]),
        "postt": _dtmp(nc, "a0postt", [D, T], BF16),
        "xln1": _dtmp(nc, "a0xln1", [T, D]),
        "xln1t": _dtmp(nc, "a0xln1t", [ND, 128, T], BF16),
        "u2": _dtmp(nc, "a0u2", [T, D]),
    }
    with ExitStack() as ctx:
        p = Prog(nc, ctx)
        cm = Common(p, nc, ctx, io)
        emit_a0(p, nc, ctx, cm, io)
        p.barrier()
        if debug:
            for nm, shp, dt in (("src", [T, D], F32), ("postt", [D, T], BF16),
                                ("xln1", [T, D], F32), ("u2", [T, D], F32),
                                ("xln1t", [ND, 128, T], BF16)):
                dbg = _dout(nc, "dbg_" + nm, shp, dt)
                p.dma("sync", dbg, io[nm])
            p.barrier()
        p.replay()
    return nc


IDEN = np.eye(128, dtype=np.float32).astype(ml_dtypes.bfloat16)


def _a0_inmaps(inputs, cores):
    m = []
    for c in cores:
        m.append({
            "x": np.ascontiguousarray(inputs["L0_attn_src"][c]),
            "in_proj_w": inputs["L0_attn_in_proj_w"],
            "in_proj_b": inputs["L0_attn_in_proj_b"],
            "out_w": inputs["L0_attn_out_w"],
            "out_b": inputs["L0_attn_out_b"],
            "lin1_w": inputs["L0_attn_lin1_w"],
            "lin1_b": inputs["L0_attn_lin1_b"],
            "lin2_w": inputs["L0_attn_lin2_w"],
            "lin2_b": inputs["L0_attn_lin2_b"],
            "ln1_g": inputs["L0_attn_ln1_g"],
            "ln1_b": inputs["L0_attn_ln1_b"],
            "ln2_g": inputs["L0_attn_ln2_g"],
            "ln2_b": inputs["L0_attn_ln2_b"],
            "iden": IDEN,
        })
    return m


def kernel(**inputs):
    return kernel_fused(**inputs)


def emit_f0(p, nc, ctx, cm, io):
    H, DH = 16, 64
    trp = TrPool(p, nc, ctx, cm, "f0trp")
    lns = LNS(nc, ctx, "f0_lns")

    with ExitStack() as sB:
        u1T = sbuf(nc, sB, "f0_u1T", [128, ND * T], BF16)
        zA = psum(nc, sB, "f0_zA", [128, 1024])
        zB = psum(nc, sB, "f0_zB", [128, 1024])
        avp = psum(nc, sB, "f0_avp", [128, 512])
        lbp = psum(nc, sB, "f0_lbp", [128, 512])
        g1_bc, t_g1 = bcast_vec(p, nc, sB, cm, "f0_g1", io["ln1_w"], D, [zA, zB])
        b1_bc, t_b1 = bcast_vec(p, nc, sB, cm, "f0_b1", io["ln1_b"], D, [zA, zB])
        s1 = sB.enter_context(ExitStack())
        x_t = [sbuf(nc, s1, f"f0_x{i}", [128, D]) for i in range(2)]
        xs_t = [sbuf(nc, s1, f"f0_xs{i}", [128, D]) for i in range(2)]
        u1_t = [sbuf(nc, s1, f"f0_u1{i}", [128, D]) for i in range(2)]
        u1bf = [sbuf(nc, s1, f"f0_u1bf{i}", [128, D], BF16) for i in range(2)]
        lastu = {}
        for ti in range(NT):
            sl = ti % 2
            r0 = ti * 128
            tx = p.dma("sync", x_t[sl][:], io["x"][r0:r0 + 128, :],
                       after=[lastu.get(("x", sl))])
            tm, tr = ln_stats(p, x_t[sl][:], lns, after=[tx])
            ts = ln_apply(p, xs_t[sl][:], x_t[sl][:], lns,
                          [tm, tr, lastu.get(("xsd", sl)), lastu.get(("bf", sl))])
            lastu[("x", sl)] = ts
            lastu[("xsd", sl)] = p.dma("sync", io["xs"][r0:r0 + 128, :],
                                       xs_t[sl][:], after=[ts])
            tm2, tr2 = ln_stats(p, xs_t[sl][:], lns, after=[ts])
            ta = ln_apply(p, u1_t[sl][:], xs_t[sl][:], lns, [tm2, tr2])
            tg = p.op("vector", (lambda e, o=u1_t[sl][:], a=u1_t[sl][:], b=g1_bc[:]:
                      e.tensor_mul(o, a, b)), after=[ta, t_g1])
            tb = p.op("vector", (lambda e, o=u1_t[sl][:], a=u1_t[sl][:], b=b1_bc[:]:
                      e.tensor_add(o, a, b)), after=[tg, t_b1])
            tbf = p.op("scalar", (lambda e, o=u1bf[sl][:], i=u1_t[sl][:]:
                                  e.copy(o, i)), after=[tb, lastu.get(("bf", sl))])
            tlast = None
            for ic in range(ND):
                tlast = trp.tr(u1bf[sl][:, ic * 128:(ic + 1) * 128],
                               u1T[:, ic * T + r0: ic * T + r0 + 128], after=[tbf])
            lastu[("bf", sl)] = tlast
        p.barrier()
        s1.close()

        wqkv = sbuf(nc, sB, "f0_wqkv", [128, ND * 3 * D], BF16)
        t_w = []
        for ic in range(ND):
            t_w.append(p.dma("gpsimd", wqkv[:, ic * 3 * D:(ic + 1) * 3 * D],
                             io["w_qkv"][ic * 128:(ic + 1) * 128, :]))
        qT = sbuf(nc, sB, "f0_qT", [128, 2 * T], BF16)
        kT = sbuf(nc, sB, "f0_kT", [128, 2 * T], BF16)
        VW = 260
        vg = sbuf(nc, sB, "f0_vg", [128, NT * VW], BF16)
        ptile = [sbuf(nc, sB, f"f0_pt{i}", [128, 1024], BF16) for i in range(2)]
        postT = sbuf(nc, sB, "f0_postT", [128, 512])
        linv_r = sbuf(nc, sB, "f0_linv", [1, 512])
        lbc_sb = sbuf(nc, sB, "f0_lbc", [128, 512])
        postbf = sbuf(nc, sB, "f0_postbf", [128, 512], BF16)

        mm_slot = [zA[:, 0:512], zB[:, 0:512]]
        last_slot = [t_g1, t_b1]
        t_vones = p.op("vector", (lambda e, o=vg[:]: e.memset(o, 1.0)))
        t_postdma = {}
        pt_free = [None, None]
        last_avp = None
        last_lbp = None
        last_postT = None
        last_pbf = None

        for g in range(4):
            for part in ("q", "k"):
                woff = (0 if part == "q" else D) + g * 256
                dest = qT if part == "q" else kT
                for rb in range(2):
                    for tc in range(8):
                        psl = tc % 2
                        mmt = None
                        for ic in range(ND):
                            mmt = p.op("tensor", (lambda e, o=mm_slot[psl],
                                       l=wqkv[:, ic * 3 * D + woff + rb * 128:
                                              ic * 3 * D + woff + (rb + 1) * 128],
                                       r=u1T[:, ic * T + tc * 512: ic * T + (tc + 1) * 512],
                                       s=(ic == 0), st=(ic == ND - 1):
                                       e.matmul(o, l, r, start=s, stop=st)),
                                       after=[t_w[ic], last_slot[psl]])
                        tev = p.op("scalar", (lambda e,
                                   o=dest[:, rb * T + tc * 512: rb * T + (tc + 1) * 512],
                                   i=mm_slot[psl], s=(0.125 if part == "k" else 1.0):
                                   e.activation(o, i, AF.Identity, scale=s)),
                                   after=[mmt])
                        last_slot[psl] = tev
            woff = 2 * D + g * 256
            for ti in range(NT):
                for hh in range(2):
                    psl = hh % 2
                    mmt = None
                    for ic in range(ND):
                        mmt = p.op("tensor", (lambda e, o=mm_slot[psl][:, 0:128],
                                   l=u1T[:, ic * T + ti * 128: ic * T + ti * 128 + 128],
                                   r=wqkv[:, ic * 3 * D + woff + hh * 128:
                                          ic * 3 * D + woff + (hh + 1) * 128],
                                   s=(ic == 0), st=(ic == ND - 1):
                                   e.matmul(o, l, r, start=s, stop=st)),
                                   after=[t_w[ic], last_slot[psl]])
                    base = ti * VW + (2 * hh) * 65
                    tev = p.op("vector", (lambda e,
                               o=vg[:, base:base + 130].rearrange("p (s c) -> p s c", c=65)[:, :, 0:64],
                               i=mm_slot[psl][:, 0:128].rearrange("p (s c) -> p s c", c=64):
                               e.tensor_copy(o, i)), after=[mmt, t_vones])
                    last_slot[psl] = tev
            for hq in range(4):
                h = 4 * g + hq
                rb = hq // 2
                pb = 64 * (hq % 2)
                for tc in range(8):
                    avtags = []
                    pend = None

                    def flush_av(pd):
                        k2, tex = pd
                        for kk in range(2):
                            sj = 2 * k2 + kk
                            ta = p.op("tensor", (lambda e, o=avp[0:65, :],
                                      l=vg[:, sj * VW + hq * 65: sj * VW + hq * 65 + 65],
                                      r=ptile[k2 % 2][:, kk * 512:(kk + 1) * 512],
                                      s=(sj == 0), st=(sj == NT - 1):
                                      e.matmul(o, l, r, start=s, stop=st)),
                                      after=[tex, last_avp if sj == 0 else None])
                            avtags.append(ta)
                        pt_free[k2 % 2] = avtags[-1]

                    for k2 in range(NT // 2):
                        zsl = k2 % 2
                        zt = [zA, zB][zsl]
                        for kk in range(2):
                            si = 2 * k2 + kk
                            tz = p.op("tensor", (lambda e, o=zt[:, kk * 512:(kk + 1) * 512],
                                      l=kT[pb:pb + 64, rb * T + si * 128: rb * T + si * 128 + 128],
                                      r=qT[pb:pb + 64, rb * T + tc * 512: rb * T + (tc + 1) * 512]:
                                      e.matmul(o, l, r, start=True, stop=True)),
                                      after=[last_slot[zsl]] if kk == 0 else [])
                        if pend is not None:
                            flush_av(pend)
                        tex = p.op("scalar", (lambda e, o=ptile[k2 % 2][:], i=zt[:],
                                   : e.activation(o, i, AF.Exp)),
                                   after=[tz, pt_free[k2 % 2]])
                        last_slot[zsl] = tex
                        pend = (k2, tex)
                    flush_av(pend)
                    t_li = p.op("vector", (lambda e, o=linv_r[:], i=avp[64:65, :]:
                                e.reciprocal(o, i)), after=[avtags[-1]])
                    t_lb = p.op("tensor", (lambda e, o=lbp[0:64, :], l=cm.ones[0:1, 0:64],
                                r=linv_r[:]: e.matmul(o, l, r, start=True, stop=True)),
                                after=[t_li, cm.t_ones, last_lbp])
                    t_lbc = p.op("vector", (lambda e, o=lbc_sb[0:64, :], i=lbp[0:64, :]:
                                 e.tensor_copy(o, i)), after=[t_lb])
                    last_lbp = t_lbc
                    t_pn = p.op("vector", (lambda e, o=postbf[0:64, :], a=avp[0:64, :],
                                b=lbc_sb[0:64, :]: e.tensor_mul(o, a, b)),
                                after=[t_lbc, last_pbf])
                    last_avp = t_pn
                    t_pd = p.dma("sync", io["postt"][h * 64:(h + 1) * 64,
                                                     tc * 512:(tc + 1) * 512],
                                 postbf[0:64, :], after=[t_pn])
                    last_pbf = t_pd
        p.barrier()

    with ExitStack() as sC:
        wpr = sbuf(nc, sC, "f0_wpr", [128, ND * D], BF16)
        t_wp = []
        for ic in range(ND):
            t_wp.append(p.dma("gpsimd", wpr[:, ic * D:(ic + 1) * D],
                              io["w_proj"][ic * 128:(ic + 1) * 128, :]))
        pblk = sbuf(nc, sC, "f0_pblk", [128, ND * 512], BF16)
        stage = sbuf(nc, sC, "f0_stage", [128, ND * 512], BF16)
        xs_t4 = sbuf(nc, sC, "f0_xs4", [128, D])
        ua = sbuf(nc, sC, "f0_ua", [128, D])
        ub = sbuf(nc, sC, "f0_ub", [128, D])
        xl2 = sbuf(nc, sC, "f0_xl2", [128, D])
        xl2bf = sbuf(nc, sC, "f0_xl2bf", [128, D], BF16)
        at2 = [psum(nc, sC, f"f0_at2{i}", [128, 512]) for i in range(2)]
        g2_bc, t_g2 = bcast_vec(p, nc, sC, cm, "f0_g2", io["ln2_w"], D, at2)
        b2_bc, t_b2 = bcast_vec(p, nc, sC, cm, "f0_b2", io["ln2_b"], D, at2)
        p.barrier()
        last_ev4 = [None, None]
        last_stage = None
        last_pblk = None
        last_xs4 = None
        last_u2md = None
        last_tbf = None
        for tb in range(8):
            tpb = []
            for ic in range(ND):
                tpb.append(p.dma("sync", pblk[:, ic * 512:(ic + 1) * 512],
                                 io["postt"][ic * 128:(ic + 1) * 128,
                                             tb * 512:(tb + 1) * 512],
                                 after=[last_pblk]))
            tstg = []
            for tsub in range(4):
                r0 = tb * 512 + tsub * 128
                txs = p.dma("sync", xs_t4[:], io["xs"][r0:r0 + 128, :],
                            after=[last_xs4])
                mmt = [None, None]
                for oc in range(2):
                    for ic in range(ND):
                        mmt[oc] = p.op("tensor", (lambda e, o=at2[oc][:],
                                       l=pblk[:, ic * 512 + tsub * 128: ic * 512 + tsub * 128 + 128],
                                       r=wpr[:, ic * D + oc * 512: ic * D + (oc + 1) * 512],
                                       s=(ic == 0), st=(ic == ND - 1):
                                       e.matmul(o, l, r, start=s, stop=st)),
                                       after=[tpb[-1], last_ev4[oc], t_wp[ic]])
                tu = []
                for oc in range(2):
                    t2 = p.op("vector", (lambda e, o=ub[:, oc * 512:(oc + 1) * 512],
                              a=at2[oc][:], b=xs_t4[:, oc * 512:(oc + 1) * 512]:
                              e.tensor_add(o, a, b)), after=[mmt[oc], txs, last_u2md])
                    last_ev4[oc] = t2
                    tu.append(t2)
                last_xs4 = tu[-1]
                last_u2md = p.dma("sync", io["u2m"][r0:r0 + 128, :], ub[:], after=tu)
                tm, tr = ln_stats(p, ub[:], lns, after=tu)
                ta = ln_apply(p, ua[:], ub[:], lns, [tm, tr])
                tg = p.op("vector", (lambda e, o=ua[:], a=ua[:], b=g2_bc[:]:
                          e.tensor_mul(o, a, b)), after=[ta, t_g2])
                tb2 = p.op("vector", (lambda e, o=xl2[:], a=ua[:], b=b2_bc[:]:
                           e.tensor_add(o, a, b)), after=[tg, t_b2, last_tbf])
                tbf = p.op("scalar", (lambda e, o=xl2bf[:], i=xl2[:]: e.copy(o, i)),
                           after=[tb2])
                last_tbf = tbf
                for ic in range(ND):
                    tstg.append(trp.tr(xl2bf[:, ic * 128:(ic + 1) * 128],
                                       stage[:, ic * 512 + tsub * 128: ic * 512 + tsub * 128 + 128],
                                       after=[tbf, last_stage]))
            xlt_dma = []
            for ic in range(ND):
                xlt_dma.append(p.dma("sync", io["xln2t"][ic, :, tb * 512:(tb + 1) * 512],
                                     stage[:, ic * 512:(ic + 1) * 512], after=[tstg[-1]]))
            last_stage = xlt_dma[-1]
            last_pblk = mmt[1]
        p.barrier()

    with ExitStack() as sE:
        gw = sbuf(nc, sE, "f0_gw", [128, ND * 8], BF16)
        t_gw = []
        for ic in range(ND):
            t_gw.append(p.dma("gpsimd", gw[:, ic * 8:(ic + 1) * 8],
                              io["gate_w"][ic * 128:(ic + 1) * 128, :]))
        xlt_in = sbuf(nc, sE, "f0_xlt5", [128, ND * 128], BF16)
        glog = sbuf(nc, sE, "f0_glog", [128, 8])
        gmax = sbuf(nc, sE, "f0_gmax", [128, 8])
        gidx = sbuf(nc, sE, "f0_gidx", [128, 8], U32)
        gidxf = sbuf(nc, sE, "f0_gidxf", [128, 8])
        iota8i = sbuf(nc, sE, "f0_iota8i", [128, 8], I32)
        iota8 = sbuf(nc, sE, "f0_iota8", [128, 8])
        masks = sbuf(nc, sE, "f0_masks", [128, NT * 8])
        gps = psum(nc, sE, "f0_gps", [128, 512])
        gb_bc, t_gb = bcast_vec(p, nc, sE, cm, "f0_gb", io["gate_b"], 8, [gps])
        t_io = p.op("gpsimd", (lambda e, o=iota8i[:]:
                    e.iota(o, [[1, 8]], channel_multiplier=0)))
        t_iof = p.op("vector", (lambda e, o=iota8[:], i=iota8i[:]:
                     e.tensor_copy(o, i)), after=[t_io])
        last_xlt = None
        last_gps = t_gb
        for ti in range(NT):
            r0 = ti * 128
            txl = []
            for ic in range(ND):
                txl.append(p.dma("sync", xlt_in[:, ic * 128:(ic + 1) * 128],
                                 io["xln2t"][ic, :, r0:r0 + 128], after=[last_xlt]))
            mmt = None
            for ic in range(ND):
                mmt = p.op("tensor", (lambda e, o=gps[:, 0:8],
                           l=xlt_in[:, ic * 128:(ic + 1) * 128],
                           r=gw[:, ic * 8:(ic + 1) * 8],
                           s=(ic == 0), st=(ic == ND - 1):
                           e.matmul(o, l, r, start=s, stop=st)),
                           after=[txl[-1], t_gw[ic], last_gps])
            t_lg = p.op("vector", (lambda e, o=glog[:], a=gps[:, 0:8], b=gb_bc[:]:
                        e.tensor_add(o, a, b)), after=[mmt, t_gb])
            last_gps = t_lg
            t_mx = p.op("vector", (lambda e, o=gmax[:], i=glog[:]: e.max(o, i)),
                        after=[t_lg])
            t_ix = p.op("vector", (lambda e, o=gidx[:], m=gmax[:], i=glog[:]:
                        e.max_index(o, m, i)), after=[t_mx])
            t_if = p.op("vector", (lambda e, o=gidxf[:], i=gidx[:]:
                        e.tensor_copy(o, i)), after=[t_ix])
            t_m = p.op("vector", (lambda e, o=masks[:, ti * 8:(ti + 1) * 8],
                       i=iota8[:], s=gidxf[:, 0:1]:
                       e.tensor_single_scalar(o, i, s, ALU.is_equal)),
                       after=[t_if, t_iof])
            last_xlt = t_m
        p.barrier()

        with ExitStack() as s6:
            we1 = sbuf(nc, s6, "f0_we1", [128, ND * 4096], BF16)
            we2 = sbuf(nc, s6, "f0_we2", [128, 32 * D], BF16)
            hT = sbuf(nc, s6, "f0_hT", [128, 32 * 512], BF16)
            xlt6 = sbuf(nc, s6, "f0_xlt6", [128, ND * 512], BF16)
            yout = sbuf(nc, s6, "f0_yout", [128, D])
            hps = [psum(nc, s6, f"f0_hps{i}", [128, 512]) for i in range(2)]
            yps = [psum(nc, s6, f"f0_yps{i}", [128, 512]) for i in range(2)]
            last_we2 = None
            last_hps = [None, None]
            last_yps = [None, None]
            last_xlt6 = None
            last_yout = None

            def emit_w1(ex_, war):
                return [p.dma("gpsimd", we1[:, ic * 4096:(ic + 1) * 4096],
                              io["w1"][ex_, ic * 128:(ic + 1) * 128, :],
                              after=[war]) for ic in range(ND)]

            t_we1 = emit_w1(0, None)
            for ex in range(8):
                nxt_we1 = None
                t_we2 = []
                for jc in range(32):
                    t_we2.append(p.dma("gpsimd",
                                       we2[:, jc * D:(jc + 1) * D],
                                       io["w2"][ex, jc * 128:(jc + 1) * 128, :],
                                       after=[last_we2]))
                last_hmm = None
                last_ymm = None
                for tbk in range(8):
                    txl = []
                    for ic in range(ND):
                        txl.append(p.dma("sync", xlt6[:, ic * 512:(ic + 1) * 512],
                                         io["xln2t"][ic, :, tbk * 512:(tbk + 1) * 512],
                                         after=[last_xlt6]))
                    th = None
                    for j in range(32):
                        psl = j % 2
                        mmt = None
                        for ic in range(ND):
                            mmt = p.op("tensor", (lambda e, o=hps[psl][:],
                                       l=we1[:, ic * 4096 + j * 128: ic * 4096 + j * 128 + 128],
                                       r=xlt6[:, ic * 512:(ic + 1) * 512],
                                       s=(ic == 0), st=(ic == ND - 1):
                                       e.matmul(o, l, r, start=s, stop=st)),
                                       after=[txl[-1], last_hps[psl], t_we1[-1]])
                        last_hmm = mmt
                        th = p.op("scalar", (lambda e, o=hT[:, j * 512:(j + 1) * 512],
                                  i=hps[psl][:]:
                                  e.activation(o, i, AF.Gelu_apprx_tanh)), after=[mmt])
                        last_hps[psl] = th
                    last_xlt6 = th
                    if tbk == 7 and ex < 7:
                        nxt_we1 = emit_w1(ex + 1, last_hmm)
                    for tsub in range(4):
                        ti = tbk * 4 + tsub
                        r0 = ti * 128
                        mmt = [None, None]
                        for oc in range(2):
                            for j in range(32):
                                mmt[oc] = p.op("tensor", (lambda e, o=yps[oc][:],
                                               l=hT[:, j * 512 + tsub * 128: j * 512 + tsub * 128 + 128],
                                               r=we2[:, j * D + oc * 512: j * D + (oc + 1) * 512],
                                               s=(j == 0), st=(j == 31):
                                               e.matmul(o, l, r, start=s, stop=st)),
                                               after=[th, last_yps[oc], t_we2[-1]])
                            last_ymm = mmt[oc]
                        tev = []
                        for oc in range(2):
                            t1 = p.op("vector", (lambda e, o=yout[:, oc * 512:(oc + 1) * 512],
                                      i=yps[oc][:], m=masks[:, ti * 8 + ex: ti * 8 + ex + 1]:
                                      e.tensor_scalar_mul(o, i, m)),
                                      after=[mmt[oc], last_yout])
                            last_yps[oc] = t1
                            tev.append(t1)
                        last_yout = p.dma("gpsimd", io["moe"][r0:r0 + 128, :], yout[:],
                                          after=tev,
                                          **({} if ex == 0 else {"accum_op": ALU.add}))
                if nxt_we1 is not None:
                    t_we1 = nxt_we1
                last_we2 = last_ymm
            p.barrier()

    with ExitStack() as sF:
        xa = sbuf(nc, sF, "f0_xa", [128, D])
        xb = sbuf(nc, sF, "f0_xb", [128, D])
        xc = sbuf(nc, sF, "f0_xc", [128, D])
        last7 = [None, None, None]
        lastst = None
        for ti in range(NT):
            r0 = ti * 128
            tu = p.dma("sync", xa[:], io["u2m"][r0:r0 + 128, :], after=[last7[0]])
            tm_ = p.dma("sync", xb[:], io["moe"][r0:r0 + 128, :], after=[last7[1]])
            tadd = p.op("vector", (lambda e, o=xa[:], a=xa[:], b=xb[:]:
                        e.tensor_add(o, a, b)), after=[tu, tm_])
            last7[0] = tadd
            tx = p.dma("sync", xb[:], io["x"][r0:r0 + 128, :], after=[tadd])
            tm, tr = ln_stats(p, xa[:], lns, after=[tadd])
            ta = ln_apply(p, xc[:], xa[:], lns, [tm, tr, lastst])
            tx2 = p.op("vector", (lambda e, o=xc[:], a=xc[:], b=xb[:]:
                       e.tensor_add(o, a, b)), after=[ta, tx])
            last7[1] = tx2
            last7[2] = tx2
            lastst = p.dma("sync", io["x2"][r0:r0 + 128, :], xc[:], after=[tx2])
        p.barrier()


def build_f0():
    nc = bass.Bass("TRN2", target_bir_lowering=False)
    io = {
        "x": _din(nc, "x", [T, D]),
        "w_qkv": _din(nc, "w_qkv", [D, 3 * D]),
        "w_proj": _din(nc, "w_proj", [D, D]),
        "ln1_w": _din(nc, "ln1_w", [D]),
        "ln1_b": _din(nc, "ln1_b", [D]),
        "ln2_w": _din(nc, "ln2_w", [D]),
        "ln2_b": _din(nc, "ln2_b", [D]),
        "gate_w": _din(nc, "gate_w", [D, 8]),
        "gate_b": _din(nc, "gate_b", [8]),
        "w1": _din(nc, "w1", [8, D, 4096]),
        "w2": _din(nc, "w2", [8, 4096, D]),
        "iden": _din(nc, "iden", [128, 128], BF16),
        "x2": _dout(nc, "x2", [T, D]),
        "xs": _dtmp(nc, "f0xs", [T, D]),
        "postt": _dtmp(nc, "f0postt", [D, T], BF16),
        "u2m": _dtmp(nc, "f0u2m", [T, D]),
        "xln2t": _dtmp(nc, "f0xln2t", [ND, 128, T], BF16),
        "moe": _dtmp(nc, "f0moe", [T, D]),
    }
    with ExitStack() as ctx:
        p = Prog(nc, ctx)
        cm = Common(p, nc, ctx, io)
        emit_f0(p, nc, ctx, cm, io)
        p.barrier()
        p.replay()
    return nc


def _f0_inmaps(inputs, x1_shards, cores):
    m = []
    for j, c in enumerate(cores):
        m.append({
            "x": np.ascontiguousarray(x1_shards[j]),
            "w_qkv": inputs["L0_ffn_w_qkv"],
            "w_proj": inputs["L0_ffn_w_proj"],
            "ln1_w": inputs["L0_ffn_ln1_w"],
            "ln1_b": inputs["L0_ffn_ln1_b"],
            "ln2_w": inputs["L0_ffn_ln2_w"],
            "ln2_b": inputs["L0_ffn_ln2_b"],
            "gate_w": inputs["L0_ffn_gate_w"],
            "gate_b": inputs["L0_ffn_gate_b"],
            "w1": inputs["L0_ffn_w1"],
            "w2": inputs["L0_ffn_w2"],
            "iden": IDEN,
        })
    return m


def emit_a1(p, nc, ctx, cm, io):
    H = 16
    trp = TrPool(p, nc, ctx, cm, "a1trp")
    lns = LNS(nc, ctx, "a1_lns")

    DEC = [float(np.log(1.0 - 2.0 ** (-5.0 - h))) for h in range(H)]

    tab = ExitStack()
    ctx.enter_context(tab)
    i32r = sbuf(nc, ctx, "a1_i32r", [1, 128], I32)
    posr = sbuf(nc, ctx, "a1_posr", [1, 128])
    ang64 = sbuf(nc, ctx, "a1_ang64", [1, 64])
    angrep = sbuf(nc, ctx, "a1_angrep", [1, D])
    tmp32 = sbuf(nc, ctx, "a1_tmp32", [1, 32])
    t_i = p.op("gpsimd", (lambda e, o=i32r[:]: e.iota(o, [[1, 128]], channel_multiplier=0)))
    t_p = p.op("vector", (lambda e, o=posr[:], i=i32r[:]: e.tensor_copy(o, i)), after=[t_i])
    t_a1 = p.op("vector", (lambda e, o=tmp32[:], i=posr[0:1, 0:32]:
                 e.tensor_scalar_mul(o, i, float(-np.log(10000.0) / 31.0))), after=[t_p])
    t_a2 = p.op("scalar", (lambda e, o=tmp32[:], i=tmp32[:]:
                 e.activation(o, i, AF.Exp)), after=[t_a1])
    t_a3 = p.op("vector", (lambda e, o=ang64[0:1, 0:64:2], i=tmp32[:]:
                 e.tensor_copy(o, i)), after=[t_a2])
    t_a4 = p.op("vector", (lambda e, o=ang64[0:1, 1:64:2], i=tmp32[:]:
                 e.tensor_copy(o, i)), after=[t_a2, t_a3])
    t_ar = []
    for rr in range(16):
        t_ar.append(p.op("vector", (lambda e, o=angrep[0:1, rr * 64:(rr + 1) * 64],
                    i=ang64[:]: e.tensor_copy(o, i)), after=[t_a4, t_a3]))
    t_ar_last = t_ar[-1]

    with ExitStack() as s1:
        x_t = [sbuf(nc, s1, f"a1_x{i}", [128, D]) for i in range(2)]
        w_t = [sbuf(nc, s1, f"a1_w{i}", [128, D]) for i in range(2)]
        wbf = [sbuf(nc, s1, f"a1_wbf{i}", [128, D], BF16) for i in range(2)]
        stage1 = sbuf(nc, s1, "a1_stage1", [128, ND * 512], BF16)
        lastu = {}
        last_st1 = None
        for tb in range(8):
            tstg = []
            for tsub in range(4):
                ti = tb * 4 + tsub
                sl = ti % 2
                r0 = ti * 128
                tx = p.dma("sync", x_t[sl][:], io["x"][r0:r0 + 128, :],
                           after=[lastu.get(("x", sl))])
                tm, tr = ln_stats(p, x_t[sl][:], lns, after=[tx])
                ts = ln_apply(p, w_t[sl][:], x_t[sl][:], lns,
                              [tm, tr, lastu.get(("wbf", sl))])
                lastu[("x", sl)] = ts
                tbf = p.op("scalar", (lambda e, o=wbf[sl][:], i=w_t[sl][:]:
                                      e.copy(o, i)), after=[ts, lastu.get(("bf", sl))])
                lastu[("wbf", sl)] = tbf
                for ic in range(ND):
                    tstg.append(trp.tr(wbf[sl][:, ic * 128:(ic + 1) * 128],
                                       stage1[:, ic * 512 + tsub * 128: ic * 512 + tsub * 128 + 128],
                                       after=[tbf, last_st1]))
                lastu[("bf", sl)] = tstg[-1]
            dmas = []
            for ic in range(ND):
                dmas.append(p.dma("sync", io["xwt"][ic, :, tb * 512:(tb + 1) * 512],
                                  stage1[:, ic * 512:(ic + 1) * 512], after=[tstg[-1]]))
            last_st1 = dmas[-1]
        p.barrier()

    qrT = sbuf(nc, ctx, "a1_qrT", [128, 8 * T], BF16)
    krT = sbuf(nc, ctx, "a1_krT", [128, 8 * T], BF16)
    io["_qrT"] = qrT
    io["_krT"] = krT
    with ExitStack() as s2:
        xwt_in = [sbuf(nc, s2, f"a1_xwt2{i}", [128, ND * 128], BF16)
                  for i in range(2)]
        qps = [psum(nc, s2, f"a1_qps{i}", [128, 1024]) for i in range(2)]
        cps = psum(nc, s2, "a1_cps", [128, 1024])
        dk_r = sbuf(nc, s2, "a1_dkr", [1, 16])
        omr_r = sbuf(nc, s2, "a1_omr", [1, 16])
        t_d1 = p.op("vector", (lambda e, o=dk_r[:], i=posr[0:1, 0:16]:
                    e.tensor_scalar(o, i, float(-np.log(2.0)), float(-5.0 * np.log(2.0)),
                                    ALU.mult, ALU.add)), after=[t_p])
        t_d2 = p.op("scalar", (lambda e, o=omr_r[:], i=dk_r[:]:
                    e.activation(o, i, AF.Exp)), after=[t_d1])
        t_d3 = p.op("vector", (lambda e, o=omr_r[:], i=omr_r[:]:
                    e.tensor_scalar(o, i, -1.0, 1.0, ALU.mult, ALU.add)), after=[t_d2])
        t_d4 = p.op("scalar", (lambda e, o=dk_r[:], i=omr_r[:]:
                    e.activation(o, i, AF.Ln)), after=[t_d3])
        t_d5 = p.op("vector", (lambda e, o=omr_r[:], i=omr_r[:]:
                    e.tensor_scalar(o, i, -1.0, 1.0, ALU.mult, ALU.add)), after=[t_d4])
        t_d6 = p.op("vector", (lambda e, o=omr_r[:], i=omr_r[:]:
                    e.reciprocal(o, i)), after=[t_d5])
        omr_bc = sbuf(nc, s2, "a1_omrbc", [128, 16])
        t_ob = p.op("tensor", (lambda e, o=cps[:, 0:16], l=cm.ones[:], r=omr_r[:]:
                    e.matmul(o, l, r, start=True, stop=True)), after=[t_d6, cm.t_ones])
        t_ob2 = p.op("vector", (lambda e, o=omr_bc[:], i=cps[:, 0:16]:
                     e.tensor_copy(o, i)), after=[t_ob])
        dk_bc = sbuf(nc, s2, "a1_dkbc", [128, 16])
        t_db = p.op("tensor", (lambda e, o=cps[:, 16:32], l=cm.ones[:], r=dk_r[:]:
                    e.matmul(o, l, r, start=True, stop=True)), after=[t_d4, cm.t_ones, t_ob2])
        t_db2 = p.op("vector", (lambda e, o=dk_bc[:], i=cps[:, 16:32]:
                     e.tensor_copy(o, i)), after=[t_db])
        ep_t = sbuf(nc, s2, "a1_ept", [128, 16])
        t_e1 = p.op("tensor", (lambda e, o=cps[:, 32:48], l=posr[:], r=dk_r[:]:
                    e.matmul(o, l, r, start=True, stop=True)), after=[t_d4, t_p, t_db2])
        t_e2 = p.op("vector", (lambda e, o=ep_t[:], i=cps[:, 32:48]:
                    e.tensor_scalar_mul(o, i, -1.0)), after=[t_e1])
        t_e3 = p.op("scalar", (lambda e, o=ep_t[:], i=ep_t[:]:
                    e.activation(o, i, AF.Exp)), after=[t_e2])
        prow = sbuf(nc, s2, "a1_prow", [1, 128])
        s2a = s2.enter_context(ExitStack())
        wq = sbuf(nc, s2a, "a1_wq", [128, ND * D], BF16)
        wk = wq
        bq_bc, t_bq = bcast_vec(p, nc, s2a, cm, "a1_bq", io["bq"], D, qps,
                                dtype=BF16)
        bk_bc, t_bk = bcast_vec(p, nc, s2a, cm, "a1_bk", io["bk"], D, qps,
                                dtype=BF16)
        t_bk = p.op("vector", (lambda e, o=bk_bc[:], i=bk_bc[:]:
                    e.tensor_scalar_mul(o, i, 0.125)), after=[t_bk])
        cosf = sbuf(nc, s2a, "a1_cosf", [128, D], BF16)
        sinf = sbuf(nc, s2a, "a1_sinf", [128, D], BF16)
        rri = sbuf(nc, s2a, "a1_rri", [128, D], I32)
        q_f = sbuf(nc, s2a, "a1_qf", [128, D])
        t1_f = sbuf(nc, s2a, "a1_t1f", [128, D])
        t2_f = sbuf(nc, s2a, "a1_t2f", [128, D])
        qrbf = [sbuf(nc, s2a, f"a1_qrbf{i}", [128, D], BF16) for i in range(2)]
        ap_t = sbuf(nc, s2a, "a1_apt", [128, 16])
        rr_t = sbuf(nc, s2a, "a1_rrt", [128, 16])
        sc1 = sbuf(nc, s2a, "a1_sc1", [128, 1])
        sc2 = sbuf(nc, s2a, "a1_sc2", [128, 1])
        TWO_PI = float(2.0 * np.pi)

        angbc = sbuf(nc, s2a, "a1_angbc", [128, D])
        t_ab1 = p.op("tensor", (lambda e, o=cps[:, 0:512], l=cm.ones[:],
                     r=angrep[0:1, 0:512]: e.matmul(o, l, r, start=True, stop=True)),
                     after=[t_ar_last, cm.t_ones, t_e2])
        t_ab2 = p.op("tensor", (lambda e, o=cps[:, 512:1024], l=cm.ones[:],
                     r=angrep[0:1, 512:1024]: e.matmul(o, l, r, start=True, stop=True)),
                     after=[t_ar_last, cm.t_ones, t_e2])
        t_ab3 = p.op("vector", (lambda e, o=angbc[:], i=cps[:]:
                     e.tensor_copy(o, i)), after=[t_ab1, t_ab2, t_e2])
        g_f = sbuf(nc, s2a, "a1_gf", [128, D])

        def build_trig(r0, after):
            tpr = p.op("vector", (lambda e, o=prow_c[:], i=pos_col[:], c=float(r0):
                       e.tensor_scalar_add(o, i, c)),
                       after=list(after) + [last_q.get("trigp")])
            war = last_q.get("trig")
            tgr = p.op("vector", (lambda e, o=g_f[:], i=angbc[:], s=prow_c[:]:
                       e.tensor_scalar_mul(o, i, s)), after=[tpr, t_ab3, t_pc2, war])
            last_q["trigp"] = tgr
            tags = []
            for fn, dest, shift in (("sin", sinf, 0.0), ("cos", cosf, float(np.pi / 2))):
                tu = p.op("vector", (lambda e, o=t1_f[:], i=g_f[:], sh=shift:
                          e.tensor_scalar(o, i, 1.0 / TWO_PI, sh / TWO_PI,
                                          ALU.mult, ALU.add)), after=[tgr, tags[-1] if tags else None])
                ti1 = p.op("vector", (lambda e, o=rri[:], i=t1_f[:]:
                           e.tensor_copy(o, i)), after=[tu])
                ti2 = p.op("vector", (lambda e, o=t2_f[:], i=rri[:]:
                           e.tensor_copy(o, i)), after=[ti1])
                ts_ = p.op("vector", (lambda e, o=t1_f[:], a=t1_f[:], b=t2_f[:]:
                           e.tensor_sub(o, a, b)), after=[ti2])
                tsc = p.op("vector", (lambda e, o=t1_f[:], i=t1_f[:]:
                           e.tensor_scalar_mul(o, i, TWO_PI)), after=[ts_])
                tsn = p.op("scalar", (lambda e, o=dest[:], i=t1_f[:]:
                           e.activation(o, i, AF.Sin)), after=[tsc])
                tags.append(tsn)
            last_q["trig"] = tags[-1]
            return tags

        def build_ap(r0, after):
            s0 = r0 % 512
            tpr = p.op("vector", (lambda e, o=prow[:], i=posr[:]:
                       e.tensor_scalar_add(o, i, float(r0 + 1))), after=after)
            tm = p.op("tensor", (lambda e, o=cps[:, 0:16], l=prow[:], r=dk_r[:]:
                      e.matmul(o, l, r, start=True, stop=True)), after=[tpr, t_d4])
            te = p.op("scalar", (lambda e, o=rr_t[:], i=cps[:, 0:16]:
                      e.activation(o, i, AF.Exp)), after=[tm])
            t1 = p.op("vector", (lambda e, o=rr_t[:], i=rr_t[:]:
                      e.tensor_scalar(o, i, -1.0, 1.0, ALU.mult, ALU.add)), after=[te])
            t2 = p.op("vector", (lambda e, o=rr_t[:], a=rr_t[:], b=omr_bc[:]:
                      e.tensor_mul(o, a, b)), after=[t1, t_ob2])
            t3 = p.op("scalar", (lambda e, o=rr_t[:], i=rr_t[:]:
                      e.activation(o, i, AF.Ln)), after=[t2])
            t4 = p.op("scalar", (lambda e, o=rr_t[:], i=rr_t[:]:
                      e.activation(o, i, AF.Exp, scale=-0.5)), after=[t3])
            t5 = p.op("vector", (lambda e, o=ap_t[:], i=dk_bc[:]:
                      e.tensor_scalar_mul(o, i, float(s0))), after=[t_db2, t4])
            return t4, t5

        pos_col = sbuf(nc, s2a, "a1_poscol", [128, 1])
        prow_c = sbuf(nc, s2a, "a1_prowc", [128, 1])
        t_pc = p.op("gpsimd", (lambda e, o=rri[:, 0:1]:
                    e.iota(o, [[1, 1]], channel_multiplier=1)))
        t_pc2 = p.op("vector", (lambda e, o=pos_col[:], i=rri[:, 0:1]:
                     e.tensor_copy(o, i)), after=[t_pc])

        last_q = {}
        for part, wsb, bias_bc, t_bias, destT in (
                ("q", wq, bq_bc, t_bq, qrT), ("k", wk, bk_bc, t_bk, krT)):
            twl = []
            for ic in range(ND):
                twl.append(p.dma("gpsimd", wsb[:, ic * D:(ic + 1) * D],
                                 io["Wq" if part == "q" else "Wk"][ic * 128:(ic + 1) * 128, :],
                                 after=[last_q.get("wconsume")]))
            pend_tr = None

            def flush_tr(pd):
                sl_, tc_, r0_ = pd
                ttp = None
                for ic in range(ND):
                    ttp = trp.tr(qrbf[sl_][:, ic * 128:(ic + 1) * 128],
                                 destT[:, ic * T + r0_: ic * T + r0_ + 128],
                                 after=[tc_])
                last_q[("qrbf", sl_)] = ttp

            for ti in range(NT):
                r0 = ti * 128
                sl = ti % 2
                txw = []
                for ic in range(ND):
                    txw.append(p.dma("sync", xwt_in[sl][:, ic * 128:(ic + 1) * 128],
                                     io["xwt"][ic, :, r0:r0 + 128],
                                     after=[last_q.get(("xwt", sl))]))
                mmt = None
                for oc in range(2):
                    for ic in range(ND):
                        mmt = p.op("tensor", (lambda e,
                                       o=qps[sl][:, oc * 512:(oc + 1) * 512],
                                       l=xwt_in[sl][:, ic * 128:(ic + 1) * 128],
                                       r=wsb[:, ic * D + oc * 512: ic * D + (oc + 1) * 512],
                                       s=(ic == 0), st=(ic == ND - 1):
                                       e.matmul(o, l, r, start=s, stop=st)),
                                       after=[txw[-1], twl[ic], last_q.get(("ev", sl))])
                last_q[("xwt", sl)] = mmt
                if pend_tr is not None:
                    flush_tr(pend_tr)
                    pend_tr = None
                sc_k = 0.125 if part == "k" else 1.0
                tq_one = p.op("vector", (lambda e, o=q_f[:], i=qps[sl][:],
                          b=bias_bc[:], s=sc_k:
                          e.scalar_tensor_tensor(out=o, in0=i, scalar=s, in1=b,
                              op0=ALU.mult, op1=ALU.add)),
                          after=[mmt, t_bias, last_q.get("qf")])
                last_q[("ev", sl)] = tq_one
                tq = [tq_one]
                ttr = build_trig(r0, tq)
                tr1 = p.op("vector", (lambda e, o=t1_f[:, 0:D:2], i=q_f[:, 1:D:2]:
                           e.tensor_scalar_mul(o, i, -1.0)),
                           after=tq + ttr + [last_q.get("t1")])
                tr2 = p.op("vector", (lambda e, o=t1_f[:, 1:D:2], i=q_f[:, 0:D:2]:
                           e.tensor_copy(o, i)), after=tq + [tr1])
                tm1 = p.op("vector", (lambda e, o=t2_f[:], a=q_f[:], b=cosf[:]:
                           e.tensor_mul(o, a, b)), after=tq + ttr + [last_q.get("t2")])
                tm2 = p.op("vector", (lambda e, o=t1_f[:], a=t1_f[:], b=sinf[:]:
                           e.tensor_mul(o, a, b)), after=[tr1, tr2] + ttr)
                tad = p.op("vector", (lambda e, o=t2_f[:], a=t2_f[:], b=t1_f[:]:
                           e.tensor_add(o, a, b)), after=[tm1, tm2])
                last_q["t1"] = tad
                last_q["qf"] = tad
                if part == "q":
                    s0 = r0 % 512
                    tp1 = p.op("vector", (lambda e, o=sc1[:], i=pos_col[:], c=float(s0):
                               e.tensor_scalar_add(o, i, c)), after=[t_pc2, last_q.get("sc1")])
                    tp2 = p.op("vector", (lambda e, o=rr_t[:], a=dk_bc[:], b=sc1[:]:
                               e.tensor_scalar_mul(o, a, b)), after=[tp1, t_db2, last_q.get("rrt")])
                    last_q["sc1"] = tp2
                    tp3 = p.op("scalar", (lambda e, o=rr_t[:], i=rr_t[:]:
                               e.activation(o, i, AF.Exp)), after=[tp2])
                    tp4 = p.op("vector", (lambda e, o=sc2[:], i=pos_col[:], c=float(r0 + 1):
                               e.tensor_scalar_add(o, i, c)), after=[t_pc2, last_q.get("sc2")])
                    tp5 = p.op("vector", (lambda e, o=ap_t[:], a=dk_bc[:], b=sc2[:]:
                               e.tensor_scalar_mul(o, a, b)), after=[tp4, last_q.get("apt")])
                    last_q["sc2"] = tp5
                    tp6 = p.op("scalar", (lambda e, o=ap_t[:], i=ap_t[:]:
                               e.activation(o, i, AF.Exp)), after=[tp5])
                    tp7 = p.op("vector", (lambda e, o=ap_t[:], i=ap_t[:]:
                               e.tensor_scalar(o, i, -1.0, 1.0, ALU.mult, ALU.add)), after=[tp6])
                    tp8 = p.op("vector", (lambda e, o=ap_t[:], a=ap_t[:], b=omr_bc[:]:
                               e.tensor_mul(o, a, b)), after=[tp7, t_ob2])
                    tp9 = p.op("scalar", (lambda e, o=ap_t[:], i=ap_t[:]:
                               e.activation(o, i, AF.Ln)), after=[tp8])
                    tp10 = p.op("scalar", (lambda e, o=ap_t[:], i=ap_t[:]:
                                e.activation(o, i, AF.Exp, scale=-0.5)), after=[tp9])
                    tfold = p.op("vector", (lambda e, o=rr_t[:], a=rr_t[:], b=ap_t[:]:
                                 e.tensor_mul(o, a, b)), after=[tp3, tp10])
                    last_q["apt"] = tfold
                    last_q["rrt"] = tfold
                    ftab, t_ft = rr_t, tfold
                else:
                    ftab, t_ft = ep_t, t_e3
                tcast = []
                for h in range(H):
                    tcast.append(p.op("vector", (lambda e,
                                 o=qrbf[sl][:, h * 64:(h + 1) * 64],
                                 i=t2_f[:, h * 64:(h + 1) * 64],
                                 s=ftab[:, h:h + 1]:
                                 e.tensor_scalar_mul(o, i, s)),
                                 after=[tad, t_ft, last_q.get(("qrbf", sl))]))
                last_q["t2"] = tcast[-1]
                pend_tr = (sl, tcast[-1], r0)
                last_q["wconsume"] = tq[-1]
            flush_tr(pend_tr)
            pend_tr = None
        p.barrier()
        s2a.close()

        u_ps = sbuf(nc, s2, "a1_ups", [128, D])
        wv = sbuf(nc, s2, "a1_wv", [128, ND * 2048], BF16)
        vstage = sbuf(nc, s2, "a1_vstage", [128, 16 * 129], BF16)
        gstage = sbuf(nc, s2, "a1_gstage", [128, 2048], BF16)
        bv_bc1, t_bv1 = bcast_vec(p, nc, s2, cm, "a1_bv1", io["bv"][0:1024], D, qps)
        bv_bc2, t_bv2 = bcast_vec(p, nc, s2, cm, "a1_bv2", io["bv"][1024:2048], D, qps)
        t_vone = p.op("vector", (lambda e, o=vstage[:]: e.memset(o, 1.0)))
        for part in ("v", "g"):
            t_wv = []
            for ic in range(ND):
                t_wv.append(p.dma("gpsimd", wv[:, ic * 2048:(ic + 1) * 2048],
                                  io["Wv" if part == "v" else "Wg"][ic * 128:(ic + 1) * 128, :],
                                  after=[last_q.get("vst")]))
            if part == "g":
                bv_bc1, t_bv1 = bcast_vec(p, nc, s2, cm, "a1_bg1", io["bg"][0:1024], D, qps)
                bv_bc2, t_bv2 = bcast_vec(p, nc, s2, cm, "a1_bg2", io["bg"][1024:2048], D, qps)
            for ti in range(NT):
                r0 = ti * 128
                sl = ti % 2
                txw = []
                for ic in range(ND):
                    txw.append(p.dma("sync", xwt_in[sl][:, ic * 128:(ic + 1) * 128],
                                     io["xwt"][ic, :, r0:r0 + 128],
                                     after=[last_q.get(("xwt", sl))]))
                mm_last = None
                for half in range(2):
                    bbc = [bv_bc1, bv_bc2][half]
                    tbb = [t_bv1, t_bv2][half]
                    psl = (2 * ti + half) % 2
                    mmt = None
                    for oc in range(2):
                        for ic in range(ND):
                            mmt = p.op("tensor", (lambda e,
                                       o=qps[psl][:, oc * 512:(oc + 1) * 512],
                                       l=xwt_in[sl][:, ic * 128:(ic + 1) * 128],
                                       r=wv[:, ic * 2048 + half * 1024 + oc * 512:
                                            ic * 2048 + half * 1024 + (oc + 1) * 512],
                                       s=(ic == 0), st=(ic == ND - 1):
                                       e.matmul(o, l, r, start=s, stop=st)),
                                       after=[txw[-1], t_wv[ic], last_q.get(("pv", psl))])
                    mm_last = mmt
                    if part == "v":
                        base = half * 8 * 129
                        tev = p.op("vector", (lambda e,
                                   o=vstage[:, base:base + 8 * 129].rearrange(
                                       "p (s c) -> p s c", c=129)[:, :, 0:128],
                                   i=qps[psl][:].rearrange("p (s c) -> p s c", c=128),
                                   b=bbc[:].rearrange("p (s c) -> p s c", c=128):
                                   e.scalar_tensor_tensor(out=o, in0=i, scalar=1.0,
                                       in1=b, op0=ALU.mult, op1=ALU.add)),
                                   after=[mmt, tbb, t_vone, last_q.get("vst")])
                        last_q["vst"] = tev
                    else:
                        t_g1 = p.op("vector", (lambda e, o=u_ps[:],
                                    i=qps[psl][:], b=bbc[:]:
                                    e.tensor_add(o, i, b)),
                                    after=[mmt, tbb, last_q.get("gs1")])
                        tev = p.op("scalar", (lambda e,
                                   o=gstage[:, half * 1024:(half + 1) * 1024],
                                   i=u_ps[:]:
                                   e.activation(o, i, AF.Silu)),
                                   after=[t_g1, last_q.get("vst")])
                        last_q["vst"] = tev
                        last_q["gs1"] = tev
                    last_q[("pv", psl)] = tev
                last_q[("xwt", sl)] = mm_last
                if part == "v":
                    tdm = p.dma("sync", io["vr"][r0:r0 + 128, :], vstage[:],
                                after=[last_q["vst"]])
                else:
                    tdm = p.dma("sync", io["sg"][r0:r0 + 128, :], gstage[:],
                                after=[last_q["vst"]])
                last_q["vst"] = tdm
                last_q["gs1"] = tdm
        p.barrier()

    with ExitStack() as s3:
        lns128 = LNS(nc, s3, "a1_lns128", width=128)
        zpp = [psum(nc, s3, f"a1_zpp{i}", [128, 512]) for i in range(2)]
        ops = [psum(nc, s3, f"a1_ops{i}", [128, 129]) for i in range(4)]
        qkbf = [sbuf(nc, s3, f"a1_qkbf{i}", [128, 512], BF16) for i in range(2)]
        cmask = sbuf(nc, s3, "a1_cmask", [128, 4 * 512])
        grid = sbuf(nc, s3, "a1_grid", [128, 512])
        i512 = sbuf(nc, s3, "a1_i512", [1, 512], I32)
        f512 = sbuf(nc, s3, "a1_f512", [1, 512])
        pcol = sbuf(nc, s3, "a1_pcol", [128, 1])
        den = sbuf(nc, s3, "a1_den", [128, 1])
        sgt = sbuf(nc, s3, "a1_sgt", [128, 128], BF16)
        ret = sbuf(nc, s3, "a1_ret", [128, 128])
        obf = sbuf(nc, s3, "a1_obf", [128, 128], BF16)
        stage3 = sbuf(nc, s3, "a1_stage3", [128, 512], BF16)
        t_i5 = p.op("gpsimd", (lambda e, o=i512[:]: e.iota(o, [[1, 512]], channel_multiplier=0)))
        t_f5 = p.op("vector", (lambda e, o=f512[:], i=i512[:]: e.tensor_copy(o, i)), after=[t_i5])
        t_pc3 = p.op("gpsimd", (lambda e, o=i512[0:1, 0:1]: 0) if False else
                     (lambda e, o=i512[0:1, 0:1]: e.iota(o, [[1, 1]], channel_multiplier=0)))
        icol = sbuf(nc, s3, "a1_icol", [128, 1], I32)
        t_ic = p.op("gpsimd", (lambda e, o=icol[:]: e.iota(o, [[1, 1]], channel_multiplier=1)))
        t_ic2 = p.op("vector", (lambda e, o=pcol[:], i=icol[:]: e.tensor_copy(o, i)), after=[t_ic])
        t_gr = p.op("tensor", (lambda e, o=zpp[0][:], l=cm.ones[:], r=f512[:]:
                    e.matmul(o, l, r, start=True, stop=True)), after=[t_f5, cm.t_ones])
        t_gr2 = p.op("vector", (lambda e, o=grid[:], i=zpp[0][:]: e.tensor_copy(o, i)), after=[t_gr])
        tmsk = []
        for j in range(4):
            tj = p.op("vector", (lambda e, o=den[:], i=pcol[:], c=float(128 * j):
                      e.tensor_scalar_add(o, i, c)), after=[t_ic2, tmsk[-1] if tmsk else None])
            tmsk.append(p.op("vector", (lambda e, o=cmask[:, j * 512:(j + 1) * 512],
                        i=grid[:], s=den[:]:
                        e.tensor_single_scalar(o, i, s, ALU.is_ge)), after=[tj, t_gr2]))
        t_cm = tmsk[-1]
        p.barrier()

        vrh = [sbuf(nc, s3, f"a1_vrh{i}", [128, 32 * 129], BF16) for i in range(2)]
        ovst = [sbuf(nc, s3, f"a1_ovst{i}", [128, 4 * 129]) for i in range(2)]
        last3 = {}

        def flush_epi(pd):
            h_, sc_, sl_, t_cps = pd
            t_r3 = None
            for sub in range(4):
                r0 = sc_ * 512 + sub * 128
                vcols = ovst[sl_][:, sub * 129: sub * 129 + 128]
                dcol = ovst[sl_][:, sub * 129 + 128: sub * 129 + 129]
                t_d = p.op("vector", (lambda e, o=den[:], i=dcol:
                           e.tensor_scalar_mul(o, i, -1.0)),
                           after=[t_cps[sub], last3.get("den")])
                t_d1 = p.op("vector", (lambda e, o=den[:], a=dcol, b=den[:]:
                            e.tensor_max(o, a, b)), after=[t_d])
                t_d2 = p.op("vector", (lambda e, o=den[:], i=den[:]:
                            e.tensor_scalar_max(o, i, 1.0)), after=[t_d1])
                t_d3 = p.op("vector", (lambda e, o=den[:], i=den[:]:
                            e.reciprocal(o, i)), after=[t_d2])
                t_r = p.op("vector", (lambda e, o=ret[:], i=vcols,
                           s=den[:]: e.tensor_scalar_mul(o, i, s)),
                           after=[t_d3, last3.get("ret")])
                t_r3 = t_r
                last3["den"] = t_r
                tm, tr = ln_stats(p, ret[:], lns128, after=[t_r])
                ta = ln_apply(p, ret[:], ret[:], lns128, [tm, tr])
                tsg = p.dma("gpsimd", sgt[:],
                            io["sg"][r0:r0 + 128, h_ * 128:(h_ + 1) * 128],
                            after=[last3.get("sgt")])
                tob = p.op("vector", (lambda e, o=obf[:], a=ret[:], b=sgt[:]:
                           e.tensor_mul(o, a, b)), after=[ta, tsg, last3.get("obf")])
                last3["sgt"] = tob
                last3["ret"] = tob
                tst = trp.tr(obf[:], stage3[:, sub * 128:(sub + 1) * 128],
                             after=[tob, last3.get("st3")])
                last3["obf"] = tst
                last3["st3d"] = tst
            last3[("ovw", sl_)] = t_r3
            tdm = p.dma("sync", io["ot"][h_, :, sc_ * 512:(sc_ + 1) * 512], stage3[:],
                        after=[last3["st3d"]])
            last3["st3"] = tdm

        pend_epi = None
        for h in range(H):
            hp, pb = h // 2, 64 * (h % 2)
            hs = h % 2
            dec = DEC[h]
            tvr_tags = []
            for tt in range(32):
                tvr_tags.append(p.dma("scalar",
                                vrh[hs][:, tt * 129:(tt + 1) * 129],
                                io["vr"][tt * 128:(tt + 1) * 128,
                                         h * 129:(h + 1) * 129],
                                after=[last3.get(("vrh", hs))]))
            av_last_head = None
            for sc in range(8):
                pend_av = None

                def flush_av(pd):
                    tt_, tev_ = pd
                    for sub in range(4):
                        avt = p.op("tensor", (lambda e, o=ops[sub][0:128, :],
                                   l=qkbf[tt_ % 2][:, sub * 128:(sub + 1) * 128],
                                   r=vrh[hs][:, tt_ * 129:(tt_ + 1) * 129],
                                   s=(tt_ == 0), st=(tt_ == 4 * sc + 3):
                                   e.matmul(o, l, r, start=s, stop=st)),
                                   after=[tev_, tvr_tags[tt_]] +
                                         ([last3.get(("ev", sub))] if tt_ == 0 else []))
                        last_avs[sub] = avt
                    last3[("qkb", tt_ % 2)] = avt

                last_avs = [None] * 4
                for tt in range(4 * sc + 4):
                    zsl = tt % 2
                    tz = p.op("tensor", (lambda e, o=zpp[zsl][:],
                              l=krT[pb:pb + 64, hp * T + tt * 128: hp * T + tt * 128 + 128],
                              r=qrT[pb:pb + 64, hp * T + sc * 512: hp * T + (sc + 1) * 512]:
                              e.matmul(o, l, r, start=True, stop=True)),
                              after=[last3.get(("z", zsl))])
                    if tt == 1 and pend_epi is not None:
                        flush_epi(pend_epi)
                        pend_epi = None
                    f_tile = float(np.exp(dec * (512 * sc - 128 * tt)))
                    diag = (tt >= 4 * sc)
                    if diag:
                        jj = tt - 4 * sc
                        ta_ = p.op("scalar", (lambda e, o=grid[:], i=zpp[zsl][:], f=f_tile:
                                   e.activation(o, i, AF.Identity, scale=f)),
                                   after=[tz, t_cm, last3.get("grd")])
                        tev = p.op("vector", (lambda e, o=qkbf[tt % 2][:], a=grid[:],
                                   b=cmask[:, jj * 512:(jj + 1) * 512]:
                                   e.tensor_mul(o, a, b)),
                                   after=[ta_, last3.get(("qkb", tt % 2))])
                        last3[("z", zsl)] = ta_
                        last3["grd"] = tev
                    else:
                        tev = p.op("scalar", (lambda e, o=qkbf[tt % 2][:],
                                   i=zpp[zsl][:], f=f_tile:
                                   e.activation(o, i, AF.Identity, scale=f)),
                                   after=[tz, last3.get(("qkb", tt % 2))])
                        last3[("z", zsl)] = tev
                    if pend_av is not None:
                        flush_av(pend_av)
                    pend_av = (tt, tev)
                flush_av(pend_av)
                av_last_head = last_avs[3]
                sl_sc = sc % 2
                t_cps = []
                for sub in range(4):
                    t_cp = p.op("vector", (lambda e,
                                o=ovst[sl_sc][:, sub * 129:(sub + 1) * 129],
                                i=ops[sub][:, 0:129]: e.tensor_copy(o, i)),
                                after=[last_avs[sub], last3.get(("ovw", sl_sc))])
                    last3[("ev", sub)] = t_cp
                    t_cps.append(t_cp)
                pend_epi = (h, sc, sl_sc, t_cps)
            last3[("vrh", hs)] = av_last_head
        flush_epi(pend_epi)
        pend_epi = None
        p.barrier()

    with ExitStack() as sC:
        wo = sbuf(nc, sC, "a1_wo", [128, 16 * D], BF16)
        t_wo = []
        for jc in range(16):
            t_wo.append(p.dma("gpsimd", wo[:, jc * D:(jc + 1) * D],
                              io["Wo"][jc * 128:(jc + 1) * 128, :]))
        otb = sbuf(nc, sC, "a1_otb", [128, 16 * 512], BF16)
        xb_ = sbuf(nc, sC, "a1_xb", [128, D])
        ya_ = sbuf(nc, sC, "a1_ya", [128, D])
        yb_ = sbuf(nc, sC, "a1_yb", [128, D])
        at2 = [psum(nc, sC, f"a1_at2{i}", [128, 512]) for i in range(2)]
        bo_bc, t_bo = bcast_vec(p, nc, sC, cm, "a1_bo", io["bo"], D, at2)
        p.barrier()
        last4 = {}
        for tb in range(8):
            tob_ = []
            for jc in range(16):
                tob_.append(p.dma("sync", otb[:, jc * 512:(jc + 1) * 512],
                                  io["ot"][jc, :, tb * 512:(tb + 1) * 512],
                                  after=[last4.get("otb")]))
            for tsub in range(4):
                r0 = tb * 512 + tsub * 128
                mmt = [None, None]
                for oc in range(2):
                    for jc in range(16):
                        mmt[oc] = p.op("tensor", (lambda e, o=at2[oc][:],
                                       l=otb[:, jc * 512 + tsub * 128: jc * 512 + tsub * 128 + 128],
                                       r=wo[:, jc * D + oc * 512: jc * D + (oc + 1) * 512],
                                       s=(jc == 0), st=(jc == 15):
                                       e.matmul(o, l, r, start=s, stop=st)),
                                       after=[tob_[-1], t_wo[jc], last4.get(("ev", oc))])
                tev = []
                for oc in range(2):
                    t1 = p.op("vector", (lambda e, o=ya_[:, oc * 512:(oc + 1) * 512],
                              i=at2[oc][:], b=bo_bc[:, oc * 512:(oc + 1) * 512]:
                              e.scalar_tensor_tensor(out=o, in0=i, scalar=1.0, in1=b,
                                  op0=ALU.mult, op1=ALU.add)),
                              after=[mmt[oc], t_bo, last4.get("ya")])
                    last4[("ev", oc)] = t1
                    tev.append(t1)
                last4["ya"] = tev[-1]
                tx = p.dma("sync", xb_[:], io["x"][r0:r0 + 128, :], after=[last4.get("xb")])
                tm, tr = ln_stats(p, ya_[:], lns, after=tev)
                ta = ln_apply(p, yb_[:], ya_[:], lns, [tm, tr, last4.get("x3d")])
                tx3 = p.op("vector", (lambda e, o=yb_[:], a=yb_[:], b=xb_[:]:
                           e.tensor_add(o, a, b)), after=[ta, tx])
                last4["xb"] = tx3
                last4["x3d"] = p.dma("sync", io["x3"][r0:r0 + 128, :], yb_[:],
                                     after=[tx3])
            last4["otb"] = mmt[1]
        p.barrier()


def build_a1(debug=False):
    nc = bass.Bass("TRN2", target_bir_lowering=False)
    io = {
        "x": _din(nc, "x", [T, D]),
        "Wq": _din(nc, "Wq", [D, D]),
        "bq": _din(nc, "bq", [D]),
        "Wk": _din(nc, "Wk", [D, D]),
        "bk": _din(nc, "bk", [D]),
        "Wv": _din(nc, "Wv", [D, 2 * D]),
        "bv": _din(nc, "bv", [2 * D]),
        "Wg": _din(nc, "Wg", [D, 2 * D]),
        "bg": _din(nc, "bg", [2 * D]),
        "Wo": _din(nc, "Wo", [2 * D, D]),
        "bo": _din(nc, "bo", [D]),
        "iden": _din(nc, "iden", [128, 128], BF16),
        "x3": _dout(nc, "x3", [T, D]),
        "xwt": _dtmp(nc, "a1xwt", [ND, 128, T], BF16),
        "vr": _dtmp(nc, "a1vr", [T, 16 * 129], BF16),
        "sg": _dtmp(nc, "a1sg", [T, 2 * D], BF16),
        "ot": _dtmp(nc, "a1ot", [16, 128, T], BF16),
    }
    with ExitStack() as ctx:
        p = Prog(nc, ctx)
        cm = Common(p, nc, ctx, io)
        emit_a1(p, nc, ctx, cm, io)
        p.barrier()
        if debug:
            for nm, shp, dt in (("vr", [T, 16 * 129], BF16), ("sg", [T, 2 * D], BF16),
                                 ("ot", [16, 128, T], BF16), ("xwt", [ND, 128, T], BF16)):
                dbg = _dout(nc, "dbg_" + nm, shp, dt)
                p.dma("sync", dbg, io[nm])
            dbq = _dout(nc, "dbg_qrT", [128, 8 * T], BF16)
            dbk = _dout(nc, "dbg_krT", [128, 8 * T], BF16)
            p.dma("sync", dbq, io["_qrT"][:])
            p.dma("sync", dbk, io["_krT"][:])
            p.barrier()
        p.replay()
    return nc


def _a1_inmaps(inputs, x2_shards, cores):
    m = []
    for j, c in enumerate(cores):
        m.append({
            "x": np.ascontiguousarray(x2_shards[j]),
            "Wq": inputs["L1_attn_Wq"], "bq": inputs["L1_attn_bq"],
            "Wk": inputs["L1_attn_Wk"], "bk": inputs["L1_attn_bk"],
            "Wv": inputs["L1_attn_Wv"], "bv": inputs["L1_attn_bv"],
            "Wg": inputs["L1_attn_Wg"], "bg": inputs["L1_attn_bg"],
            "Wo": inputs["L1_attn_Wo"], "bo": inputs["L1_attn_bo"],
            "iden": IDEN,
        })
    return m


def emit_f1(p, nc, ctx, cm, io, debug=False):
    trp = TrPool(p, nc, ctx, cm, "f1trp")
    lns = LNS(nc, ctx, "f1_lns")

    with ExitStack() as s1:
        nw_ps = [psum(nc, s1, f"f1_nwps{i}", [128, 512]) for i in range(2)]
        nw_bc, t_nw = bcast_vec(p, nc, s1, cm, "f1_nw", io["norm_weight"], D, nw_ps)
        x_t = [sbuf(nc, s1, f"f1_x{i}", [128, D]) for i in range(2)]
        w_t = [sbuf(nc, s1, f"f1_w{i}", [128, D]) for i in range(2)]
        wbf = [sbuf(nc, s1, f"f1_wbf{i}", [128, D], BF16) for i in range(2)]
        stage1 = sbuf(nc, s1, "f1_stage1", [128, ND * 512], BF16)
        lastu = {}
        last_st1 = None
        for tb in range(8):
            tstg = []
            for tsub in range(4):
                ti = tb * 4 + tsub
                sl = ti % 2
                r0 = ti * 128
                tx = p.dma("sync", x_t[sl][:], io["x"][r0:r0 + 128, :],
                           after=[lastu.get(("x", sl))])
                tm, tr = ln_stats(p, x_t[sl][:], lns, after=[tx])
                ts = ln_apply(p, w_t[sl][:], x_t[sl][:], lns, [tm, tr])
                lastu[("x", sl)] = ts
                tm2, tr2 = ln_stats(p, w_t[sl][:], lns, after=[ts],
                                    eps=1e-8, mean_zero=True)
                ta = p.op("vector", (lambda e, o=w_t[sl][:], i=w_t[sl][:],
                          r=lns.rstd[:]: e.tensor_scalar_mul(o, i, r)),
                          after=[tr2, lastu.get(("wbf", sl))])
                tg = p.op("vector", (lambda e, o=w_t[sl][:], a=w_t[sl][:], b=nw_bc[:]:
                          e.tensor_mul(o, a, b)), after=[ta, t_nw])
                tbf = p.op("scalar", (lambda e, o=wbf[sl][:], i=w_t[sl][:]:
                                      e.copy(o, i)), after=[tg, lastu.get(("bf", sl))])
                lastu[("wbf", sl)] = tbf
                for ic in range(ND):
                    tstg.append(trp.tr(wbf[sl][:, ic * 128:(ic + 1) * 128],
                                       stage1[:, ic * 512 + tsub * 128: ic * 512 + tsub * 128 + 128],
                                       after=[tbf, last_st1]))
                lastu[("bf", sl)] = tstg[-1]
            dmas = []
            for ic in range(ND):
                dmas.append(p.dma("sync", io["nt"][ic, :, tb * 512:(tb + 1) * 512],
                                  stage1[:, ic * 512:(ic + 1) * 512], after=[tstg[-1]]))
            last_st1 = dmas[-1]
        p.barrier()

    combs = sbuf(nc, ctx, "f1_combs", [128, NT * 8])
    io["_combs"] = combs
    with ExitStack() as s2:
        rwT = sbuf(nc, s2, "f1_rwT", [128, ND * 8])
        rwbf = sbuf(nc, s2, "f1_rwbf", [128, ND * 8], BF16)
        t_rw = []
        for ic in range(ND):
            for ex in range(8):
                t_rw.append(load_col(p, rwT[:, ic * 8 + ex: ic * 8 + ex + 1],
                                     io["router_w"][ex, :], ic * 128, 128))
        t_rwb = p.op("vector", (lambda e, o=rwbf[:], i=rwT[:]: e.tensor_copy(o, i)),
                     after=[t_rw[-1]])
        xlt_in = sbuf(nc, s2, "f1_xlt2", [128, ND * 128], BF16)
        probs = sbuf(nc, s2, "f1_probs", [128, 8])
        gmax = sbuf(nc, s2, "f1_gmax", [128, 8])
        gidx = sbuf(nc, s2, "f1_gidx", [128, 8], U32)
        gidxf = sbuf(nc, s2, "f1_gidxf", [128, 8])
        iota8i = sbuf(nc, s2, "f1_iota8i", [128, 8], I32)
        iota8 = sbuf(nc, s2, "f1_iota8", [128, 8])
        wsum = sbuf(nc, s2, "f1_wsum", [128, 1])
        wn = sbuf(nc, s2, "f1_wn", [128, 2])
        eq2 = sbuf(nc, s2, "f1_eq2", [128, 8])
        gps = psum(nc, s2, "f1_gps", [128, 512])
        t_io = p.op("gpsimd", (lambda e, o=iota8i[:]:
                    e.iota(o, [[1, 8]], channel_multiplier=0)))
        t_iof = p.op("vector", (lambda e, o=iota8[:], i=iota8i[:]:
                     e.tensor_copy(o, i)), after=[t_io])
        last_xlt = None
        last_gps = None
        for ti in range(NT):
            r0 = ti * 128
            txl = []
            for ic in range(ND):
                txl.append(p.dma("sync", xlt_in[:, ic * 128:(ic + 1) * 128],
                                 io["nt"][ic, :, r0:r0 + 128], after=[last_xlt]))
            mmt = None
            for ic in range(ND):
                mmt = p.op("tensor", (lambda e, o=gps[:, 0:8],
                           l=xlt_in[:, ic * 128:(ic + 1) * 128],
                           r=rwbf[:, ic * 8:(ic + 1) * 8],
                           s=(ic == 0), st=(ic == ND - 1):
                           e.matmul(o, l, r, start=s, stop=st)),
                           after=[txl[-1], t_rwb, last_gps])
            t_pr = p.op("scalar", (lambda e, o=probs[:], i=gps[:, 0:8]:
                        e.activation(o, i, AF.Sigmoid)), after=[mmt])
            last_gps = t_pr
            t_mx = p.op("vector", (lambda e, o=gmax[:], i=probs[:]: e.max(o, i)),
                        after=[t_pr])
            t_ix = p.op("vector", (lambda e, o=gidx[:], m=gmax[:], i=probs[:]:
                        e.max_index(o, m, i)), after=[t_mx])
            t_if = p.op("vector", (lambda e, o=gidxf[:], i=gidx[:]:
                        e.tensor_copy(o, i)), after=[t_ix])
            t_ws = p.op("vector", (lambda e, o=wsum[:], i=gmax[:, 0:2]:
                        e.reduce_sum(o, i, axis=AX.X)), after=[t_mx])
            t_w2 = p.op("vector", (lambda e, o=wsum[:], i=wsum[:]:
                        e.tensor_scalar_add(o, i, 1e-6)), after=[t_ws])
            t_w3 = p.op("vector", (lambda e, o=wsum[:], i=wsum[:]:
                        e.reciprocal(o, i)), after=[t_w2])
            t_w4 = p.op("vector", (lambda e, o=wn[:], i=gmax[:, 0:2], s=wsum[:]:
                        e.tensor_scalar_mul(o, i, s)), after=[t_w3])
            t_c1 = p.op("vector", (lambda e, o=combs[:, ti * 8:(ti + 1) * 8],
                        i=iota8[:], s1=gidxf[:, 0:1], s2=wn[:, 0:1]:
                        e.tensor_scalar(o, i, s1, s2, ALU.is_equal, ALU.mult)),
                        after=[t_if, t_w4, t_iof])
            t_c2 = p.op("vector", (lambda e, o=eq2[:],
                        i=iota8[:], s1=gidxf[:, 1:2], s2=wn[:, 1:2]:
                        e.tensor_scalar(o, i, s1, s2, ALU.is_equal, ALU.mult)),
                        after=[t_c1])
            t_c3 = p.op("vector", (lambda e, o=combs[:, ti * 8:(ti + 1) * 8],
                        a=combs[:, ti * 8:(ti + 1) * 8], b=eq2[:]:
                        e.tensor_add(o, a, b)), after=[t_c2])
            last_xlt = t_c3
        p.barrier()

    with ExitStack() as s3:
        sw1 = sbuf(nc, s3, "f1_sw1", [128, ND * 4096], BF16)
        sw2 = sbuf(nc, s3, "f1_sw2", [128, 32 * D], BF16)
        t_w1, t_w2_ = [], []
        for ic in range(ND):
            t_w1.append(p.dma("gpsimd",
                              sw1[:, ic * 4096:(ic + 1) * 4096],
                              io["sw1"][ic * 128:(ic + 1) * 128, :]))
        for jc in range(32):
            t_w2_.append(p.dma("gpsimd",
                               sw2[:, jc * D:(jc + 1) * D],
                               io["sw2"][jc * 128:(jc + 1) * 128, :]))
        sb1c = sbuf(nc, s3, "f1_sb1c", [128, 32])
        t_b1 = [load_col(p, sb1c[:, j:j + 1], io["sb1"], j * 128, 128)
                for j in range(32)]
        hT = sbuf(nc, s3, "f1_hT", [128, 32 * 512], BF16)
        xlt3 = sbuf(nc, s3, "f1_xlt3", [128, ND * 512], BF16)
        yout = sbuf(nc, s3, "f1_yout", [128, D])
        hps = [psum(nc, s3, f"f1_hps{i}", [128, 512]) for i in range(2)]
        yps = [psum(nc, s3, f"f1_yps{i}", [128, 512]) for i in range(2)]
        sb2_bc, t_b2 = bcast_vec(p, nc, s3, cm, "f1_sb2", io["sb2"], D, yps)
        last_hps = [None, None]
        last_yps = [t_b2, t_b2]
        last_xlt3 = None
        last_yout = None
        for tbk in range(8):
            txl = []
            for ic in range(ND):
                txl.append(p.dma("sync", xlt3[:, ic * 512:(ic + 1) * 512],
                                 io["nt"][ic, :, tbk * 512:(tbk + 1) * 512],
                                 after=[last_xlt3]))
            th = None
            for j in range(32):
                psl = j % 2
                mmt = None
                for ic in range(ND):
                    mmt = p.op("tensor", (lambda e, o=hps[psl][:],
                               l=sw1[:, ic * 4096 + j * 128: ic * 4096 + j * 128 + 128],
                               r=xlt3[:, ic * 512:(ic + 1) * 512],
                               s=(ic == 0), st=(ic == ND - 1):
                               e.matmul(o, l, r, start=s, stop=st)),
                               after=[txl[-1], t_w1[ic], last_hps[psl]])
                th = p.op("scalar", (lambda e, o=hT[:, j * 512:(j + 1) * 512],
                          i=hps[psl][:], b=sb1c[:, j:j + 1]:
                          e.activation(o, i, AF.Relu, bias=b)),
                          after=[mmt, t_b1[j]])
                last_hps[psl] = th
            last_xlt3 = th
            for tsub in range(4):
                r0 = tbk * 512 + tsub * 128
                mmt = [None, None]
                for oc in range(2):
                    for j in range(32):
                        mmt[oc] = p.op("tensor", (lambda e, o=yps[oc][:],
                                       l=hT[:, j * 512 + tsub * 128: j * 512 + tsub * 128 + 128],
                                       r=sw2[:, j * D + oc * 512: j * D + (oc + 1) * 512],
                                       s=(j == 0), st=(j == 31):
                                       e.matmul(o, l, r, start=s, stop=st)),
                                       after=[th, t_w2_[-2], t_w2_[-1], last_yps[oc]])
                tev = []
                for oc in range(2):
                    t1 = p.op("vector", (lambda e, o=yout[:, oc * 512:(oc + 1) * 512],
                              i=yps[oc][:], b=sb2_bc[:, oc * 512:(oc + 1) * 512]:
                              e.scalar_tensor_tensor(out=o, in0=i, scalar=1.0, in1=b,
                                  op0=ALU.mult, op1=ALU.add)),
                              after=[mmt[oc], t_b2, last_yout])
                    last_yps[oc] = t1
                    tev.append(t1)
                last_yout = p.dma("sync", io["acc"][r0:r0 + 128, :], yout[:], after=tev)
        p.barrier()
        if debug:
            p.dma("sync", io["dbg_accmid"], io["acc"])
            p.barrier()

    with ExitStack() as s6:
        we1 = sbuf(nc, s6, "f1_we1", [128, ND * 4096], BF16)
        we2 = sbuf(nc, s6, "f1_we2", [128, 32 * D], BF16)
        eb1c = sbuf(nc, s6, "f1_eb1c", [128, 32])
        eb2_bc = sbuf(nc, s6, "f1_eb2bc", [128, D])
        hT = sbuf(nc, s6, "f1_hT6", [128, 32 * 512], BF16)
        xlt6 = sbuf(nc, s6, "f1_xlt6", [128, ND * 512], BF16)
        yout = sbuf(nc, s6, "f1_yout6", [128, D])
        hps = [psum(nc, s6, f"f1_hps6{i}", [128, 512]) for i in range(2)]
        yps = [psum(nc, s6, f"f1_yps6{i}", [128, 512]) for i in range(2)]
        last_we1 = None
        last_we2 = None
        last_hps = [None, None]
        last_yps = [None, None]
        last_xlt6 = None
        last_yout = None
        t_eb2 = None
        t_eb2_reader = None

        def emit_ew1(ex_, war):
            return [p.dma("gpsimd", we1[:, ic * 4096:(ic + 1) * 4096],
                          io["ew1"][ex_, ic * 128:(ic + 1) * 128, :],
                          after=[war]) for ic in range(ND)]

        t_we1 = emit_ew1(0, None)
        for ex in range(8):
            nxt_we1 = None
            t_we2 = []
            for jc in range(32):
                t_we2.append(p.dma("gpsimd",
                                   we2[:, jc * D:(jc + 1) * D],
                                   io["ew2"][ex, jc * 128:(jc + 1) * 128, :],
                                   after=[last_we2]))
            t_eb1 = [load_col(p, eb1c[:, j:j + 1], io["eb1"][ex, :], j * 128, 128)
                     for j in range(32)]
            _, t_eb2 = bcast_vec(p, nc, s6, cm, f"f1_eb2_{ex}", io["eb2"][ex, :],
                                      D, yps, war=[last_yps[0], last_yps[1], t_eb2_reader],
                                      tile=eb2_bc)
            last_hmm = None
            last_ymm = None
            for tbk in range(8):
                txl = []
                for ic in range(ND):
                    txl.append(p.dma("sync", xlt6[:, ic * 512:(ic + 1) * 512],
                                     io["nt"][ic, :, tbk * 512:(tbk + 1) * 512],
                                     after=[last_xlt6]))
                th = None
                for j in range(32):
                    psl = j % 2
                    mmt = None
                    for ic in range(ND):
                        mmt = p.op("tensor", (lambda e, o=hps[psl][:],
                                   l=we1[:, ic * 4096 + j * 128: ic * 4096 + j * 128 + 128],
                                   r=xlt6[:, ic * 512:(ic + 1) * 512],
                                   s=(ic == 0), st=(ic == ND - 1):
                                   e.matmul(o, l, r, start=s, stop=st)),
                                   after=[txl[-1], last_hps[psl], t_we1[-1]])
                    last_hmm = mmt
                    th = p.op("scalar", (lambda e, o=hT[:, j * 512:(j + 1) * 512],
                              i=hps[psl][:], b=eb1c[:, j:j + 1]:
                              e.activation(o, i, AF.Relu, bias=b)),
                              after=[mmt, t_eb1[j]])
                    last_hps[psl] = th
                last_xlt6 = th
                if tbk == 7 and ex < 7:
                    nxt_we1 = emit_ew1(ex + 1, last_hmm)
                for tsub in range(4):
                    ti = tbk * 4 + tsub
                    r0 = ti * 128
                    mmt = [None, None]
                    for oc in range(2):
                        for j in range(32):
                            mmt[oc] = p.op("tensor", (lambda e, o=yps[oc][:],
                                           l=hT[:, j * 512 + tsub * 128: j * 512 + tsub * 128 + 128],
                                           r=we2[:, j * D + oc * 512: j * D + (oc + 1) * 512],
                                           s=(j == 0), st=(j == 31):
                                           e.matmul(o, l, r, start=s, stop=st)),
                                           after=[th, last_yps[oc], t_eb2, t_we2[-1]])
                        last_ymm = mmt[oc]
                    tev = []
                    for oc in range(2):
                        t1 = p.op("vector", (lambda e, o=yout[:, oc * 512:(oc + 1) * 512],
                                  i=yps[oc][:], b=eb2_bc[:, oc * 512:(oc + 1) * 512],
                                  m=combs[:, ti * 8 + ex: ti * 8 + ex + 1]:
                                  e.scalar_tensor_tensor(out=o, in0=i, scalar=1.0,
                                      in1=b, op0=ALU.mult, op1=ALU.add)),
                                  after=[mmt[oc], t_eb2, last_yout])
                        t2 = p.op("vector", (lambda e, o=yout[:, oc * 512:(oc + 1) * 512],
                                  i=yout[:, oc * 512:(oc + 1) * 512],
                                  m=combs[:, ti * 8 + ex: ti * 8 + ex + 1]:
                                  e.tensor_scalar_mul(o, i, m)), after=[t1])
                        last_yps[oc] = t2
                        tev.append(t2)
                        t_eb2_reader = t1
                    last_yout = p.dma("gpsimd", io["acc"][r0:r0 + 128, :], yout[:],
                                      after=tev, accum_op=ALU.add)
            if nxt_we1 is not None:
                t_we1 = nxt_we1
            last_we2 = last_ymm
        p.barrier()

    with ExitStack() as sF:
        xa = sbuf(nc, sF, "f1_xa", [128, D])
        xb = sbuf(nc, sF, "f1_xb", [128, D])
        xc = sbuf(nc, sF, "f1_xc", [128, D])
        last7 = [None, None]
        lastst = None
        for ti in range(NT):
            r0 = ti * 128
            tu = p.dma("sync", xa[:], io["acc"][r0:r0 + 128, :], after=[last7[0]])
            tx = p.dma("sync", xb[:], io["x"][r0:r0 + 128, :], after=[last7[1]])
            tm, tr = ln_stats(p, xa[:], lns, after=[tu])
            ta = ln_apply(p, xc[:], xa[:], lns, [tm, tr, lastst])
            last7[0] = ta
            tx4 = p.op("vector", (lambda e, o=xc[:], a=xc[:], b=xb[:]:
                       e.tensor_add(o, a, b)), after=[ta, tx])
            last7[1] = tx4
            lastst = p.dma("sync", io["x4"][r0:r0 + 128, :], xc[:], after=[tx4])
        p.barrier()


def build_f1(debug=False):
    nc = bass.Bass("TRN2", target_bir_lowering=False)
    io = {
        "x": _din(nc, "x", [T, D]),
        "norm_weight": _din(nc, "norm_weight", [D]),
        "router_w": _din(nc, "router_w", [8, D]),
        "sw1": _din(nc, "sw1", [D, 4096]),
        "sb1": _din(nc, "sb1", [4096]),
        "sw2": _din(nc, "sw2", [4096, D]),
        "sb2": _din(nc, "sb2", [D]),
        "ew1": _din(nc, "ew1", [8, D, 4096]),
        "eb1": _din(nc, "eb1", [8, 4096]),
        "ew2": _din(nc, "ew2", [8, 4096, D]),
        "eb2": _din(nc, "eb2", [8, D]),
        "iden": _din(nc, "iden", [128, 128], BF16),
        "x4": _dout(nc, "x4", [T, D]),
        "nt": _dtmp(nc, "f1nt", [ND, 128, T], BF16),
        "acc": _dtmp(nc, "f1acc", [T, D]),
    }
    with ExitStack() as ctx:
        p = Prog(nc, ctx)
        cm = Common(p, nc, ctx, io)
        if debug:
            io["dbg_accmid"] = _dout(nc, "dbg_accmid", [T, D])
        emit_f1(p, nc, ctx, cm, io, debug=debug)
        p.barrier()
        if debug:
            for nm, shp, dt in (("nt", [ND, 128, T], BF16), ("acc", [T, D], F32)):
                dbg = _dout(nc, "dbg_" + nm, shp, dt)
                p.dma("sync", dbg, io[nm])
            dbgc = _dout(nc, "dbg_combs", [128, NT * 8])
            p.dma("sync", dbgc, io["_combs"][:])
            p.barrier()
        p.replay()
    return nc


def _f1_inmaps(inputs, x3_shards, cores):
    m = []
    for j, c in enumerate(cores):
        m.append({
            "x": np.ascontiguousarray(x3_shards[j]),
            "norm_weight": inputs["L1_ffn_norm_weight"],
            "router_w": inputs["L1_ffn_router_w"],
            "sw1": inputs["L1_ffn_sw1"], "sb1": inputs["L1_ffn_sb1"],
            "sw2": inputs["L1_ffn_sw2"], "sb2": inputs["L1_ffn_sb2"],
            "ew1": inputs["L1_ffn_ew1"], "eb1": inputs["L1_ffn_eb1"],
            "ew2": inputs["L1_ffn_ew2"], "eb2": inputs["L1_ffn_eb2"],
            "iden": IDEN,
        })
    return m


def build_fused():
    nc = bass.Bass("TRN2", target_bir_lowering=False)
    io = {
        "x": _din(nc, "x", [T, D]),
        "iden": _din(nc, "iden", [128, 128], BF16),
        "x4": _dout(nc, "x4", [T, D]),
        "in_proj_w": _din(nc, "in_proj_w", [3 * D, D]),
        "in_proj_b": _din(nc, "in_proj_b", [3 * D]),
        "out_w": _din(nc, "out_w", [D, D]),
        "out_b": _din(nc, "out_b", [D]),
        "lin1_w": _din(nc, "lin1_w", [4096, D]),
        "lin1_b": _din(nc, "lin1_b", [4096]),
        "lin2_w": _din(nc, "lin2_w", [D, 4096]),
        "lin2_b": _din(nc, "lin2_b", [D]),
        "a0_ln1_g": _din(nc, "a0_ln1_g", [D]),
        "a0_ln1_b": _din(nc, "a0_ln1_b", [D]),
        "a0_ln2_g": _din(nc, "a0_ln2_g", [D]),
        "a0_ln2_b": _din(nc, "a0_ln2_b", [D]),
        "w_qkv": _din(nc, "w_qkv", [D, 3 * D]),
        "w_proj": _din(nc, "w_proj", [D, D]),
        "f0_ln1_w": _din(nc, "f0_ln1_w", [D]),
        "f0_ln1_b": _din(nc, "f0_ln1_b", [D]),
        "f0_ln2_w": _din(nc, "f0_ln2_w", [D]),
        "f0_ln2_b": _din(nc, "f0_ln2_b", [D]),
        "gate_w": _din(nc, "gate_w", [D, 8]),
        "gate_b": _din(nc, "gate_b", [8]),
        "w1": _din(nc, "w1", [8, D, 4096]),
        "w2": _din(nc, "w2", [8, 4096, D]),
        "Wq": _din(nc, "Wq", [D, D]), "bq": _din(nc, "bq", [D]),
        "Wk": _din(nc, "Wk", [D, D]), "bk": _din(nc, "bk", [D]),
        "Wv": _din(nc, "Wv", [D, 2 * D]), "bv": _din(nc, "bv", [2 * D]),
        "Wg": _din(nc, "Wg", [D, 2 * D]), "bg": _din(nc, "bg", [2 * D]),
        "Wo": _din(nc, "Wo", [2 * D, D]), "bo": _din(nc, "bo", [D]),
        "norm_weight": _din(nc, "norm_weight", [D]),
        "router_w": _din(nc, "router_w", [8, D]),
        "sw1": _din(nc, "sw1", [D, 4096]), "sb1": _din(nc, "sb1", [4096]),
        "sw2": _din(nc, "sw2", [4096, D]), "sb2": _din(nc, "sb2", [D]),
        "ew1": _din(nc, "ew1", [8, D, 4096]), "eb1": _din(nc, "eb1", [8, 4096]),
        "ew2": _din(nc, "ew2", [8, 4096, D]), "eb2": _din(nc, "eb2", [8, D]),
        "x1": _dtmp(nc, "x1t", [T, D]),
        "x2": _dtmp(nc, "x2t", [T, D]),
        "x3": _dtmp(nc, "x3t", [T, D]),
        "a0src": _dtmp(nc, "a0src", [T, D]),
        "a0postt": _dtmp(nc, "a0postt", [D, T], BF16),
        "a0xln1": _dtmp(nc, "a0xln1", [T, D]),
        "a0xln1t": _dtmp(nc, "a0xln1t", [ND, 128, T], BF16),
        "a0u2": _dtmp(nc, "a0u2", [T, D]),
        "f0xs": _dtmp(nc, "f0xs", [T, D]),
        "f0postt": _dtmp(nc, "f0postt", [D, T], BF16),
        "f0u2m": _dtmp(nc, "f0u2m", [T, D]),
        "f0xln2t": _dtmp(nc, "f0xln2t", [ND, 128, T], BF16),
        "f0moe": _dtmp(nc, "f0moe", [T, D]),
        "a1xwt": _dtmp(nc, "a1xwt", [ND, 128, T], BF16),
        "a1vr": _dtmp(nc, "a1vr", [T, 16 * 129], BF16),
        "a1sg": _dtmp(nc, "a1sg", [T, 2 * D], BF16),
        "a1ot": _dtmp(nc, "a1ot", [16, 128, T], BF16),
        "f1nt": _dtmp(nc, "f1nt", [ND, 128, T], BF16),
        "f1acc": _dtmp(nc, "f1acc", [T, D]),
    }
    with ExitStack() as ctx:
        p = Prog(nc, ctx)
        cm = Common(p, nc, ctx, io)
        io_a0 = {k: io[v] for k, v in dict(
            x="x", iden="iden", in_proj_w="in_proj_w", in_proj_b="in_proj_b",
            out_w="out_w", out_b="out_b", lin1_w="lin1_w", lin1_b="lin1_b",
            lin2_w="lin2_w", lin2_b="lin2_b", ln1_g="a0_ln1_g", ln1_b="a0_ln1_b",
            ln2_g="a0_ln2_g", ln2_b="a0_ln2_b", x1="x1", src="a0src",
            postt="a0postt", xln1="a0xln1", xln1t="a0xln1t", u2="a0u2").items()}
        with ExitStack() as pc:
            emit_a0(p, nc, pc, cm, io_a0)
        p.barrier()
        io_f0 = {k: io[v] for k, v in dict(
            x="x1", iden="iden", w_qkv="w_qkv", w_proj="w_proj",
            ln1_w="f0_ln1_w", ln1_b="f0_ln1_b", ln2_w="f0_ln2_w", ln2_b="f0_ln2_b",
            gate_w="gate_w", gate_b="gate_b", w1="w1", w2="w2", x2="x2",
            xs="f0xs", postt="f0postt", u2m="f0u2m", xln2t="f0xln2t",
            moe="f0moe").items()}
        with ExitStack() as pc:
            emit_f0(p, nc, pc, cm, io_f0)
        p.barrier()
        io_a1 = {k: io[v] for k, v in dict(
            x="x2", iden="iden", Wq="Wq", bq="bq", Wk="Wk", bk="bk", Wv="Wv",
            bv="bv", Wg="Wg", bg="bg", Wo="Wo", bo="bo", x3="x3",
            xwt="a1xwt", vr="a1vr", sg="a1sg", ot="a1ot").items()}
        with ExitStack() as pc:
            emit_a1(p, nc, pc, cm, io_a1)
        p.barrier()
        io_f1 = {k: io[v] for k, v in dict(
            x="x3", iden="iden", norm_weight="norm_weight", router_w="router_w",
            sw1="sw1", sb1="sb1", sw2="sw2", sb2="sb2", ew1="ew1", eb1="eb1",
            ew2="ew2", eb2="eb2", x4="x4", nt="f1nt", acc="f1acc").items()}
        with ExitStack() as pc:
            emit_f1(p, nc, pc, cm, io_f1)
        p.barrier()
        p.replay()
    return nc


def _fused_inmaps(inputs, cores):
    m = []
    for c in cores:
        m.append({
            "x": np.ascontiguousarray(inputs["L0_attn_src"][c]),
            "iden": IDEN,
            "in_proj_w": inputs["L0_attn_in_proj_w"],
            "in_proj_b": inputs["L0_attn_in_proj_b"],
            "out_w": inputs["L0_attn_out_w"], "out_b": inputs["L0_attn_out_b"],
            "lin1_w": inputs["L0_attn_lin1_w"], "lin1_b": inputs["L0_attn_lin1_b"],
            "lin2_w": inputs["L0_attn_lin2_w"], "lin2_b": inputs["L0_attn_lin2_b"],
            "a0_ln1_g": inputs["L0_attn_ln1_g"], "a0_ln1_b": inputs["L0_attn_ln1_b"],
            "a0_ln2_g": inputs["L0_attn_ln2_g"], "a0_ln2_b": inputs["L0_attn_ln2_b"],
            "w_qkv": inputs["L0_ffn_w_qkv"], "w_proj": inputs["L0_ffn_w_proj"],
            "f0_ln1_w": inputs["L0_ffn_ln1_w"], "f0_ln1_b": inputs["L0_ffn_ln1_b"],
            "f0_ln2_w": inputs["L0_ffn_ln2_w"], "f0_ln2_b": inputs["L0_ffn_ln2_b"],
            "gate_w": inputs["L0_ffn_gate_w"], "gate_b": inputs["L0_ffn_gate_b"],
            "w1": inputs["L0_ffn_w1"], "w2": inputs["L0_ffn_w2"],
            "Wq": inputs["L1_attn_Wq"], "bq": inputs["L1_attn_bq"],
            "Wk": inputs["L1_attn_Wk"], "bk": inputs["L1_attn_bk"],
            "Wv": inputs["L1_attn_Wv"], "bv": inputs["L1_attn_bv"],
            "Wg": inputs["L1_attn_Wg"], "bg": inputs["L1_attn_bg"],
            "Wo": inputs["L1_attn_Wo"], "bo": inputs["L1_attn_bo"],
            "norm_weight": inputs["L1_ffn_norm_weight"],
            "router_w": inputs["L1_ffn_router_w"],
            "sw1": inputs["L1_ffn_sw1"], "sb1": inputs["L1_ffn_sb1"],
            "sw2": inputs["L1_ffn_sw2"], "sb2": inputs["L1_ffn_sb2"],
            "ew1": inputs["L1_ffn_ew1"], "eb1": inputs["L1_ffn_eb1"],
            "ew2": inputs["L1_ffn_ew2"], "eb2": inputs["L1_ffn_eb2"],
        })
    return m


def kernel_fused(**inputs):
    cores = list(range(8))
    nc = build_fused()
    res = run_bass_kernel_spmd(nc, _fused_inmaps(inputs, cores), core_ids=cores)
    return np.stack([res.results[i]["x4"] for i in range(len(cores))])
